# version 09
# speedup vs baseline: 1.5265x; 1.0157x over previous
import numpy as np
import ml_dtypes
from contextlib import ExitStack

import concourse.bass as bass
import concourse.mybir as mybir
from concourse.bass_utils import run_bass_kernel_spmd

F32 = mybir.dt.float32
BF16 = mybir.dt.bfloat16
I32 = mybir.dt.int32
AF = mybir.ActivationFunctionType
ALU = mybir.AluOpType
AX = mybir.AxisListType
BF = ml_dtypes.bfloat16

NCORES = 8
B = 2
S = 8192
DM = 1024
TPC = (B * S) // NCORES
GPB = NCORES // B
NTT = TPC // 128
NDC = DM // 128

ENGS = ("sync", "scalar", "vector", "tensor", "gpsimd")


def ceil_div(a, b):
    return (a + b - 1) // b


_NAME_CTR = [0]


def _uname(n):
    _NAME_CTR[0] += 1
    return f"{n}__{_NAME_CTR[0]}"


class StageCtx:

    def __init__(self, nc, sch, io):
        self.nc = nc
        self.sch = sch
        self.io = io


def _stage_begin(ctx):
    if ctx is None:
        nc = bass.Bass("TRN2", target_bir_lowering=False)
        return nc, ExitStack(), True
    return ctx.nc, ExitStack(), False


def _mk_D(nc, ctx):
    def D(name, shape, dtype, kind):
        if ctx is None:
            return nc.dram_tensor(name, shape, dtype, kind=kind).ap()
        return ctx.io[name]
    return D


class _Op:
    __slots__ = ("engine", "kind", "fn", "deps", "cum", "tag")

    def __init__(self, engine, kind, fn, deps, tag=""):
        self.engine = engine
        self.kind = kind
        self.fn = fn
        self.deps = deps
        self.cum = None
        self.tag = tag


class Sched:

    def __init__(self, nc, es):
        self.nc = nc
        self.es = es
        self.ops = []
        self._counts = {}
        self.default_deps = []
        self.last_op = {}

    def stage_barrier(self):
        self.default_deps = list(self.last_op.values())

    def op(self, engine, fn, deps=(), tag=""):
        o = _Op(engine, "cmp", fn, list(deps) + self.default_deps, tag)
        c = self._counts.get((engine, "cmp"), 0) + 1
        self._counts[(engine, "cmp")] = c
        o.cum = c
        self.ops.append(o)
        self.last_op[engine] = o
        return o

    def dma(self, engine, out, in_, deps=(), tag="", dma_kw=None):
        return self.dma_batch(engine, [(out, in_)], deps, tag, dma_kw)

    def dma_batch(self, engine, pairs, deps=(), tag="", dma_kw=None):
        sched = self
        c = self._counts.get((engine, "dma"), 0) + 16 * len(pairs)
        self._counts[(engine, "dma")] = c

        def fn(eng):
            dsem = sched.sems[(engine, "dma")]
            for out, in_ in pairs:
                eng.dma_start(out=out, in_=in_, **(dma_kw or {})).then_inc(dsem, 16)
            eng.wait_ge(dsem, c)
            return None

        o = _Op(engine, "dmab", fn, list(deps) + self.default_deps, tag)
        cc = self._counts.get((engine, "cmp"), 0) + 1
        self._counts[(engine, "cmp")] = cc
        o.cum = cc
        self.ops.append(o)
        self.last_op[engine] = o
        return o

    def emit(self):
        nc, es = self.nc, self.es
        used = set()
        for o in self.ops:
            if o.kind == "dmab":
                used.add((o.engine, "dma"))
                used.add((o.engine, "cmp"))
            else:
                used.add((o.engine, "cmp"))
        used = sorted(used)
        self.sems = {
            key: es.enter_context(nc.semaphore(f"pg_{key[0]}_{key[1]}"))
            for key in used
        }
        per_engine = {e: [o for o in self.ops if o.engine == e] for e in ENGS}
        totals = {key: self._counts.get(key, 0) for key in used}
        sched = self

        with nc.Block() as block:
            for eng_name in ENGS:
                ops = per_engine[eng_name]
                if not ops:
                    continue

                def make_body(eng_name, ops):
                    def body(eng):
                        hwm = {}
                        own_dma = 0
                        for o in ops:
                            needs = {}
                            for d in o.deps:
                                key = (d.engine, "cmp")
                                v = d.cum
                                if needs.get(key, 0) < v:
                                    needs[key] = v
                            for key, v in sorted(needs.items()):
                                if hwm.get(key, 0) >= v:
                                    continue
                                eng.wait_ge(sched.sems[key], v)
                                hwm[key] = v
                            inst = o.fn(eng)
                            hwm[(eng_name, "cmp")] = max(hwm.get((eng_name, "cmp"), 0),
                                                         o.cum - 1) if o.kind != "dmab" else hwm.get((eng_name, "cmp"), 0)
                            if o.kind == "dmab":
                                eng.sem_inc(sched.sems[(eng_name, "cmp")], 1)
                            else:
                                inst.then_inc(sched.sems[(eng_name, "cmp")], 1)
                        if (eng_name, "dma") in sched.sems:
                            eng.wait_ge(sched.sems[(eng_name, "dma")],
                                        totals[(eng_name, "dma")])
                    return body

                getattr(block, eng_name)(make_body(eng_name, ops))



def e_matmul_acc(sch, psum_ap, pairs, deps, tag="mm", tile_position=None):
    def fn(t):
        inst = None
        n = len(pairs)
        for i, (l, r) in enumerate(pairs):
            inst = t.matmul(psum_ap, l, r, start=(i == 0), stop=(i == n - 1),
                            tile_position=tile_position)
        return inst
    return sch.op("tensor", fn, deps, tag)


def e_transpose(sch, psum_ap, in_ap, ident_ap, deps, tag="tr"):
    return sch.op("tensor", lambda t: t.transpose(psum_ap, in_ap, ident_ap), deps, tag)


def e_copy(sch, eng, out_ap, in_ap, deps, tag="cp"):
    return sch.op(eng, lambda e: e.tensor_copy(out_ap, in_ap), deps, tag)


def e_act(sch, out_ap, in_ap, func, deps, bias=None, scale=None, accum_out=None, tag="act"):
    kw = {}
    if bias is not None:
        kw["bias"] = bias
    if scale is not None:
        kw["scale"] = scale
    if accum_out is not None:
        kw["accum_out"] = accum_out
    return sch.op("scalar", lambda s: s.activation(out=out_ap, in_=in_ap, func=func, **kw),
                  deps, tag)


def e_tt(sch, out_ap, a_ap, b_ap, opk, deps, tag="tt", eng="vector"):
    return sch.op(eng, lambda v: v.tensor_tensor(out_ap, a_ap, b_ap, opk), deps, tag)


def e_ts(sch, out_ap, a_ap, scalar, opk, deps, tag="ts", op1=None, scalar2=None, eng="vector"):
    if op1 is None:
        def fn(v):
            return v.tensor_scalar(out_ap, a_ap, scalar, 0.0, opk)
    else:
        def fn(v):
            return v.tensor_scalar(out_ap, a_ap, scalar, scalar2, opk, op1)
    return sch.op(eng, fn, deps, tag)


def e_stt(sch, out_ap, in0_ap, scalar, in1_ap, op0, op1, deps, tag="stt", eng="vector"):
    return sch.op(eng,
                  lambda v: v.scalar_tensor_tensor(out_ap, in0_ap, scalar, in1_ap, op0, op1),
                  deps, tag)


def e_red(sch, out_ap, in_ap, opk, deps, tag="red", absval=None):
    return sch.op("vector",
                  lambda v: v.tensor_reduce(out_ap, in_ap, AX.X, opk,
                                            apply_absolute_value=absval),
                  deps, tag)


def e_memset(sch, eng, ap, val, deps=(), tag="ms"):
    return sch.op(eng, lambda e: e.memset(ap, val), deps, tag)




def shard_tokens(x_btd):
    flat = np.ascontiguousarray(x_btd.reshape(B * S, -1))
    return [np.ascontiguousarray(flat[c * TPC:(c + 1) * TPC]) for c in range(NCORES)]


def f32(x):
    return np.ascontiguousarray(x, dtype=np.float32)


IDENT128 = np.eye(128, dtype=np.float32)

_prog_cache = {}


def _run(name, builder, in_maps):
    if name not in _prog_cache:
        _prog_cache[name] = builder()
    nc = _prog_cache[name]
    res = run_bass_kernel_spmd(nc, in_maps, core_ids=list(range(NCORES)))
    return res.results


def _launch(name, builder, maps):
    if name not in _prog_cache:
        _prog_cache[name] = builder()
    res = run_bass_kernel_spmd(_prog_cache[name], maps, core_ids=list(range(NCORES)))
    return res.results


def kernel_unfused(**inp):
    pre = _launch("b0_pre", build_b0_pre, stage_b0_pre(inp))
    s0 = _launch("l0_attn", build_l0_attn, stage_l0_attn(inp, pre))
    moe = _launch("l0_moe", build_l0_moe, stage_l0_moe(inp, pre))
    xfm = [pre[c]["o_xfm"] for c in range(NCORES)]
    sfm = [s0[c]["o_s"] for c in range(NCORES)]
    cb = _launch("combine_moe", lambda: build_combine("moe"),
                 stage_combine_moe(inp, xfm, sfm, moe))
    x = [cb[c]["o_x"] for c in range(NCORES)]
    xn = [cb[c]["o_xn"] for c in range(NCORES)]
    s1 = _launch("l1_attn", lambda: build_l1_attn(), stage_l1_attn(inp, xn))
    g1 = _launch("fp4_ffn", build_fp4_ffn, stage_fp4_ffn(inp, xn, "L1_ffn"))
    cb = _launch("combine_plain", lambda: build_combine("plain"),
                 stage_combine_plain(x, [s1[c]["o_s"] for c in range(NCORES)],
                                     [g1[c]["o_g"] for c in range(NCORES)]))
    x = [cb[c]["o_x"] for c in range(NCORES)]
    xn = [cb[c]["o_xn"] for c in range(NCORES)]
    s2 = _launch("l2_attn", build_l2_attn, stage_l2_attn(inp, xn))
    g2 = _launch("fp4_ffn", build_fp4_ffn, stage_fp4_ffn(inp, xn, "L2_ffn"))
    cb = _launch("combine_plain", lambda: build_combine("plain"),
                 stage_combine_plain(x, [s2[c]["o_s"] for c in range(NCORES)],
                                     [g2[c]["o_g"] for c in range(NCORES)]))
    x = [cb[c]["o_x"] for c in range(NCORES)]
    xn = [cb[c]["o_xn"] for c in range(NCORES)]
    sc = _launch("l3_scan", build_l3_scan, stage_l3_scan(inp, xn))
    s3 = _launch("l3_fix", build_l3_fix, stage_l3_fix(sc))
    xc = xn
    for layer in range(2):
        at = _launch("l3f_attn", build_l3f_attn, stage_l3f_attn(inp, xc, layer))
        xm = [at[c]["o_x"] for c in range(NCORES)]
        ff = _launch("l3f_ffn", build_l3f_ffn, stage_l3f_ffn(inp, xm, layer))
        xc = [ff[c]["o_x"] for c in range(NCORES)]
    fin = _launch("combine_final", lambda: build_combine("final"),
                  stage_combine_final(x, [s3[c]["o_s"] for c in range(NCORES)], xc))
    out = np.concatenate([fin[c]["o_out"] for c in range(NCORES)], axis=0)
    return np.ascontiguousarray(out.reshape(B, S, DM), dtype=np.float32)


def build_b0_pre(ctx=None):
    nc, _es0, _own = _stage_begin(ctx)
    D = _mk_D(nc, ctx)
    x_tm = D("x_tm", [TPC, DM], F32, "ExternalInput")
    wq = D("wq", [DM, DM], F32, "ExternalInput")
    wk = D("wk", [DM, 256], F32, "ExternalInput")
    wv = D("wv", [DM, 256], F32, "ExternalInput")
    gw = D("gw", [DM, 8], F32, "ExternalInput")
    bqc_d = D("bqc", [128, NDC], F32, "ExternalInput")
    bkc_d = D("bkc", [128, 2], F32, "ExternalInput")
    bvr_d = D("bvr", [1, 256], F32, "ExternalInput")
    gbc_d = D("gbc", [8, 1], F32, "ExternalInput")
    ident_d = D("ident", [128, 128], F32, "ExternalInput")

    o_xfm = D("o_xfm", [DM, TPC], F32, "ExternalOutput")
    o_xn = D("o_xn", [DM, TPC], F32, "ExternalOutput")
    o_q = D("o_q", [DM, TPC], F32, "ExternalOutput")
    o_k = D("o_k", [256, TPC], F32, "ExternalOutput")
    o_v = D("o_v", [TPC, 256], F32, "ExternalOutput")
    o_lg = D("o_lg", [8, TPC], F32, "ExternalOutput")

    es = _es0
    sb = lambda n, s, d=F32: es.enter_context(nc.sbuf_tensor(_uname(n), s, d))
    psm_ = lambda n, s: es.enter_context(nc.psum_tensor(_uname(n), s, F32))
    with es:
        xtm = sb("xtm", [128, NTT * DM])
        xfm = [sb(f"xfm{c}", [128, TPC]) for c in range(NDC)]
        xnb = [sb(f"xnb{c}", [128, TPC], BF16) for c in range(NDC)]
        idn = sb("idn", [128, 128])
        onesc = sb("onesc", [128, 1])
        onesr = sb("onesr", [1, 128])
        epsc = sb("epsc", [1, 1])
        sq = [sb(f"sq{i}", [128, 512]) for i in range(2)]
        srow = sb("srow", [1, 2 * TPC])
        mub = sb("mub", [128, 512])
        rsb = sb("rsb", [128, 512])
        xscr = [sb(f"xscr{i}", [128, 512]) for i in range(2)]
        gwf = sb("gwf", [128, NDC * 8])
        xtm_bf = xtm.bitcast(BF16)
        wqb = xtm_bf[:, 0:NDC * DM]
        wkb = xtm_bf[:, NDC * DM:NDC * DM + NDC * 256]
        wvb = xtm_bf[:, NDC * DM + NDC * 256:NDC * DM + 2 * NDC * 256]
        gwb = xtm_bf[:, NDC * DM + 2 * NDC * 256:NDC * DM + 2 * NDC * 256 + NDC * 8]
        bqc = sb("bqc_s", [128, NDC])
        bkc = sb("bkc_s", [128, 2])
        bvr = sb("bvr_s", [1, 256])
        bvb = sb("bvb", [128, 256])
        gbc = sb("gbc_s", [8, 1])
        qout = [sb(f"qout{i}", [128, 512]) for i in range(2)]
        lgout = sb("lgout", [8, TPC])
        vtile = [sb(f"vtile{i}", [128, 256]) for i in range(2)]
        pst = [psm_(f"pst{i}", [128, 512]) for i in range(2)]
        psx = psm_("psx", [1, 512])
        psx2 = psm_("psx2", [1, 512])
        psbc = psm_("psbc", [128, 512])
        psv = psm_("psv", [128, 512])
        psl = psm_("psl", [8, 512])

        sch = ctx.sch if ctx else Sched(nc, es)
        h_x = [sch.dma("sync", xtm[:, t * DM:(t + 1) * DM], x_tm[t * 128:(t + 1) * 128, :],
                       tag=f"ldx{t}") for t in range(NTT)]
        h_idn = sch.dma("sync", idn[:], ident_d[:, :], tag="ldid")
        h_bias = [sch.dma("sync", bqc[:, :], bqc_d[:, :]),
                  sch.dma("sync", bkc[:, :], bkc_d[:, :]),
                  sch.dma("sync", bvr[:], bvr_d[:, :]),
                  sch.dma("sync", gbc[:], gbc_d[:, :])]
        h_ones = [sch.op("vector", lambda v: v.memset(onesc[:], 1.0)),
                  sch.op("vector", lambda v: v.memset(onesr[:], 1.0)),
                  sch.op("vector", lambda v: v.memset(epsc[:], 1e-5))]

        ev = {}
        tr_ev = []
        for t in range(NTT):
            for c in range(NDC):
                i = (t * NDC + c) % 2
                deps = [h_x[t], h_idn]
                prev = ev.get(i)
                if prev is not None:
                    deps.append(prev)
                h_tr = e_transpose(sch, pst[i][:, 0:128],
                                   xtm[:, t * DM + c * 128: t * DM + (c + 1) * 128],
                                   idn[:], deps, tag=f"tr{t}_{c}")
                h_e = e_copy(sch, "vector", xfm[c][:, t * 128:(t + 1) * 128],
                             pst[i][:, 0:128], [h_tr], tag=f"trev{t}_{c}")
                ev[i] = h_e
                tr_ev.append(h_e)

        h_trs = [o for o in sch.ops if o.tag.startswith("tr") and o.engine == "tensor"]
        h_w = []
        for c in range(NDC):
            h_w.append(sch.dma("gpsimd", wqb[:, c * DM:(c + 1) * DM], wq[c * 128:(c + 1) * 128, :], deps=h_trs))
            h_w.append(sch.dma("gpsimd", wkb[:, c * 256:(c + 1) * 256], wk[c * 128:(c + 1) * 128, :], deps=h_trs))
            h_w.append(sch.dma("gpsimd", wvb[:, c * 256:(c + 1) * 256], wv[c * 128:(c + 1) * 128, :], deps=h_trs))
            h_w.append(sch.dma("gpsimd", gwb[:, c * 8:(c + 1) * 8], gw[c * 128:(c + 1) * 128, :], deps=h_trs))
            h_w.append(sch.dma("sync", gwf[:, c * 8:(c + 1) * 8], gw[c * 128:(c + 1) * 128, :], deps=h_trs))

        h_st_xfm = [sch.dma("sync", o_xfm[c * 128:(c + 1) * 128, :], xfm[c][:, :],
                            deps=tr_ev, tag=f"stxfm{c}") for c in range(NDC)]

        h_evx = {}
        h_evx2 = {}
        last_sq_user = {0: None, 1: None}
        for tb in range(4):
            sl = slice(tb * 512, (tb + 1) * 512)
            h_mmx = None
            h_mmx2 = None
            for c in range(NDC):
                h_mmx = sch.op(
                    "tensor",
                    (lambda c, sl: lambda t: t.matmul(psx[0:1, 0:512], onesc[:], xfm[c][:, sl],
                                                      start=(c == 0), stop=(c == NDC - 1)))(c, sl),
                    tr_ev + ([h_evx[tb - 1]] if tb > 0 else []), tag=f"mmx{tb}_{c}")
                i = c % 2
                sq_deps = tr_ev + ([last_sq_user[i]] if last_sq_user[i] else [])
                h_sq = e_act(sch, sq[i][:, 0:512], xfm[c][:, sl], AF.Square, sq_deps,
                             tag=f"sq{tb}_{c}")
                h_mmx2 = sch.op(
                    "tensor",
                    (lambda c, i: lambda t: t.matmul(psx2[0:1, 0:512], onesc[:], sq[i][:, 0:512],
                                                     start=(c == 0), stop=(c == NDC - 1)))(c, i),
                    [h_sq] + ([h_evx2[tb - 1]] if tb > 0 else []), tag=f"mmx2{tb}_{c}")
                last_sq_user[i] = h_mmx2
            h_evx[tb] = e_copy(sch, "vector", srow[0:1, sl], psx[0:1, 0:512], [h_mmx],
                               tag=f"evx{tb}")
            h_evx2[tb] = e_copy(sch, "vector", srow[0:1, TPC + tb * 512:TPC + (tb + 1) * 512],
                                psx2[0:1, 0:512], [h_mmx2], tag=f"evx2{tb}")
        stats_done = list(h_evx.values()) + list(h_evx2.values())

        T = TPC
        h_mu = e_ts(sch, srow[0:1, 0:T], srow[0:1, 0:T], 1.0 / DM, ALU.mult,
                    stats_done + [last_sq_user[0], last_sq_user[1]], tag="mu")
        h_ex2 = e_ts(sch, srow[0:1, T:2 * T], srow[0:1, T:2 * T], 1.0 / DM, ALU.mult, stats_done, tag="ex2")
        h_var_ops = []
        prev = [h_mu, h_ex2]
        for tb in range(4):
            sl0 = slice(tb * 512, (tb + 1) * 512)
            sl1 = slice(T + tb * 512, T + (tb + 1) * 512)
            ha = e_tt(sch, sq[0][0:1, 0:512], srow[0:1, sl0], srow[0:1, sl0], ALU.mult,
                      prev, tag=f"mu2_{tb}")
            hb = e_tt(sch, srow[0:1, sl1], srow[0:1, sl1], sq[0][0:1, 0:512], ALU.subtract,
                      [ha], tag=f"var_{tb}")
            prev = [hb]
            h_var_ops.append(hb)
        h_sd = e_act(sch, srow[0:1, T:2 * T], srow[0:1, T:2 * T], AF.Sqrt,
                     h_var_ops + h_ones, bias=epsc[0:1, 0:1], tag="sd")
        h_rs = sch.op("vector", lambda v: v.reciprocal(srow[0:1, T:2 * T], srow[0:1, T:2 * T]),
                      [h_rs_dep for h_rs_dep in [h_sd]], tag="rs")

        xn_done = []
        prev_scr_dma = {0: None, 1: None}
        xn_lgmm = {0: None, 1: None}
        h_lg_last = None
        prev_bc = None
        for tb in range(4):
            sl = slice(tb * 512, (tb + 1) * 512)
            h_bm = e_matmul_acc(sch, psbc[:, 0:512], [(onesr[:], srow[0:1, sl])],
                                [h_rs] + ([prev_bc] if prev_bc else []), tag=f"bcmu{tb}")
            h_bmv = e_copy(sch, "vector", mub[:, 0:512], psbc[:, 0:512], [h_bm], tag=f"bcmuv{tb}")
            h_br = e_matmul_acc(sch, psbc[:, 0:512],
                                [(onesr[:], srow[0:1, T + tb * 512:T + (tb + 1) * 512])],
                                [h_bmv], tag=f"bcrs{tb}")
            h_brv = e_copy(sch, "vector", rsb[:, 0:512], psbc[:, 0:512], [h_br], tag=f"bcrsv{tb}")
            prev_bc = h_brv
            for c in range(NDC):
                i = c % 2
                deps = [h_bmv, h_brv]
                if prev_scr_dma[i] is not None:
                    deps.extend(prev_scr_dma[i])
                h_s1 = e_tt(sch, xscr[i][:, 0:512], xfm[c][:, sl], mub[:, 0:512], ALU.subtract,
                            deps, tag=f"xnsub{tb}_{c}")
                h_s2 = e_tt(sch, xscr[i][:, 0:512], xscr[i][:, 0:512], rsb[:, 0:512], ALU.mult,
                            [h_s1], tag=f"xnmul{tb}_{c}")
                h_cb = e_copy(sch, "vector", xnb[c][:, sl], xscr[i][:, 0:512], [h_s2],
                              tag=f"xncast{tb}_{c}")
                h_dm = sch.dma("sync", o_xn[c * 128:(c + 1) * 128, tb * 512:(tb + 1) * 512],
                               xscr[i][:, 0:512], deps=[h_s2], tag=f"xndma{tb}_{c}")
                h_lgm = sch.op("tensor",
                               (lambda c, i: lambda t: t.matmul(psl[0:8, 0:512], gwf[:, c * 8:(c + 1) * 8],
                                                                xscr[i][:, 0:512], start=(c == 0),
                                                                stop=(c == NDC - 1),
                                                                skip_group_check=True))(c, i),
                               [h_s2, h_w[-1]] + ([h_lg_last] if h_lg_last else []),
                               tag=f"lgmm{tb}_{c}")
                xn_lgmm[i] = h_lgm
                prev_scr_dma[i] = [h_dm, h_lgm]
                xn_done.append(h_cb)
            h_lg_last = e_act(sch, lgout[0:8, tb * 512:(tb + 1) * 512], psl[0:8, 0:512],
                              AF.Identity, [h_lgm], bias=gbc[:, 0:1], tag=f"lgev{tb}")

        prev_ev_bank = {0: None, 1: None}
        prev_q_dma = {0: None, 1: None}
        for oi in range(NDC + 2):
            for tb in range(4):
                n = oi * 4 + tb
                bank = n % 2
                qb = n % 2
                deps = list(xn_done) + h_w[-2:]
                if prev_ev_bank[bank] is not None:
                    deps.append(prev_ev_bank[bank])
                if oi < NDC:
                    pairs = [(wqb[:, c * DM + oi * 128:c * DM + (oi + 1) * 128],
                              xnb[c][:, tb * 512:(tb + 1) * 512]) for c in range(NDC)]
                    bias_ap = bqc[:, oi:oi + 1]
                    dst = o_q[oi * 128:(oi + 1) * 128, tb * 512:(tb + 1) * 512]
                else:
                    ko = oi - NDC
                    pairs = [(wkb[:, c * 256 + ko * 128:c * 256 + (ko + 1) * 128],
                              xnb[c][:, tb * 512:(tb + 1) * 512]) for c in range(NDC)]
                    bias_ap = bkc[:, ko:ko + 1]
                    dst = o_k[ko * 128:(ko + 1) * 128, tb * 512:(tb + 1) * 512]
                h_mm = e_matmul_acc(sch, pst[bank][:, 0:512], pairs, deps, tag=f"pj{oi}_{tb}")
                ev_deps = [h_mm]
                if prev_q_dma[qb] is not None:
                    ev_deps.append(prev_q_dma[qb])
                h_ev = e_act(sch, qout[qb][:, 0:512], pst[bank][:, 0:512], AF.Identity,
                             ev_deps, bias=bias_ap, tag=f"pjev{oi}_{tb}")
                prev_ev_bank[bank] = h_ev
                h_dma = sch.dma("sync", dst, qout[qb][:, 0:512], deps=[h_ev],
                                tag=f"pjdma{oi}_{tb}")
                prev_q_dma[qb] = h_dma

        h_lg_dma = sch.dma("sync", o_lg[:, :], lgout[:, :], deps=[h_lg_last], tag="lgdma")

        h_bvb_mm = e_matmul_acc(sch, psv[:, 0:256], [(onesr[:], bvr[0:1, :])],
                                h_bias[2:3] + h_ones, tag="bvb")
        h_bvb = e_copy(sch, "vector", bvb[:], psv[:, 0:256], [h_bvb_mm], tag="bvbev")
        prev_v_mm = h_bvb
        prev_v_dma = {0: None, 1: None}
        for t in range(NTT):
            i = t % 2
            pairs = [(xnb[c][:, t * 128:(t + 1) * 128], wvb[:, c * 256:(c + 1) * 256])
                     for c in range(NDC)]
            h_mm = e_matmul_acc(sch, psv[:, 0:256], pairs, list(xn_done) + [prev_v_mm] + h_w[-2:],
                                tag=f"v{t}")
            ev_deps = [h_mm]
            if prev_v_dma[i] is not None:
                ev_deps.append(prev_v_dma[i])
            h_ev = e_tt(sch, vtile[i][:, :], psv[:, 0:256], bvb[:], ALU.add, ev_deps,
                        tag=f"vev{t}")
            prev_v_mm = h_ev
            h_dma = sch.dma("sync", o_v[t * 128:(t + 1) * 128, :], vtile[i][:, :],
                            deps=[h_ev], tag=f"vdma{t}")
            prev_v_dma[i] = h_dma

        if _own:
            sch.emit()
    return nc


def stage_b0_pre(inp):
    xs = shard_tokens(inp["L0_attn_x"])
    wq = f32(inp["L0_attn_Wq"]); wk = f32(inp["L0_attn_Wk"]); wv = f32(inp["L0_attn_Wv"])
    gw = f32(inp["L0_ffn_gate_w"])
    bqc = f32(inp["L0_attn_bq"].reshape(NDC, 128).T)
    bkc = f32(inp["L0_attn_bk"].reshape(2, 128).T)
    bvr = f32(inp["L0_attn_bv"].reshape(1, 256))
    gbc = f32(inp["L0_ffn_gate_b"].reshape(8, 1))
    maps = []
    for c in range(NCORES):
        maps.append(dict(x_tm=xs[c], wq=wq, wk=wk, wv=wv, gw=gw,
                         bqc=bqc, bkc=bkc, bvr=bvr, gbc=gbc, ident=IDENT128))
    return maps


def build_l0_attn(ctx=None):
    nc, _es0, _own = _stage_begin(ctx)
    D = _mk_D(nc, ctx)
    q_d = D("q", [DM, TPC], F32, "ExternalInput")
    k_d = D("k", [256, S], F32, "ExternalInput")
    v_d = D("v", [S, 256], F32, "ExternalInput")
    qidx_d = D("qidx", [128, TPC], mybir.dt.int16, "ExternalInput")
    kidx_d = D("kidx", [128, 64], mybir.dt.int16, "ExternalInput")
    sel16_d = D("sel16", [16, 8 * 128], F32, "ExternalInput")
    wo_d = D("wo", [DM, DM], F32, "ExternalInput")
    lng_d = D("lng", [128, NDC], F32, "ExternalInput")
    lnb_d = D("lnb", [128, NDC], F32, "ExternalInput")
    boc_d = D("boc", [128, NDC], F32, "ExternalInput")
    ident_d = D("ident", [128, 128], F32, "ExternalInput")
    o_s = D("o_s", [DM, TPC], F32, "ExternalOutput")

    NKC = S // 128
    es = _es0
    sb = lambda n, s, d=F32: es.enter_context(nc.sbuf_tensor(_uname(n), s, d))
    psm_ = lambda n, s: es.enter_context(nc.psum_tensor(_uname(n), s, F32))
    with es:
        kb = [sb(f"kb{c}", [128, S], BF16) for c in range(2)]
        vb = sb("vb", [128, NKC * 260], BF16)
        ofm = [sb(f"ofm{c}", [128, TPC], BF16) for c in range(NDC)]
        NPT = 4
        pt = sb("pt", [128, NPT * 1024], BF16)
        qi16 = sb("qi16", [128, TPC], mybir.dt.int16)
        kic16 = sb("kic16", [128, 64], mybir.dt.int16)
        sel16b = sb("sel16b", [16, 8 * 128])
        surow = sb("surow", [16, TPC])
        ssr = sb("ssr", [1, 2 * 512])
        qz = [sb(f"qz{i}", [128, TPC], BF16) for i in range(2)]
        onesr = sb("onesr", [1, 128])
        onesc_b = sb("onescb", [128, 1], BF16)
        sqb = [sb(f"sqb{i}", [128, 512], BF16) for i in range(2)]
        srow = sb("srow", [1, 2 * TPC])
        mub = sb("mub", [128, 512])
        rsb = sb("rsb", [128, 512])
        epsc = sb("epsc", [1, 1])
        wob = sb("wob", [128, NDC * DM], BF16)
        lngc = sb("lngc", [128, NDC])
        lnbc = sb("lnbc", [128, NDC])
        boc = sb("boc_s", [128, NDC])
        onb = [sb(f"onb{i}", [128, 512], BF16) for i in range(2)]
        sout = [sb(f"sout{i}", [128, 512]) for i in range(2)]
        idn = sb("idn", [128, 128])
        st4 = psm_("st4", [128, 3072])
        ot = [psm_(f"ot{i}", [128, 512]) for i in range(2)]
        psb = ot[0][:, 0:512]
        psx = st4[0:1, 0:512]
        psx2 = st4[0:1, 512:1024]
        psm2 = [st4[:, 1024:1536]]

        sch = ctx.sch if ctx else Sched(nc, es)
        if k_d.shape[0] == 4 * 256:
            kprs = []
            for c in range(2):
                for r in range(4):
                    kprs.append((kb[c][:, r * TPC:(r + 1) * TPC],
                                 k_d[r * 256 + c * 128:r * 256 + (c + 1) * 128, :]))
            h_k = sch.dma_batch("gpsimd", kprs, tag="ldk")
        else:
            h_k = sch.dma_batch("gpsimd", [(kb[c][:, :], k_d[c * 128:(c + 1) * 128, :]) for c in range(2)], tag="ldk")
        h_vmemset = sch.op("vector", lambda v: v.memset(vb[:], 1.0), tag="vms")
        vb3 = vb.rearrange("p (n h w) -> p n h w", n=NKC, h=4, w=65)
        h_v = sch.dma_batch("gpsimd",
                            [(vb3[:, kc, :, 0:64], v_d[kc * 128:(kc + 1) * 128, :].rearrange("p (h w) -> p h w", w=64))
                             for kc in range(NKC)],
                            deps=[h_vmemset], tag="ldv")
        h_wo = sch.dma_batch("gpsimd", [(wob[:, c * DM:(c + 1) * DM], wo_d[c * 128:(c + 1) * 128, :]) for c in range(NDC)], tag="ldwo")
        h_small = sch.dma_batch("sync", [
            (kic16[:, :], kidx_d[:, :]), (lngc[:, :], lng_d[:, :]), (lnbc[:, :], lnb_d[:, :]),
            (boc[:, :], boc_d[:, :]), (idn[:], ident_d[:, :]),
            (qi16[:, :], qidx_d[:, :]), (sel16b[:, :], sel16_d[:, :]),
        ], tag="ldsm")
        h_ones = [sch.op("vector", lambda v: v.memset(onesr[:], 1.0)),
                  sch.op("vector", lambda v: v.memset(onesc_b[:], 1.0)),
                  sch.op("vector", lambda v: v.memset(epsc[:], 1e-5))]
        h_qzms = [sch.op("vector", (lambda i: lambda v: v.memset(qz[i][:], 0.0))(i), tag=f"qzms{i}")
                  for i in range(2)]
        h_qi = [h_small]

        NPAIR = NKC // 2
        tasks = [(h, qbl, p) for h in range(16) for qbl in range(4) for p in range(NPAIR)]
        NT = len(tasks)
        NSL = 6
        st_user = {i: None for i in range(NSL)}
        pt_user = {i: None for i in range(NPT)}
        qz_user = {0: None, 1: None}
        ot_free = {0: None, 1: None}
        h_qzc = {}
        S_h, E_h, M_h, A_h = {}, {}, {}, {}
        ocp = {}
        sums_copies = []
        ssr_user = {i: None for i in range(2)}

        def emit_qz(h):
            qbase = 64 * (h % 2)
            zs = h % 2
            h_qzc[h] = sch.dma("gpsimd", qz[zs][qbase:qbase + 64, :], q_d[h * 64:(h + 1) * 64, :],
                               deps=[h_qzms[zs]] + ([qz_user[zs]] if qz_user[zs] else []),
                               tag=f"qz{h}")

        def emit_S(p):
            h, qbl, pr = tasks[p]
            kcb = kb[(h % 4) // 2]
            zs = h % 2
            slq = slice(qbl * 512, (qbl + 1) * 512)
            hs = []
            for j in range(2):
                kc = 2 * pr + j
                i = (2 * p + j) % NSL
                deps = [h_k, h_qzc[h]] + h_qi
                if st_user[i] is not None:
                    deps.append(st_user[i])
                h_st = sch.op("tensor",
                              (lambda i, kcb, zs, slq, kc:
                               lambda t: t.matmul(st4[:, i * 512:(i + 1) * 512],
                                                  kcb[:, kc * 128:(kc + 1) * 128],
                                                  qz[zs][:, slq],
                                                  start=True, stop=True, skip_group_check=True))(
                                  i, kcb, zs, slq, kc),
                              deps, tag=f"st{p}_{j}")
                hs.append(h_st)
                qz_user[zs] = h_st
            S_h[p] = hs

        def emit_E(p):
            i0 = (2 * p) % NSL
            pi = p % NPT
            deps = list(S_h.pop(p))
            if pt_user[pi] is not None:
                deps.append(pt_user[pi])
            h_exp = e_act(sch, pt[:, pi * 1024:(pi + 1) * 1024],
                          st4[:, i0 * 512:(i0 + 2) * 512], AF.Exp, deps,
                          scale=0.125, tag=f"ex{p}")
            st_user[i0] = h_exp
            st_user[i0 + 1] = h_exp
            E_h[p] = h_exp

        def emit_M(p):
            h, qbl, pr = tasks[p]
            pi = p % NPT
            slq = slice(qbl * 512, (qbl + 1) * 512)
            hm = E_h.pop(p)
            for j in range(2):
                kc = 2 * pr + j
                hm = e_stt(sch, pt[:, pi * 1024 + j * 512:pi * 1024 + (j + 1) * 512],
                           qi16[:, slq], kic16[:, kc:kc + 1],
                           pt[:, pi * 1024 + j * 512:pi * 1024 + (j + 1) * 512],
                           ALU.is_ge, ALU.mult, [hm], tag=f"mk{p}_{j}")
            M_h[p] = hm

        def emit_A(p):
            h, qbl, pr = tasks[p]
            kh = h % 4
            pi = p % NPT
            oslot = (h * 4 + qbl) % 2
            deps = [M_h.pop(p), h_v]
            if pr > 0:
                deps.append(A_h[p - 1])
            elif ot_free[oslot] is not None:
                deps.extend(ot_free[oslot])
            h_ot = None
            for j in range(2):
                kc = 2 * pr + j
                h_ot = sch.op("tensor",
                              (lambda oslot, kc, kh, pi, j:
                               lambda t: t.matmul(ot[oslot][0:65, 0:512],
                                                  vb3[:, kc, kh, :],
                                                  pt[:, pi * 1024 + j * 512:pi * 1024 + (j + 1) * 512],
                                                  start=(kc == 0), stop=(kc == NKC - 1),
                                                  skip_group_check=True))(oslot, kc, kh, pi, j),
                              deps, tag=f"ot{p}_{j}")
                deps = [h_ot]
            A_h[p] = h_ot
            pt_user[pi] = h_ot
            if pr == NPAIR - 1:
                obase = 64 * (h % 2)
                slq = slice(qbl * 512, (qbl + 1) * 512)
                h_oc = e_act(sch, ofm[h // 2][obase:obase + 64, slq], ot[oslot][0:64, 0:512],
                             AF.Copy, [h_ot], tag=f"oc{h}_{qbl}")
                ssl = (h * 4 + qbl) % 2
                h_ss = e_act(sch, ssr[0:1, ssl * 512:(ssl + 1) * 512], ot[oslot][64:65, 0:512],
                             AF.Copy, [h_oc] + ([ssr_user[ssl]] if ssr_user[ssl] else []),
                             tag=f"ss{h}_{qbl}")
                h_sc = sch.dma("sync", surow[h:h + 1, slq], ssr[0:1, ssl * 512:(ssl + 1) * 512],
                               deps=[h_ss], tag=f"sc{h}_{qbl}")
                ssr_user[ssl] = h_sc
                ocp[(h, qbl)] = [h_oc, h_sc]
                sums_copies.append(h_sc)
                ot_free[oslot] = [h_ss]

        emit_qz(0)
        emit_qz(1)
        h_wo = sch.dma_batch("gpsimd", [(wob[:, c * DM:(c + 1) * DM], wo_d[c * 128:(c + 1) * 128, :]) for c in range(NDC)], tag="ldwo")
        for p in range(NT + 2):
            for h in range(2, 16):
                if p == h * 4 * NPAIR - 6:
                    emit_qz(h)
            if p < NT:
                emit_S(p)
            if 0 <= p - 1 < NT:
                emit_E(p - 1)
                emit_M(p - 1)
            if 0 <= p - 2 < NT:
                emit_A(p - 2)

        h_rr16 = sch.op("vector", lambda v: v.reciprocal(surow[0:16, 0:TPC], surow[0:16, 0:TPC]),
                        sums_copies, tag="rr16")
        attn_done = []
        rb_user = {0: None, 1: None, 2: None}
        hi = 0
        for c in range(NDC):
            for u in range(2):
                sl2 = hi % 3
                hi += 1
                deps = [h_rr16, st_user[2 * sl2], st_user[2 * sl2 + 1]]
                if rb_user[sl2] is not None:
                    deps.append(rb_user[sl2])
                mm = []
                for w in range(2):
                    h_mm = e_matmul_acc(
                        sch, st4[:, (2 * sl2 + w) * 512:(2 * sl2 + w + 1) * 512],
                        [(sel16b[0:16, c * 128:(c + 1) * 128],
                          surow[0:16, u * 1024 + w * 512:u * 1024 + (w + 1) * 512])],
                        [d for d in deps if d is not None], tag=f"rbm{c}_{u}_{w}")
                    mm.append(h_mm)
                h_rt = e_tt(sch, ofm[c][:, u * 1024:(u + 1) * 1024],
                            ofm[c][:, u * 1024:(u + 1) * 1024],
                            st4[:, (2 * sl2) * 512:(2 * sl2 + 2) * 512], ALU.mult,
                            mm + ocp[(2 * c, 2 * u)] + ocp[(2 * c, 2 * u + 1)]
                            + ocp[(2 * c + 1, 2 * u)] + ocp[(2 * c + 1, 2 * u + 1)],
                            tag=f"rbt{c}_{u}")
                rb_user[sl2] = h_rt
                attn_done.append(h_rt)

        h_evx = {}
        h_evx2 = {}
        last_sq_user = {0: None, 1: None}
        for tb in range(4):
            sl = slice(tb * 512, (tb + 1) * 512)
            h_mmx = None
            h_mmx2 = None
            for c in range(NDC):
                h_mmx = sch.op(
                    "tensor",
                    (lambda c, sl: lambda t: t.matmul(psx[0:1, 0:512], onesc_b[:], ofm[c][:, sl],
                                                      start=(c == 0), stop=(c == NDC - 1),
                                                      skip_group_check=True))(c, sl),
                    attn_done + ([h_evx[tb - 1]] if tb > 0 else []), tag=f"lmmx{tb}_{c}")
                i = c % 2
                sq_deps = attn_done + ([last_sq_user[i]] if last_sq_user[i] else [])
                h_sq = e_act(sch, sqb[i][:, 0:512], ofm[c][:, sl], AF.Square, sq_deps,
                             tag=f"lsq{tb}_{c}")
                h_mmx2 = sch.op(
                    "tensor",
                    (lambda c, i: lambda t: t.matmul(psx2[0:1, 0:512], onesc_b[:], sqb[i][:, 0:512],
                                                     start=(c == 0), stop=(c == NDC - 1),
                                                     skip_group_check=True))(c, i),
                    [h_sq] + ([h_evx2[tb - 1]] if tb > 0 else []), tag=f"lmmx2{tb}_{c}")
                last_sq_user[i] = h_mmx2
            h_evx[tb] = e_copy(sch, "vector", srow[0:1, sl], psx[0:1, 0:512], [h_mmx], tag=f"levx{tb}")
            h_evx2[tb] = e_copy(sch, "vector", srow[0:1, TPC + tb * 512:TPC + (tb + 1) * 512],
                                psx2[0:1, 0:512], [h_mmx2], tag=f"levx2{tb}")
        stats_done = list(h_evx.values()) + list(h_evx2.values())
        T = TPC
        h_mu = e_ts(sch, srow[0:1, 0:T], srow[0:1, 0:T], 1.0 / DM, ALU.mult,
                    stats_done + [last_sq_user[0], last_sq_user[1]], tag="lmu")
        h_ex2 = e_ts(sch, srow[0:1, T:2 * T], srow[0:1, T:2 * T], 1.0 / DM, ALU.mult, stats_done, tag="lex2")
        h_var_ops = []
        prev = [h_mu, h_ex2]
        for tb in range(4):
            sl0 = slice(tb * 512, (tb + 1) * 512)
            sl1 = slice(T + tb * 512, T + (tb + 1) * 512)
            ha = e_tt(sch, ssr[0:1, 0:512], srow[0:1, sl0], srow[0:1, sl0], ALU.mult, prev, tag=f"lmu2_{tb}")
            hb = e_tt(sch, srow[0:1, sl1], srow[0:1, sl1], ssr[0:1, 0:512], ALU.subtract, [ha], tag=f"lvar_{tb}")
            prev = [hb]
            h_var_ops.append(hb)
        h_sd = e_act(sch, srow[0:1, T:2 * T], srow[0:1, T:2 * T], AF.Sqrt,
                     h_var_ops + h_ones, bias=epsc[0:1, 0:1], tag="lsd")
        h_rs = sch.op("vector", lambda v: v.reciprocal(srow[0:1, T:2 * T], srow[0:1, T:2 * T]),
                      [h_sd], tag="lrs")

        prev_bc = None
        norm_done = {}
        for tb in range(4):
            sl = slice(tb * 512, (tb + 1) * 512)
            h_bm = e_matmul_acc(sch, psb[:, 0:512], [(onesr[:], srow[0:1, sl])],
                                [h_rs] + ([prev_bc] if prev_bc else []) + attn_done, tag=f"lbcmu{tb}")
            h_bmv = e_copy(sch, "vector", mub[:, 0:512], psb[:, 0:512], [h_bm], tag=f"lbcmuv{tb}")
            h_br = e_matmul_acc(sch, psb[:, 0:512],
                                [(onesr[:], srow[0:1, T + tb * 512:T + (tb + 1) * 512])],
                                [h_bmv], tag=f"lbcrs{tb}")
            h_brv = e_copy(sch, "vector", rsb[:, 0:512], psb[:, 0:512], [h_br], tag=f"lbcrsv{tb}")
            prev_bc = h_brv
            for c in range(NDC):
                h_s1 = e_tt(sch, onb[0][:, 0:512],
                            ofm[c][:, sl], mub[:, 0:512], ALU.subtract, [h_bmv, h_brv],
                            tag=f"lnsub{tb}_{c}")
                h_s2 = e_tt(sch, onb[1][:, 0:512], onb[0][:, 0:512], rsb[:, 0:512], ALU.mult,
                            [h_s1], tag=f"lnmul{tb}_{c}")
                h_s3 = e_stt(sch, ofm[c][:, sl], onb[1][:, 0:512], lngc[:, c:c + 1],
                             lnbc[:, c:c + 1].broadcast_to([128, 512]), ALU.mult, ALU.add,
                             [h_s2], tag=f"lngb{tb}_{c}")
                norm_done[(tb, c)] = h_s3
        all_norm = [norm_done[(tb, c)] for tb in range(4) for c in range(NDC)]

        prev_bank = {0: None}
        prev_dma = {0: None, 1: None}
        for oi in range(NDC):
            for tb in range(4):
                n = oi * 4 + tb
                qbuf = n % 2
                deps = list(all_norm) + [h_wo]
                if prev_bank[0] is not None:
                    deps.append(prev_bank[0])
                pairs = [(wob[:, c * DM + oi * 128:c * DM + (oi + 1) * 128],
                          ofm[c][:, tb * 512:(tb + 1) * 512]) for c in range(NDC)]
                h_mm = e_matmul_acc(sch, psm2[0][:, 0:512], pairs, deps, tag=f"wo{oi}_{tb}")
                ev_deps = [h_mm]
                if prev_dma[qbuf] is not None:
                    ev_deps.append(prev_dma[qbuf])
                h_ev = e_act(sch, sout[qbuf][:, 0:512], psm2[0][:, 0:512], AF.Identity,
                             ev_deps, bias=boc[:, oi:oi + 1], tag=f"woev{oi}_{tb}")
                prev_bank[0] = h_ev
                h_dma = sch.dma("sync", o_s[oi * 128:(oi + 1) * 128, tb * 512:(tb + 1) * 512],
                                sout[qbuf][:, 0:512], deps=[h_ev], tag=f"wodma{oi}_{tb}")
                prev_dma[qbuf] = h_dma

        if _own:
            sch.emit()
    return nc


def stage_l0_attn(inp, pre_results):
    kfull = {}
    vfull = {}
    for b in range(B):
        ks = [pre_results[b * GPB + g]["o_k"] for g in range(GPB)]
        vs = [pre_results[b * GPB + g]["o_v"] for g in range(GPB)]
        kfull[b] = np.ascontiguousarray(np.concatenate(ks, axis=1))
        vfull[b] = np.ascontiguousarray(np.concatenate(vs, axis=0))
    wo = f32(inp["L0_attn_Wo"])
    lng = f32(inp["L0_attn_ln_g"].reshape(NDC, 128).T)
    lnb = f32(inp["L0_attn_ln_b"].reshape(NDC, 128).T)
    boc = f32(inp["L0_attn_bo"].reshape(NDC, 128).T)
    kidx = np.ascontiguousarray(
        (np.arange(64)[None, :] * 128 + np.arange(128)[:, None]).astype(np.int16))
    maps = []
    for c in range(NCORES):
        b = c // GPB
        qidx = np.ascontiguousarray(np.broadcast_to(
            ((c % GPB) * TPC + np.arange(TPC)).astype(np.int16)[None, :], (128, TPC)))
        maps.append(dict(q=f32(pre_results[c]["o_q"]), k=kfull[b], v=vfull[b],
                         qidx=qidx, kidx=kidx, sel16=_sel16_table(),
                         wo=wo, lng=lng, lnb=lnb, boc=boc,
                         ident=IDENT128))
    return maps


def _sel16_table():
    t = np.zeros((16, 8 * 128), np.float32)
    for c in range(8):
        t[2 * c, c * 128:c * 128 + 64] = 1.0
        t[2 * c + 1, c * 128 + 64:(c + 1) * 128] = 1.0
    return t


def build_l0_moe(ctx=None):
    nc, _es0, _own = _stage_begin(ctx)
    D = _mk_D(nc, ctx)
    xn_d = D("xn", [DM, TPC], F32, "ExternalInput")
    lgf_d = D("lgf", [8, S], F32, "ExternalInput")
    lgm_d = D("lgm", [8, TPC], F32, "ExternalInput")
    w1_d = D("w1", [8 * DM, DM], F32, "ExternalInput")
    w2_d = D("w2", [8 * DM, DM], F32, "ExternalInput")
    w3_d = D("w3", [8 * DM, DM], F32, "ExternalInput")
    b1_d = D("b1", [128, 8 * NDC], F32, "ExternalInput")
    b2_d = D("b2", [128, 8 * NDC], F32, "ExternalInput")
    b3_d = D("b3", [1, 8 * DM], F32, "ExternalInput")
    eye8_d = D("eye8", [8, 8], F32, "ExternalInput")
    ident_d = D("ident", [128, 128], F32, "ExternalInput")
    o_vpart = D("o_vpart", [1, 8 * DM], F32, "ExternalOutput")
    o_ind = D("o_ind", [8, TPC], F32, "ExternalOutput")

    es = _es0
    sb = lambda n, s, d=F32: es.enter_context(nc.sbuf_tensor(_uname(n), s, d))
    psm_ = lambda n, s: es.enter_context(nc.psum_tensor(_uname(n), s, F32))
    with es:
        xnb = [sb(f"xnb{c}", [128, TPC], BF16) for c in range(NDC)]
        h1b = [sb(f"h1b{c}", [128, TPC], BF16) for c in range(NDC)]
        h2b = [sb(f"h2b{c}", [128, TPC], BF16) for c in range(NDC)]
        wexp = sb("wexp", [128, NDC * DM], BF16)
        lgt = sb("lgt", [8, 512])
        lgm = sb("lgm_s", [8, TPC])
        macc = sb("macc", [8, 4])
        escr = sb("escr", [8, 512], BF16)
        srow8 = sb("srow8", [8, 1])
        ptile = sb("ptile", [128, 8 * NTT])
        zt = sb("zt", [128, 8])
        yt = sb("yt", [128, 8])
        m1c = sb("m1c", [128, 1])
        m2c = sb("m2c", [128, 1])
        wtb = sb("wtb", [128, 8 * NTT])
        indf = sb("indf", [8, TPC])
        eye8 = sb("eye8_s", [8, 8])
        idn = sb("idn", [128, 128])
        onesr = sb("onesr", [1, 128])
        b1c = sb("b1c", [128, 8 * NDC])
        b2c = sb("b2c", [128, 8 * NDC])
        b3r = sb("b3r", [1, 8 * DM])
        b3b = sb("b3b", [128, DM])
        ytf = [sb(f"ytf{i}", [128, 512]) for i in range(2)]
        vout = sb("vout", [1, 8 * DM])
        pst = [psm_(f"pst{i}", [128, 512]) for i in range(2)]
        psv = psm_("psv", [1, DM])
        pstr = psm_("pstr", [128, 512])
        psbc = psm_("psbc", [128, 512])

        sch = ctx.sch if ctx else Sched(nc, es)
        h_xn = sch.dma_batch("gpsimd", [(xnb[c][:, :], xn_d[c * 128:(c + 1) * 128, :]) for c in range(NDC)], tag="ldxn")
        h_lgm = sch.dma_batch("sync", [(lgm[:, :], lgm_d[:, :])], tag="ldlgm")
        h_small = sch.dma_batch("sync", [
            (eye8[:, :], eye8_d[:, :]), (idn[:], ident_d[:, :]),
            (b1c[:, :], b1_d[:, :]), (b2c[:, :], b2_d[:, :]), (b3r[:, :], b3_d[:, :]),
        ], tag="ldsm")
        h_ones = sch.op("vector", lambda v: v.memset(onesr[:], 1.0))

        h_m = e_memset(sch, "vector", macc[:, 0:1], -1e30, tag="m0")
        prev = h_m
        def lgf_tile(t):
            if lgf_d.shape[0] == 32:
                r, lt = t // 4, t % 4
                return lgf_d[8 * r:8 * r + 8, lt * 512:(lt + 1) * 512]
            return lgf_d[:, t * 512:(t + 1) * 512]
        h_lgtile = []
        for t in range(S // 512):
            h_ld = sch.dma_batch("sync", [(lgt[:, :], lgf_tile(t))],
                                 deps=[prev], tag=f"lgld{t}")
            h_r = e_red(sch, srow8[:, 0:1], lgt[:, :], ALU.max, [h_ld], tag=f"lgr{t}")
            prev = e_tt(sch, macc[:, 0:1], macc[:, 0:1], srow8[:, 0:1], ALU.max, [h_r], tag=f"lgm{t}")
        h_negm = e_ts(sch, macc[:, 1:2], macc[:, 0:1], -1.0, ALU.mult, [prev], tag="negm")
        h_s0 = e_memset(sch, "vector", macc[:, 2:3], 0.0, tag="s0")
        prev = [h_negm, h_s0]
        for t in range(S // 512):
            h_ld = sch.dma_batch("sync", [(lgt[:, :], lgf_tile(t))],
                                 deps=prev, tag=f"lgld2{t}")
            h_e = e_act(sch, escr[:, :], lgt[:, :], AF.Exp, [h_ld], bias=macc[:, 1:2],
                        accum_out=srow8[:, 0:1], tag=f"lge{t}")
            prev = [e_tt(sch, macc[:, 2:3], macc[:, 2:3], srow8[:, 0:1], ALU.add, [h_e], tag=f"lgs{t}")]
        h_rs = sch.op("vector", lambda v: v.reciprocal(macc[:, 3:4], macc[:, 2:3]), prev, tag="lgrs")

        h_pm = e_act(sch, lgm[:, :], lgm[:, :], AF.Exp, [h_rs, h_lgm], bias=macc[:, 1:2], tag="pm")
        h_pm2 = e_ts(sch, lgm[:, :], lgm[:, :], macc[:, 3:4], ALU.mult, [h_pm], tag="pm2")

        ind_done = []
        wt_done = []
        prev_tr = None
        for t in range(NTT):
            deps = [h_pm2, h_small]
            if prev_tr is not None:
                deps.append(prev_tr)
            h_tr = e_matmul_acc(sch, pstr[:, 0:8], [(lgm[:, t * 128:(t + 1) * 128], eye8[:, :])],
                                deps, tag=f"ptr{t}")
            h_pt = e_copy(sch, "vector", ptile[:, t * 8:(t + 1) * 8], pstr[:, 0:8], [h_tr], tag=f"ptev{t}")
            prev_tr = h_pt
            pt_ap = ptile[:, t * 8:(t + 1) * 8]
            h_m1 = e_red(sch, m1c[:, :], pt_ap, ALU.max, [h_pt], tag=f"m1_{t}")
            h_z = e_ts(sch, zt[:, :], pt_ap, m1c[:, 0:1], ALU.is_equal, [h_m1], tag=f"z{t}")
            h_y = e_stt(sch, yt[:, :], zt[:, :], -1e30, pt_ap, ALU.mult, ALU.add, [h_z], tag=f"y{t}")
            h_m2 = e_red(sch, m2c[:, :], yt[:, :], ALU.max, [h_y], tag=f"m2_{t}")
            h_wt = e_stt(sch, wtb[:, t * 8:(t + 1) * 8], pt_ap, m2c[:, 0:1], pt_ap,
                         ALU.is_ge, ALU.mult, [h_m2], tag=f"wt{t}")
            h_in = e_ts(sch, zt[:, :], pt_ap, m2c[:, 0:1], ALU.is_ge, [h_wt], tag=f"in{t}")
            h_it = e_matmul_acc(sch, pstr[0:8, 0:128], [(zt[:, :], idn[:, :])], [h_in], tag=f"it{t}")
            h_ie = e_copy(sch, "vector", indf[:, t * 128:(t + 1) * 128], pstr[0:8, 0:128],
                          [h_it], tag=f"ie{t}")
            prev_tr = h_ie
            ind_done.append(h_ie)
            wt_done.append(h_wt)
        h_ind_dma = sch.dma_batch("sync", [(o_ind[:, :], indf[:, :])], deps=ind_done, tag="inddma")

        h_w_prev = None
        h_v_ev_prev = None
        for e in range(8):
            h_w1 = sch.dma_batch("gpsimd",
                                 [(wexp[:, c * DM:(c + 1) * DM], w1_d[e * DM + c * 128:e * DM + (c + 1) * 128, :])
                                  for c in range(NDC)],
                                 deps=([h_w_prev] if h_w_prev else []) +
                                      ([h_v_ev_prev] if h_v_ev_prev else []), tag=f"ldw1_{e}")
            h1_done = []
            prev_bank = {0: None, 1: None}
            for oi in range(NDC):
                for tb in range(4):
                    bank = (oi * 4 + tb) % 2
                    deps = [h_w1, h_xn]
                    if prev_bank[bank] is not None:
                        deps.append(prev_bank[bank])
                    pairs = [(wexp[:, c * DM + oi * 128:c * DM + (oi + 1) * 128],
                              xnb[c][:, tb * 512:(tb + 1) * 512]) for c in range(NDC)]
                    h_mm = e_matmul_acc(sch, pst[bank][:, 0:512], pairs, deps, tag=f"e{e}h1_{oi}_{tb}")
                    h_ev = e_act(sch, h1b[oi][:, tb * 512:(tb + 1) * 512], pst[bank][:, 0:512],
                                 AF.Relu, [h_mm], bias=b1c[:, e * NDC + oi:e * NDC + oi + 1],
                                 tag=f"e{e}h1ev_{oi}_{tb}")
                    prev_bank[bank] = h_ev
                    h1_done.append(h_ev)
            h_w2 = sch.dma_batch("gpsimd",
                                 [(wexp[:, c * DM:(c + 1) * DM], w2_d[e * DM + c * 128:e * DM + (c + 1) * 128, :])
                                  for c in range(NDC)],
                                 deps=h1_done[-1:] + [h_w1], tag=f"ldw2_{e}")
            h2_done = []
            prev_bank = {0: None, 1: None}
            for oi in range(NDC):
                for tb in range(4):
                    bank = (oi * 4 + tb) % 2
                    deps = [h_w2] + h1_done
                    if prev_bank[bank] is not None:
                        deps.append(prev_bank[bank])
                    pairs = [(wexp[:, c * DM + oi * 128:c * DM + (oi + 1) * 128],
                              h1b[c][:, tb * 512:(tb + 1) * 512]) for c in range(NDC)]
                    h_mm = e_matmul_acc(sch, pst[bank][:, 0:512], pairs, deps, tag=f"e{e}h2_{oi}_{tb}")
                    h_ev = e_act(sch, h2b[oi][:, tb * 512:(tb + 1) * 512], pst[bank][:, 0:512],
                                 AF.Relu, [h_mm], bias=b2c[:, e * NDC + oi:e * NDC + oi + 1],
                                 tag=f"e{e}h2ev_{oi}_{tb}")
                    prev_bank[bank] = h_ev
                    h2_done.append(h_ev)
            h_w3 = sch.dma_batch("gpsimd",
                                 [(wexp[:, c * DM:(c + 1) * DM], w3_d[e * DM + c * 128:e * DM + (c + 1) * 128, :])
                                  for c in range(NDC)],
                                 deps=h2_done[-1:] + [h_w2], tag=f"ldw3_{e}")
            b3_done = []
            for half in range(2):
                h_b3m = e_matmul_acc(sch, psbc[:, 0:512],
                                     [(onesr[:], b3r[0:1, e * DM + half * 512:e * DM + (half + 1) * 512])],
                                     [h_small, h_ones] + ([h_v_ev_prev] if h_v_ev_prev else []) + b3_done,
                                     tag=f"e{e}b3m{half}")
                h_b3e = e_copy(sch, "vector", b3b[:, half * 512:(half + 1) * 512], psbc[:, 0:512],
                               [h_b3m], tag=f"e{e}b3e{half}")
                b3_done.append(h_b3e)
            v_mm_last = None
            prev_y = {0: None, 1: None}
            for t in range(NTT):
                for half in range(2):
                    i = half
                    deps = [h_w3] + h2_done
                    if prev_y[i] is not None:
                        deps.append(prev_y[i])
                    pairs = [(h2b[c][:, t * 128:(t + 1) * 128],
                              wexp[:, c * DM + half * 512:c * DM + (half + 1) * 512]) for c in range(NDC)]
                    h_mm = e_matmul_acc(sch, pst[i][:, 0:512], pairs, deps, tag=f"e{e}y{t}_{half}")
                    h_add = e_stt(sch, ytf[i][:, 0:512], pst[i][:, 0:512], 1.0,
                                  b3b[:, half * 512:(half + 1) * 512], ALU.mult, ALU.add,
                                  [h_mm] + b3_done, tag=f"e{e}ya{t}_{half}")
                    h_rel = e_act(sch, ytf[i][:, 0:512], ytf[i][:, 0:512], AF.Relu, [h_add],
                                  tag=f"e{e}yr{t}_{half}")
                    h_vm = sch.op("tensor",
                                  (lambda t, half, i, e:
                                   lambda tt: tt.matmul(psv[0:1, half * 512:(half + 1) * 512],
                                                        wtb[:, t * 8 + e:t * 8 + e + 1],
                                                        ytf[i][:, 0:512],
                                                        start=(t == 0), stop=(t == NTT - 1),
                                                        skip_group_check=True))(t, half, i, e),
                                  [h_rel] + wt_done, tag=f"e{e}vm{t}_{half}")
                    prev_y[i] = h_vm
                    v_mm_last = h_vm
            h_v_ev = e_copy(sch, "vector", vout[0:1, e * DM:(e + 1) * DM], psv[0:1, 0:DM],
                            [v_mm_last], tag=f"e{e}vev")
            h_v_ev_prev = h_v_ev
            h_w_prev = h_w3
        h_v_dma = sch.dma_batch("sync", [(o_vpart[:, :], vout[:, :])], deps=[h_v_ev_prev], tag="vdma")

        if _own:
            sch.emit()
    return nc


def stage_l0_moe(inp, pre_results):
    lgfull = {}
    for b in range(B):
        lgs = [pre_results[b * GPB + g]["o_lg"] for g in range(GPB)]
        lgfull[b] = np.ascontiguousarray(np.concatenate(lgs, axis=1))
    w1 = f32(inp["L0_ffn_w1"].reshape(8 * DM, DM))
    w2 = f32(inp["L0_ffn_w2"].reshape(8 * DM, DM))
    w3 = f32(inp["L0_ffn_w3"].reshape(8 * DM, DM))
    b1 = f32(inp["L0_ffn_b1"].reshape(8, NDC, 128).transpose(2, 0, 1).reshape(128, 8 * NDC))
    b2 = f32(inp["L0_ffn_b2"].reshape(8, NDC, 128).transpose(2, 0, 1).reshape(128, 8 * NDC))
    b3 = f32(inp["L0_ffn_b3"].reshape(1, 8 * DM))
    eye8 = np.eye(8, dtype=np.float32)
    maps = []
    for c in range(NCORES):
        b = c // GPB
        maps.append(dict(xn=f32(pre_results[c]["o_xn"]), lgf=lgfull[b],
                         lgm=f32(pre_results[c]["o_lg"]), w1=w1, w2=w2, w3=w3,
                         b1=b1, b2=b2, b3=b3, eye8=eye8, ident=IDENT128))
    return maps


def emit_ln_fm(sch, bufs, xs, out_cb, deps, T=TPC, tag="ln"):
    sq, srow, mub, rsb = bufs["sq"], bufs["srow"], bufs["mub"], bufs["rsb"]
    onesc, onesr, epsc = bufs["onesc"], bufs["onesr"], bufs["epsc"]
    psx, psx2, psbc = bufs["psx"], bufs["psx2"], bufs["psbc"]
    vscr = bufs["vscr"]
    nb = T // 512
    h_evx, h_evx2 = {}, {}
    last_sq_user = {0: None, 1: None}
    for tb in range(nb):
        sl = slice(tb * 512, (tb + 1) * 512)
        h_mmx = h_mmx2 = None
        for c in range(NDC):
            h_mmx = sch.op(
                "tensor",
                (lambda c, sl: lambda t: t.matmul(psx[0:1, 0:512], onesc[:], xs[c][:, sl],
                                                  start=(c == 0), stop=(c == NDC - 1),
                                                  skip_group_check=True))(c, sl),
                deps + ([h_evx[tb - 1]] if tb > 0 else []), tag=f"{tag}mmx{tb}_{c}")
            i = c % 2
            sq_deps = deps + ([last_sq_user[i]] if last_sq_user[i] else [])
            h_sq = e_act(sch, sq[i][:, 0:512], xs[c][:, sl], AF.Square, sq_deps,
                         tag=f"{tag}sq{tb}_{c}")
            h_mmx2 = sch.op(
                "tensor",
                (lambda c, i: lambda t: t.matmul(psx2[0:1, 0:512], onesc[:], sq[i][:, 0:512],
                                                 start=(c == 0), stop=(c == NDC - 1),
                                                 skip_group_check=True))(c, i),
                [h_sq] + ([h_evx2[tb - 1]] if tb > 0 else []), tag=f"{tag}mmx2{tb}_{c}")
            last_sq_user[i] = h_mmx2
        h_evx[tb] = e_copy(sch, "vector", srow[0:1, sl], psx[0:1, 0:512], [h_mmx], tag=f"{tag}evx{tb}")
        h_evx2[tb] = e_copy(sch, "vector", srow[0:1, T + tb * 512:T + (tb + 1) * 512],
                            psx2[0:1, 0:512], [h_mmx2], tag=f"{tag}evx2{tb}")
    stats_done = list(h_evx.values()) + list(h_evx2.values())
    h_mu = e_ts(sch, srow[0:1, 0:T], srow[0:1, 0:T], 1.0 / DM, ALU.mult,
                stats_done + [last_sq_user[0], last_sq_user[1]], tag=f"{tag}mu")
    h_ex2 = e_ts(sch, srow[0:1, T:2 * T], srow[0:1, T:2 * T], 1.0 / DM, ALU.mult,
                 stats_done, tag=f"{tag}ex2")
    prev = [h_mu, h_ex2]
    h_var_ops = []
    for tb in range(nb):
        sl0 = slice(tb * 512, (tb + 1) * 512)
        sl1 = slice(T + tb * 512, T + (tb + 1) * 512)
        ha = e_tt(sch, vscr[0:1, 0:512], srow[0:1, sl0], srow[0:1, sl0], ALU.mult, prev,
                  tag=f"{tag}mu2_{tb}")
        hb = e_tt(sch, srow[0:1, sl1], srow[0:1, sl1], vscr[0:1, 0:512], ALU.subtract, [ha],
                  tag=f"{tag}var_{tb}")
        prev = [hb]
        h_var_ops.append(hb)
    h_sd = e_act(sch, srow[0:1, T:2 * T], srow[0:1, T:2 * T], AF.Sqrt, h_var_ops,
                 bias=epsc[0:1, 0:1], tag=f"{tag}sd")
    h_rs = sch.op("vector", lambda v: v.reciprocal(srow[0:1, T:2 * T], srow[0:1, T:2 * T]),
                  [h_sd], tag=f"{tag}rs")
    outs = []
    prev_bc = None
    for tb in range(nb):
        sl = slice(tb * 512, (tb + 1) * 512)
        h_bm = e_matmul_acc(sch, psbc[:, 0:512], [(onesr[:], srow[0:1, sl])],
                            [h_rs] + ([prev_bc] if prev_bc else []), tag=f"{tag}bcmu{tb}")
        h_bmv = e_copy(sch, "vector", mub[:, 0:512], psbc[:, 0:512], [h_bm], tag=f"{tag}bcmuv{tb}")
        h_br = e_matmul_acc(sch, psbc[:, 0:512],
                            [(onesr[:], srow[0:1, T + tb * 512:T + (tb + 1) * 512])],
                            [h_bmv], tag=f"{tag}bcrs{tb}")
        h_brv = e_copy(sch, "vector", rsb[:, 0:512], psbc[:, 0:512], [h_br], tag=f"{tag}bcrsv{tb}")
        prev_bc = h_brv
        for c in range(NDC):
            outs.append(out_cb(c, tb, mub[:, 0:512], rsb[:, 0:512], [h_bmv, h_brv]))
        prev_bc = outs[-1]
    return outs



def build_combine(kind, ctx=None):
    nc, _es0, _own = _stage_begin(ctx)
    D = _mk_D(nc, ctx)
    x_d = D("x", [DM, TPC], F32, "ExternalInput")
    s_d = D("s", [DM, TPC], F32, "ExternalInput")
    if kind == "moe":
        vall_d = D("vall", [8, NCORES * DM], F32, "ExternalInput")
        ind_d = D("ind", [8, TPC], F32, "ExternalInput")
    else:
        g_d = D("g", [DM, TPC], F32, "ExternalInput")
    if kind == "final":
        ident_d = D("ident", [128, 128], F32, "ExternalInput")
        o_out = D("o_out", [TPC, DM], F32, "ExternalOutput")
    else:
        o_x = D("o_x", [DM, TPC], F32, "ExternalOutput")
        o_xn = D("o_xn", [DM, TPC], F32, "ExternalOutput")

    es = _es0
    sb = lambda n, s, d=F32: es.enter_context(nc.sbuf_tensor(_uname(n), s, d))
    psm_ = lambda n, s: es.enter_context(nc.psum_tensor(_uname(n), s, F32))
    with es:
        x1 = [sb(f"x1_{c}", [128, TPC]) for c in range(NDC)]
        xs = sb("xs", [128, TPC])
        ss = sb("ss", [128, TPC])
        gt = sb("gt", [128, TPC])
        sch = ctx.sch if ctx else Sched(nc, es)
        if kind == "moe":
            reduced = vall_d.shape[0] == 1
            v8 = sb("v8", [8, DM if reduced else NCORES * DM])
            indf = sb("indf", [8, TPC])
            pse = psm_("pse", [128, 512])
            if reduced:
                h_vsum = sch.dma_batch("sync", [(v8[:, :], vall_d.rearrange("one (e d) -> (one e) d", e=8)),
                                               (indf[:, :], ind_d[:, :])], tag="ldv")
            else:
                h_v = sch.dma_batch("sync", [(v8[:, :], vall_d[:, :]), (indf[:, :], ind_d[:, :])],
                                    tag="ldv")
                prev = h_v
                for k in range(1, NCORES):
                    prev = e_tt(sch, v8[:, 0:DM], v8[:, 0:DM], v8[:, k * DM:(k + 1) * DM],
                                ALU.add, [prev], tag=f"vsum{k}")
                h_vsum = prev
        else:
            pse = psm_("pse", [128, 512])

        upd_done = []
        h_g = None
        for c in range(NDC):
            h_x = sch.dma_batch("sync", [(xs[:, c * 256:(c + 1) * 256] if False else xs[:, :],
                                          x_d[c * 128:(c + 1) * 128, :])], deps=upd_done[-1:],
                                tag=f"ldx{c}")
            h_s = sch.dma_batch("sync", [(ss[:, :], s_d[c * 128:(c + 1) * 128, :])],
                                deps=upd_done[-1:], tag=f"lds{c}")
            if kind == "moe":
                for tb in range(4):
                    sl = slice(tb * 512, (tb + 1) * 512)
                    h_mm = e_matmul_acc(sch, pse[:, 0:512],
                                        [(v8[0:8, c * 128:(c + 1) * 128], indf[0:8, sl])],
                                        [h_vsum] + ([h_g] if h_g else []) + upd_done[-1:],
                                        tag=f"g{c}_{tb}")
                    h_g = e_act(sch, gt[:, sl], pse[:, 0:512], AF.Silu, [h_mm], tag=f"gs{c}_{tb}")
                h_gd = h_g
            else:
                h_gl = sch.dma_batch("sync", [(gt[:, :], g_d[c * 128:(c + 1) * 128, :])],
                                     deps=upd_done[-1:], tag=f"ldg{c}")
                h_gd = e_act(sch, gt[:, :], gt[:, :], AF.Silu, [h_gl], tag=f"gs{c}")
            h_m = e_tt(sch, gt[:, :], gt[:, :], ss[:, :], ALU.mult, [h_gd, h_s], tag=f"um{c}")
            h_a = e_tt(sch, x1[c][:, :], xs[:, :], gt[:, :], ALU.add, [h_m, h_x], tag=f"ua{c}")
            upd_done.append(h_a)
            if kind != "final":
                sch.dma_batch("sync", [(o_x[c * 128:(c + 1) * 128, :], x1[c][:, :])],
                              deps=[h_a], tag=f"stx{c}")

        if kind == "final":
            idn = sb("idn", [128, 128])
            oscr = [sb(f"oscr{i}", [128, 128]) for i in range(2)]
            pstr = [psm_(f"pstr{i}", [128, 512]) for i in range(2)]
            h_id = sch.dma_batch("sync", [(idn[:], ident_d[:, :])], tag="ldid")
            ev = {0: None, 1: None}
            for t in range(NTT):
                for c in range(NDC):
                    i = (t * NDC + c) % 2
                    deps = [h_id, upd_done[c]]
                    if ev[i] is not None:
                        deps.append(ev[i])
                    h_tr = e_transpose(sch, pstr[i][:, 0:128],
                                       x1[c][:, t * 128:(t + 1) * 128], idn[:], deps,
                                       tag=f"ftr{t}_{c}")
                    h_e = e_copy(sch, "vector", oscr[i][:, :], pstr[i][:, 0:128], [h_tr],
                                 tag=f"fte{t}_{c}")
                    h_d = sch.dma_batch("sync",
                                        [(o_out[t * 128:(t + 1) * 128, c * 128:(c + 1) * 128],
                                          oscr[i][:, :])], deps=[h_e], tag=f"ftd{t}_{c}")
                    ev[i] = h_d
        else:
            bufs = dict(
                sq=[sb(f"lsq{i}", [128, 512]) for i in range(2)],
                srow=sb("lsrow", [1, 2 * TPC]),
                mub=sb("lmub", [128, 512]), rsb=sb("lrsb", [128, 512]),
                onesc=sb("lonesc", [128, 1]), onesr=sb("lonesr", [1, 128]),
                epsc=sb("lepsc", [1, 1]),
                psx=psm_("lpsx", [1, 512]), psx2=psm_("lpsx2", [1, 512]),
                psbc=psm_("lpsbc", [128, 512]),
                vscr=sb("lvscr", [1, 512]),
            )
            h_c = [sch.op("vector", lambda v: v.memset(bufs["onesc"][:], 1.0)),
                   sch.op("vector", lambda v: v.memset(bufs["onesr"][:], 1.0)),
                   sch.op("vector", lambda v: v.memset(bufs["epsc"][:], 1e-5))]
            xscr = [sb(f"lxscr{i}", [128, 512]) for i in range(2)]
            prev_dma = {0: None, 1: None}

            def out_cb(c, tb, mub_ap, rsb_ap, deps):
                i = (c * 4 + tb) % 2
                d2 = list(deps)
                if prev_dma[i] is not None:
                    d2.append(prev_dma[i])
                h1 = e_tt(sch, xscr[i][:, 0:512], x1[c][:, tb * 512:(tb + 1) * 512], mub_ap,
                          ALU.subtract, d2, tag=f"lnap1_{c}_{tb}")
                h2 = e_tt(sch, xscr[i][:, 0:512], xscr[i][:, 0:512], rsb_ap, ALU.mult, [h1],
                          tag=f"lnap2_{c}_{tb}")
                h3 = sch.dma_batch("sync",
                                   [(o_xn[c * 128:(c + 1) * 128, tb * 512:(tb + 1) * 512],
                                     xscr[i][:, 0:512])], deps=[h2], tag=f"lnd{c}_{tb}")
                prev_dma[i] = h3
                return h3

            emit_ln_fm(sch, bufs, x1, out_cb, upd_done + h_c, tag="cl")
        if _own:
            sch.emit()
    return nc


def stage_combine_moe(inp, xfm_list, s_list, moe_results):
    vparts = [moe_results[c]["o_vpart"].reshape(8, DM) for c in range(NCORES)]
    vall = f32(np.concatenate(vparts, axis=1))
    maps = []
    for c in range(NCORES):
        maps.append(dict(x=f32(xfm_list[c]), s=f32(s_list[c]), vall=vall,
                         ind=f32(moe_results[c]["o_ind"])))
    return maps


def stage_combine_plain(xfm_list, s_list, g_list):
    return [dict(x=f32(xfm_list[c]), s=f32(s_list[c]), g=f32(g_list[c]))
            for c in range(NCORES)]


def stage_combine_final(xfm_list, s_list, g_list):
    return [dict(x=f32(xfm_list[c]), s=f32(s_list[c]), g=f32(g_list[c]), ident=IDENT128)
            for c in range(NCORES)]


L1H = 128

def build_l1_attn(stage=4, ctx=None):
    nc, _es0, _own = _stage_begin(ctx)
    D = _mk_D(nc, ctx)
    TE = TPC + L1H
    xn_d = D("xn", [DM, TE], F32, "ExternalInput")
    gcol_d = D("gcol", [128, NDC], F32, "ExternalInput")
    wqkv_d = D("wqkv", [DM, 3 * DM], F32, "ExternalInput")
    wo_d = D("wo", [DM, DM], F32, "ExternalInput")
    maskp_d = D("maskp", [128, 128], F32, "ExternalInput")
    maskc_d = D("maskc", [128, 128], F32, "ExternalInput")
    fw_d = D("fw", [128, 1], F32, "ExternalInput")
    o_s = D("o_s", [DM, TPC], F32, "ExternalOutput")

    NB = TE // 512
    es = _es0
    sb = lambda n, s, d=F32: es.enter_context(nc.sbuf_tensor(_uname(n), s, d))
    psm_ = lambda n, s: es.enter_context(nc.psum_tensor(_uname(n), s, F32))
    with es:
        xne = [sb(f"xne{c}", [128, TE], BF16) for c in range(NDC)]
        xnr = [sb(f"xnr{c}", [128, TE], BF16) for c in range(NDC)]
        qB = [sb(f"qB{c}", [128, TPC], BF16) for c in range(4)]
        kB = [sb(f"kB{c}", [128, TE], BF16) for c in range(4)]
        vaug = sb("vaug", [128, 17 * 8 * 65], BF16)
        ofm = [sb(f"ofm{c}", [128, TPC], BF16) for c in range(NDC)]
        wsl = sb("wsl", [128, NDC * 512], BF16)
        wo_b = sb("wo_b", [128, NDC * DM], BF16)
        sq = [sb(f"sq{i}", [128, 512], BF16) for i in range(2)]
        srow = sb("srow", [1, TE])
        rstb = sb("rstb", [128, 512])
        gcol = sb("gcol_s", [128, NDC])
        maskp = sb("maskp_s", [128, 128])
        maskc = sb("maskc_s", [128, 128])
        fwc = sb("fw_s", [128, 1])
        onesc_b = sb("onescb", [128, 1], BF16)
        onesr = sb("onesr", [1, 128])
        ptp = sb("ptp", [128, 512], BF16)
        ptc = sb("ptc", [128, 512], BF16)
        rrow = sb("rrow", [1, 512])
        rbb = sb("rbb", [64, 512])
        xres = [sb(f"xres{i}", [128, 512]) for i in range(2)]
        sout = [sb(f"sout{i}", [128, 512]) for i in range(2)]
        eps6 = sb("eps6", [1, 1])
        pst = [psm_(f"pst{i}", [128, 512]) for i in range(2)]
        stp2 = [psm_(f"stp{i}", [128, 512]) for i in range(2)]
        stc2 = [psm_(f"stc{i}", [128, 512]) for i in range(2)]
        oth = psm_("oth", [128, 512])
        ps8 = psm_("ps8", [128, 512])
        psx = ps8[0:1, 0:512]
        psn = ps8[0:64, 0:512]
        psbc = oth[:, 0:512]

        sch = ctx.sch if ctx else Sched(nc, es)
        h_xn = sch.dma_batch("gpsimd", [(xne[c][:, :], xn_d[c * 128:(c + 1) * 128, :]) for c in range(NDC)], tag="ldxn")
        h_sm = sch.dma_batch("sync", [(gcol[:, :], gcol_d[:, :]), (maskp[:], maskp_d[:, :]),
                                      (maskc[:], maskc_d[:, :]), (fwc[:], fw_d[:, :])], tag="ldsm")
        h_wo = sch.dma_batch("gpsimd", [(wo_b[:, c * DM:(c + 1) * DM], wo_d[c * 128:(c + 1) * 128, :]) for c in range(NDC)], tag="ldwo")
        h_ones = [sch.op("vector", lambda v: v.memset(onesc_b[:], 1.0)),
                  sch.op("vector", lambda v: v.memset(onesr[:], 1.0)),
                  sch.op("vector", lambda v: v.memset(eps6[:], 1e-6)),
                  sch.op("vector", lambda v: v.memset(vaug[:], 1.0))]

        blocks = [(i * 512, 512) for i in range(4)] + [(2048, 128)]
        h_rms_done = []
        h_prev_mmx = None
        last_sq_user = {0: None, 1: None}
        rs_done = {}
        for bi, (off, w) in enumerate(blocks):
            h_mmx2 = None
            for c in range(NDC):
                i = c % 2
                sq_deps = [h_xn] + ([last_sq_user[i]] if last_sq_user[i] else [])
                h_sq = e_act(sch, sq[i][:, 0:w], xne[c][:, off:off + w], AF.Square, sq_deps,
                             tag=f"rsq{bi}_{c}")
                h_mmx2 = sch.op(
                    "tensor",
                    (lambda c, i, w: lambda t: t.matmul(psx[0:1, 0:w], onesc_b[:], sq[i][:, 0:w],
                                                        start=(c == 0), stop=(c == NDC - 1),
                                                        skip_group_check=True))(c, i, w),
                    [h_sq] + h_ones + ([rs_done[bi - 1]] if bi > 0 else []), tag=f"rmm{bi}_{c}")
                last_sq_user[i] = h_mmx2
            rs_done[bi] = e_copy(sch, "vector", srow[0:1, off:off + w], psx[0:1, 0:w], [h_mmx2],
                                 tag=f"rev{bi}")
        h_sd = e_act(sch, srow[0:1, 0:TE], srow[0:1, 0:TE], AF.Sqrt, list(rs_done.values()) + h_ones,
                     bias=eps6[0:1, 0:1], scale=1.0 / DM, tag="rsd")
        h_rr = sch.op("vector", lambda v: v.reciprocal(srow[0:1, 0:TE], srow[0:1, 0:TE]), [h_sd], tag="rrec")
        xnr_done = []
        prev_bc = None
        for bi, (off, w) in enumerate(blocks):
            h_bm = e_matmul_acc(sch, psbc[:, 0:w], [(onesr[:], srow[0:1, off:off + w])],
                                [h_rr] + ([prev_bc] if prev_bc else []), tag=f"rbc{bi}")
            h_bv = e_copy(sch, "vector", rstb[:, 0:w], psbc[:, 0:w], [h_bm], tag=f"rbv{bi}")
            prev_bc = h_bv
            for c in range(NDC):
                h_ap = e_stt(sch, xnr[c][:, off:off + w], xne[c][:, off:off + w], gcol[:, c:c + 1],
                             rstb[:, 0:w], ALU.mult, ALU.mult, [h_bv, h_sm], tag=f"rap{bi}_{c}")
                xnr_done.append(h_ap)
            prev_bc = xnr_done[-1]

        if stage == 1:
            pd = None
            for c in range(NDC):
                h = sch.dma_batch("gpsimd", [(o_s[c * 128:(c + 1) * 128, :], xnr[c][:, L1H:])],
                                  deps=xnr_done + ([pd] if pd else []), tag=f"dbg{c}")
                pd = h
            sch.emit()
            return nc
        NW = TPC // 128
        h_res_dma = []
        prev_sl_user = None
        for half in range(2):
            h_wq = sch.dma_batch("gpsimd",
                                 [(wsl[:, c * 512:(c + 1) * 512],
                                   wqkv_d[c * 128:(c + 1) * 128, 512 * half:512 * half + 512])
                                  for c in range(NDC)],
                                 deps=([prev_sl_user] if prev_sl_user else []), tag=f"h{half}ldq")
            q_done = []
            prev_bank = {0: None, 1: None}
            for oi in range(4):
                for tb in range(4):
                    bank = (oi * 4 + tb) % 2
                    deps = [h_wq] + xnr_done
                    if prev_bank[bank] is not None:
                        deps.append(prev_bank[bank])
                    pairs = [(wsl[:, c * 512 + oi * 128:c * 512 + (oi + 1) * 128],
                              xnr[c][:, L1H + tb * 512:L1H + (tb + 1) * 512]) for c in range(NDC)]
                    h_mm = e_matmul_acc(sch, pst[bank][:, 0:512], pairs, deps, tag=f"h{half}q{oi}_{tb}")
                    h_ev = e_copy(sch, "vector", qB[oi][:, tb * 512:(tb + 1) * 512], pst[bank][:, 0:512],
                                  [h_mm], tag=f"h{half}qe{oi}_{tb}")
                    prev_bank[bank] = h_ev
                    q_done.append(h_ev)
            h_wk = sch.dma_batch("gpsimd",
                                 [(wsl[:, c * 512:(c + 1) * 512],
                                   wqkv_d[c * 128:(c + 1) * 128, DM + 512 * half:DM + 512 * half + 512])
                                  for c in range(NDC)],
                                 deps=q_done, tag=f"h{half}ldk")
            k_done = []
            prev_bank = {0: None, 1: None}
            n = 0
            for oi in range(4):
                for bi, (off, w) in enumerate(blocks):
                    bank = n % 2
                    n += 1
                    deps = [h_wk] + xnr_done
                    if prev_bank[bank] is not None:
                        deps.append(prev_bank[bank])
                    pairs = [(wsl[:, c * 512 + oi * 128:c * 512 + (oi + 1) * 128],
                              xnr[c][:, off:off + w]) for c in range(NDC)]
                    h_mm = e_matmul_acc(sch, pst[bank][:, 0:w], pairs, deps, tag=f"h{half}k{oi}_{bi}")
                    h_ev = e_copy(sch, "vector", kB[oi][:, off:off + w], pst[bank][:, 0:w],
                                  [h_mm], tag=f"h{half}ke{oi}_{bi}")
                    prev_bank[bank] = h_ev
                    k_done.append(h_ev)
            h_wv = sch.dma_batch("gpsimd",
                                 [(wsl[:, c * 512:(c + 1) * 512],
                                   wqkv_d[c * 128:(c + 1) * 128, 2 * DM + 512 * half:2 * DM + 512 * half + 512])
                                  for c in range(NDC)],
                                 deps=k_done, tag=f"h{half}ldv")
            v_done = []
            prev_bank = {0: None, 1: None}
            vaug4 = vaug.rearrange("p (n h w) -> p n h w", n=17, h=8, w=65)
            for ti in range(17):
                bank = ti % 2
                deps = [h_wv] + xnr_done + h_ones
                if prev_bank[bank] is not None:
                    deps.append(prev_bank[bank])
                pairs = [(xnr[c][:, ti * 128:(ti + 1) * 128], wsl[:, c * 512:(c + 1) * 512])
                         for c in range(NDC)]
                h_mm = e_matmul_acc(sch, pst[bank][:, 0:512], pairs, deps, tag=f"h{half}v{ti}")
                h_ev = e_copy(sch, "vector", vaug4[:, ti, :, 0:64].bitcast(BF16) if False else vaug4[:, ti, :, 0:64],
                              pst[bank][:, 0:512].rearrange("p (h w) -> p h w", w=64), [h_mm],
                              tag=f"h{half}ve{ti}")
                prev_bank[bank] = h_ev
                v_done.append(h_ev)
            prev_sl_user = v_done[-1]

            if stage == 2:
                prev_sl_user = v_done[-1]
                continue
            exp_prev = {0: None, 1: None}
            on_prev = None
            for w_i in range(NW):
                for hg in range(2):
                    h_stp = h_stc = None
                    for j in range(4):
                        hj = hg * 4 + j
                        qc = qB[hj // 2]
                        kc = kB[hj // 2]
                        base = 64 * (hj % 2)
                        bk = hj % 2
                        slc = (j // 2) * 128
                        deps = q_done + k_done
                        if j == 0 and exp_prev[0] is not None:
                            deps = deps + [exp_prev[0], exp_prev[1]]
                        h_stp = sch.op("tensor",
                                       (lambda kc, qc, base, bk, slc, w_i:
                                        lambda t: t.matmul(stp2[bk][:, slc:slc + 128],
                                                           kc[base:base + 64, w_i * 128:(w_i + 1) * 128],
                                                           qc[base:base + 64, w_i * 128:(w_i + 1) * 128],
                                                           start=True, stop=True, skip_group_check=True))(
                                           kc, qc, base, bk, slc, w_i),
                                       deps, tag=f"h{half}stp{w_i}_{hg}_{j}")
                        h_stc = sch.op("tensor",
                                       (lambda kc, qc, base, bk, slc, w_i:
                                        lambda t: t.matmul(stc2[bk][:, slc:slc + 128],
                                                           kc[base:base + 64, L1H + w_i * 128:L1H + (w_i + 1) * 128],
                                                           qc[base:base + 64, w_i * 128:(w_i + 1) * 128],
                                                           start=True, stop=True, skip_group_check=True))(
                                           kc, qc, base, bk, slc, w_i),
                                       deps, tag=f"h{half}stc{w_i}_{hg}_{j}")
                    ed = [h_stp, h_stc]
                    if exp_prev[0] is not None:
                        ed.append(exp_prev[0])
                    e_act(sch, ptp[:, 0:256], stp2[0][:, 0:256], AF.Exp, ed, scale=0.125,
                          tag=f"h{half}ep0{w_i}_{hg}")
                    h_ep = e_act(sch, ptp[:, 256:512], stp2[1][:, 0:256], AF.Exp, [], scale=0.125,
                                 tag=f"h{half}ep1{w_i}_{hg}")
                    ed2 = [h_stp, h_stc]
                    if exp_prev[1] is not None:
                        ed2.append(exp_prev[1])
                    e_act(sch, ptc[:, 0:256], stc2[0][:, 0:256], AF.Exp, ed2, scale=0.125,
                          tag=f"h{half}ec0{w_i}_{hg}")
                    h_ec = e_act(sch, ptc[:, 256:512], stc2[1][:, 0:256], AF.Exp, [], scale=0.125,
                                 tag=f"h{half}ec1{w_i}_{hg}")
                    if stage == 30:
                        exp_prev = {0: h_ep, 1: h_ec}
                        on_prev = h_ec
                        continue
                    mp = maskp[:, :].rearrange("p (one w) -> p one w", one=1).broadcast_to([128, 4, 128])
                    mc = maskc[:, :].rearrange("p (one w) -> p one w", one=1).broadcast_to([128, 4, 128])
                    h_mp = e_tt(sch, ptp[:, 0:512].rearrange("p (n w) -> p n w", w=128), 
                                ptp[:, 0:512].rearrange("p (n w) -> p n w", w=128), mp, ALU.mult,
                                [h_ep, h_sm], tag=f"h{half}mp{w_i}_{hg}")
                    if w_i == 0:
                        h_mp = e_ts(sch, ptp[:, 0:512], ptp[:, 0:512], fwc[:, 0:1], ALU.mult,
                                    [h_mp], tag=f"h{half}fw{hg}")
                    h_mc = e_tt(sch, ptc[:, 0:512].rearrange("p (n w) -> p n w", w=128),
                                ptc[:, 0:512].rearrange("p (n w) -> p n w", w=128), mc, ALU.mult,
                                [h_ec, h_sm], tag=f"h{half}mc{w_i}_{hg}")
                    if stage == 31:
                        exp_prev = {0: h_mp, 1: h_mc}
                        on_prev = h_mc
                        continue
                    h_ot = None
                    for j in range(4):
                        hj = hg * 4 + j
                        habs = 8 * half + hj
                        deps_ot = [h_mp, h_mc] + v_done
                        if on_prev is not None and j == 0:
                            deps_ot.append(on_prev)
                        ptcol = (j % 2) * 256 + (j // 2) * 128
                        h_ot1 = sch.op("tensor",
                                       (lambda w_i, hj, j, ptcol:
                                        lambda t: t.matmul(oth[0:65, j * 128:(j + 1) * 128],
                                                           vaug4[:, w_i, hj, :],
                                                           ptp[:, ptcol:ptcol + 128],
                                                           start=True, stop=False, skip_group_check=True))(w_i, hj, j, ptcol),
                                       deps_ot, tag=f"h{half}ot1_{w_i}_{hg}_{j}")
                        h_ot = sch.op("tensor",
                                      (lambda w_i, hj, j, ptcol:
                                       lambda t: t.matmul(oth[0:65, j * 128:(j + 1) * 128],
                                                          vaug4[:, w_i + 1, hj, :],
                                                          ptc[:, ptcol:ptcol + 128],
                                                          start=False, stop=True, skip_group_check=True))(w_i, hj, j, ptcol),
                                      [h_ot1], tag=f"h{half}ot2_{w_i}_{hg}_{j}")
                    if stage == 32:
                        h_cp = e_copy(sch, "vector", rbb[:, :], oth[0:64, 0:512], [h_ot], tag=f"dbgcp{w_i}_{hg}")
                        exp_prev = {0: h_cp, 1: h_cp}
                        on_prev = h_cp
                        continue
                    h_rr2 = sch.op("vector", lambda v: v.reciprocal(rrow[0:1, 0:512], oth[64:65, 0:512]),
                                   [h_ot], tag=f"h{half}rr{w_i}_{hg}")
                    h_rb = e_matmul_acc(sch, psn[0:64, 0:512], [(onesr[0:1, 0:64], rrow[0:1, 0:512])],
                                        [h_rr2], tag=f"h{half}rb{w_i}_{hg}")
                    h_rbe = e_copy(sch, "vector", rbb[:, :], psn[0:64, 0:512], [h_rb], tag=f"h{half}rbe{w_i}_{hg}")
                    ons = []
                    for j in range(4):
                        hj = hg * 4 + j
                        habs = 8 * half + hj
                        h_on = e_stt(sch, ofm[habs // 2][64 * (habs % 2):64 * (habs % 2) + 64, w_i * 128:(w_i + 1) * 128],
                                     oth[0:64, j * 128:(j + 1) * 128], 1.0, rbb[:, j * 128:(j + 1) * 128],
                                     ALU.mult, ALU.mult, [h_rbe], tag=f"h{half}on{w_i}_{hg}_{j}")
                        ons.append(h_on)
                    on_prev = ons[-1]
                    exp_prev = {0: h_mp if w_i > 0 else h_mp, 1: h_mc}
                    exp_prev = {0: ons[-1], 1: ons[-1]}

        if stage in (2, 30, 31, 32):
            pd = None
            for c in range(4):
                h = sch.dma_batch("gpsimd", [(o_s[c * 128:(c + 1) * 128, :], qB[c][:, :])],
                                  deps=q_done + ([pd] if pd else []), tag=f"dbg{c}")
                pd = h
            sch.emit()
            return nc
        attn_done = [on_prev]
        if stage == 3:
            pd = None
            for c in range(NDC):
                h = sch.dma_batch("gpsimd", [(o_s[c * 128:(c + 1) * 128, :], ofm[c][:, :])],
                                  deps=[on_prev] + ([pd] if pd else []), tag=f"dbg{c}")
                pd = h
            sch.emit()
            return nc

        prev_bank = {0: None}
        prev_dma = {0: None, 1: None}
        prev_res = {0: None, 1: None}
        for oi in range(NDC):
            for tb in range(4):
                n = oi * 4 + tb
                qb_ = n % 2
                h_res = sch.dma_batch("sync", [(xres[qb_][:, 0:512],
                                                xn_d[oi * 128:(oi + 1) * 128, L1H + tb * 512:L1H + (tb + 1) * 512])],
                                      deps=([prev_res[qb_]] if prev_res[qb_] else []), tag=f"res{oi}_{tb}")
                deps = attn_done + [h_wo]
                if prev_bank[0] is not None:
                    deps.append(prev_bank[0])
                pairs = [(wo_b[:, c * DM + oi * 128:c * DM + (oi + 1) * 128],
                          ofm[c][:, tb * 512:(tb + 1) * 512]) for c in range(NDC)]
                h_mm = e_matmul_acc(sch, pst[0][:, 0:512], pairs, deps, tag=f"wo{oi}_{tb}")
                ev_deps = [h_mm, h_res]
                if prev_dma[qb_] is not None:
                    ev_deps.append(prev_dma[qb_])
                h_ev = e_stt(sch, sout[qb_][:, 0:512], pst[0][:, 0:512], 1.0, xres[qb_][:, 0:512],
                             ALU.mult, ALU.add, ev_deps, tag=f"woev{oi}_{tb}")
                prev_bank[0] = h_ev
                prev_res[qb_] = h_ev
                h_dma = sch.dma("sync", o_s[oi * 128:(oi + 1) * 128, tb * 512:(tb + 1) * 512],
                                sout[qb_][:, 0:512], deps=[h_ev], tag=f"wodma{oi}_{tb}")
                prev_dma[qb_] = h_dma
        if _own:
            sch.emit()
    return nc


def stage_l1_attn(inp, xn_list, xn_is_fm=True):
    wqkv = f32(inp["L1_attn_Wqkv"])
    wo = f32(inp["L1_attn_Wo"])
    gcol = f32(inp["L1_attn_g"].reshape(NDC, 128).T)
    i = np.arange(128)[:, None]; j = np.arange(128)[None, :]
    maskp = f32((i > j))
    maskc = f32((i <= j))
    maps = []
    for c in range(NCORES):
        first = (c % GPB) == 0
        halo = np.zeros((DM, L1H), np.float32) if first else f32(xn_list[c - 1][:, -L1H:])
        xne = np.concatenate([halo, f32(xn_list[c])], axis=1)
        fw = np.zeros((128, 1), np.float32) if first else np.ones((128, 1), np.float32)
        maps.append(dict(xn=f32(xne), gcol=gcol, wqkv=wqkv, wo=wo,
                         maskp=maskp, maskc=maskc, fw=fw))
    return maps


FF = 2816
NFC = FF // 128


_MAGIC22 = float(3 * 2 ** 21)
_MAGIC23 = float(3 * 2 ** 22)


def _emit_staircase(sch, u_ap, a_ap, g_ap, acc_ap, sb_ap, out_ap, sexp_ap, deps, tag, eng="vector",
                    k22=None, k23=None):
    h_a = e_act(sch, a_ap, u_ap, AF.Abs, deps, tag=f"{tag}a")
    h1 = e_ts(sch, g_ap, a_ap, 2.0, ALU.is_ge, [h_a], tag=f"{tag}m2", eng=eng,
              op1=ALU.mult, scalar2=_MAGIC22)
    h2 = e_ts(sch, sb_ap, a_ap, 4.0, ALU.is_ge, [h_a, h1], tag=f"{tag}m4", eng=eng,
              op1=ALU.mult, scalar2=_MAGIC23)
    h3 = e_stt(sch, sb_ap, g_ap, _MAGIC22, sb_ap, ALU.add, ALU.add, [h2],
               tag=f"{tag}cf", eng=eng)
    h4 = e_tt(sch, a_ap, u_ap, sb_ap, ALU.add, [h3], tag=f"{tag}t", eng=eng)
    h5 = e_tt(sch, sb_ap, a_ap, sb_ap, ALU.subtract, [h4], tag=f"{tag}r", eng=eng)
    h6 = e_tt(sch, out_ap, sb_ap, sexp_ap, ALU.mult, [h5], tag=f"{tag}dq", eng=eng)
    return h6


def build_fp4_ffn(ctx=None):
    nc, _es0, _own = _stage_begin(ctx)
    D = _mk_D(nc, ctx)
    xn_d = D("xn", [DM, TPC], F32, "ExternalInput")
    wg_n = D("wg_n", [FF, DM], F32, "ExternalInput")
    wu_n = D("wu_n", [FF, DM], F32, "ExternalInput")
    wd_n = D("wd_n", [DM, FF], F32, "ExternalInput")
    wg_t = D("wg_t", [DM, FF], F32, "ExternalInput")
    wu_t = D("wu_t", [DM, FF], F32, "ExternalInput")
    wd_t = D("wd_t", [FF, DM], F32, "ExternalInput")
    ident_d = D("ident", [128, 128], F32, "ExternalInput")
    e16_d = D("e16", [128, 16 * 128], F32, "ExternalInput")
    o_g = D("o_g", [DM, TPC], F32, "ExternalOutput")
    wq_g = D("wq_g", [DM, FF], BF16, "Internal")
    wq_u = D("wq_u", [DM, FF], BF16, "Internal")
    wq_d = D("wq_d", [FF, DM], BF16, "Internal")

    es = _es0
    sbf = lambda n, s, d=F32: es.enter_context(nc.sbuf_tensor(_uname(n), s, d))
    psm_ = lambda n, s: es.enter_context(nc.psum_tensor(_uname(n), s, F32))
    with es:
        slot = [sbf(f"slot{i}", [128, FF]) for i in range(5)]
        sexp = sbf("sexp", [128, FF])
        rsexp = sbf("rsexp", [128, FF])
        sT = sbf("sT", [128, FF])
        rsT = sbf("rsT", [128, FF])
        mrows = sbf("mrows", [128, 192])
        msl = sbf("msl", [128, 64])
        e16 = sbf("e16_s", [128, 16 * 128])
        idn = sbf("idn", [128, 128])
        wqo = [sbf(f"wqo{i}", [128, FF], BF16) for i in range(2)]
        xq = [sbf(f"xq{c}", [128, TPC], BF16) for c in range(NDC)]
        mtm = sbf("mtm", [128, NTT * 64])
        hqb = sbf("hqb", [128, NFC * 512], BF16)
        wsg = sbf("wsg", [128, NDC * 512], BF16)
        wsu = sbf("wsu", [128, NDC * 512], BF16)
        wsd = [sbf(f"wsd{i}", [128, DM], BF16) for i in range(2)]
        gout = [sbf(f"gout{i}", [128, 512]) for i in range(2)]
        ps = [psm_(f"ps{i}", [128, 512]) for i in range(8)]

        sch = ctx.sch if ctx else Sched(nc, es)
        h_e16 = sch.dma_batch("sync", [(e16[:, :], e16_d[:, :]), (idn[:], ident_d[:, :])], tag="lde")
        h_init = [sch.op("vector", lambda v: v.memset(sT[:], 1.0)),
                  sch.op("vector", lambda v: v.memset(rsT[:], 1.0))]

        def quant_weight(nat_d, t_d, out_d, R, C, tagw, deps0):
            RC, CC, SC = R // 128, C // 128, C // 16
            nsub = ceil_div(SC, 128)
            h_fill = []
            prev_ld = {0: None, 1: None}
            prev_tr = {0: None, 1: None}
            for r in range(RC):
                i = r % 2
                d0 = list(deps0)
                if prev_ld[i] is not None:
                    d0.append(prev_ld[i])
                h_ld = sch.dma_batch("sync", [(slot[i][:, 0:C], nat_d[r * 128:(r + 1) * 128, :])],
                                     deps=d0, tag=f"{tagw}nl{r}")
                h_m = e_red(sch, mrows[:, 0:SC], slot[i][:, 0:C].rearrange("p (s n) -> p s n", n=16),
                            ALU.max, [h_ld] + (prev_tr[0] and [prev_tr[0]] or []) + (prev_tr[1] and [prev_tr[1]] or []),
                            tag=f"{tagw}mx{r}", absval=True)
                prev_ld[i] = h_m
                h_s = e_ts(sch, mrows[:, 0:SC], mrows[:, 0:SC], 1.0 / 6.0, ALU.mult, [h_m], tag=f"{tagw}s6{r}")
                h_s2 = e_ts(sch, mrows[:, 0:SC], mrows[:, 0:SC], 1e-12, ALU.max, [h_s], tag=f"{tagw}sm{r}")
                for sj in range(nsub):
                    w = min(128, SC - sj * 128)
                    ii = (r * nsub + sj) % 2
                    dtr = [h_s2, h_e16]
                    if prev_tr[ii] is not None:
                        dtr.append(prev_tr[ii])
                    h_t = e_transpose(sch, ps[ii][0:w, 0:128], mrows[:, sj * 128:sj * 128 + w],
                                      idn[:], dtr, tag=f"{tagw}tr{r}_{sj}")
                    h_c = e_copy(sch, "vector", sT[0:w, sj * R + r * 128:sj * R + (r + 1) * 128],
                                 ps[ii][0:w, 0:128], [h_t] + h_init, tag=f"{tagw}trc{r}_{sj}")
                    prev_tr[ii] = h_c
                    h_fill.append(h_c)
            h_rs = sch.op("vector", lambda v: v.reciprocal(rsT[:, 0:nsub * R], sT[:, 0:nsub * R]),
                          h_fill, tag=f"{tagw}rs")
            outs = []
            prev_q = None
            prev_dma = {0: None, 1: None}
            prev_ev = None
            h_ld_prev = {0: None, 1: None}
            for kc in range(CC):
                sblk = kc % 16
                reg = kc // 16
                exp_done = []
                for which, (dst, srcT) in enumerate(((sexp, sT), (rsexp, rsT))):
                    for nsb in range(ceil_div(R, 512)):
                        wns = min(512, R - nsb * 512)
                        bank = 6 + (len(exp_done) % 2)
                        dpe = [h_rs, h_e16] + ([prev_ev] if prev_ev else []) + ([prev_q] if prev_q else [])
                        h_mm = e_matmul_acc(sch, ps[bank][:, 0:wns],
                                            [(e16[:, sblk * 128:(sblk + 1) * 128],
                                              srcT[:, reg * R + nsb * 512:reg * R + nsb * 512 + wns])],
                                            dpe, tag=f"{tagw}ex{kc}_{which}_{nsb}")
                        h_ev = e_copy(sch, "vector", dst[:, nsb * 512:nsb * 512 + wns],
                                      ps[bank][:, 0:wns], [h_mm], tag=f"{tagw}exe{kc}_{which}_{nsb}")
                        exp_done.append(h_ev)
                        prev_ev = h_ev
                i = kc % 2
                d0 = [h_rs] + ([h_ld_prev[i]] if h_ld_prev[i] else []) + ([prev_q] if prev_q else [])
                h_ld = sch.dma_batch("sync", [(slot[i][:, 0:R], t_d[kc * 128:(kc + 1) * 128, :])],
                                     deps=d0, tag=f"{tagw}tl{kc}")
                h_ld_prev[i] = h_ld
                dq = [h_ld] + exp_done + ([prev_q] if prev_q else [])
                j = kc % 2
                if prev_dma[j] is not None:
                    dq.append(prev_dma[j])
                qeng = "vector"
                h_u = e_tt(sch, slot[2][:, 0:R], slot[i][:, 0:R], rsexp[:, 0:R], ALU.mult, dq,
                           tag=f"{tagw}u{kc}", eng=qeng)
                h_q = _emit_staircase(sch, slot[2][:, 0:R], slot[i][:, 0:R], slot[3][:, 0:R],
                                      slot[2][:, 0:R], slot[4][:, 0:R], wqo[j][:, 0:R],
                                      sexp[:, 0:R], [h_u], tag=f"{tagw}q{kc}_", eng=qeng)
                prev_q = h_q
                h_dma = sch.dma_batch("sync", [(out_d[kc * 128:(kc + 1) * 128, :], wqo[j][:, 0:R])],
                                      deps=[h_q], tag=f"{tagw}qd{kc}")
                prev_dma[j] = h_dma
                outs.append(h_dma)
            return outs

        h_wg = quant_weight(wg_n, wg_t, wq_g, FF, DM, "G", [h_e16] + h_init)
        h_wu = quant_weight(wu_n, wu_t, wq_u, FF, DM, "U", h_wg[-1:])
        h_wd = quant_weight(wd_n, wd_t, wq_d, DM, FF, "D", h_wu[-1:])

        h_m_all = []
        prev_ld = {0: None, 1: None}
        prev_tp = {0: None, 1: None}
        for c in range(NDC):
            i = c % 2
            d0 = h_wd[-1:] + ([prev_ld[i]] if prev_ld[i] else [])
            h_ld = sch.dma_batch("sync", [(slot[i][:, 0:TPC], xn_d[c * 128:(c + 1) * 128, :])],
                                 deps=d0, tag=f"Xl{c}")
            h_ab = e_act(sch, slot[2 + i][:, 0:TPC], slot[i][:, 0:TPC], AF.Abs, [h_ld] +
                         ([prev_tp[i]] if prev_tp[i] else []), tag=f"Xa{c}")
            prev_ld[i] = h_ab
            for t in range(NTT):
                ii = t % 2
                dtp = [h_ab]
                if prev_tp[ii] is not None:
                    dtp.append(prev_tp[ii])
                h_t = e_transpose(sch, ps[ii][:, 0:128], slot[2 + i][:, t * 128:(t + 1) * 128],
                                  idn[:], dtp, tag=f"Xt{c}_{t}")
                h_r = e_red(sch, mtm[:, t * 64 + c * 8:t * 64 + c * 8 + 8],
                            ps[ii][:, 0:128].rearrange("p (s n) -> p s n", n=16), ALU.max,
                            [h_t], tag=f"Xr{c}_{t}")
                prev_tp[ii] = h_r
                h_m_all.append(h_r)
        h_s = e_ts(sch, mtm[:, :], mtm[:, :], 1.0 / 6.0, ALU.mult, h_m_all, tag="Xs6")
        h_s2 = e_ts(sch, mtm[:, :], mtm[:, :], 1e-12, ALU.max, [h_s], tag="Xsm")
        h_fill = []
        prev_tr = {0: None, 1: None}
        for t in range(NTT):
            ii = t % 2
            dtr = [h_s2]
            if prev_tr[ii] is not None:
                dtr.append(prev_tr[ii])
            h_t = e_transpose(sch, ps[ii][0:64, 0:128], mtm[:, t * 64:(t + 1) * 64], idn[:], dtr,
                              tag=f"Xst{t}")
            h_c = e_copy(sch, "vector", sT[0:64, t * 128:(t + 1) * 128], ps[ii][0:64, 0:128],
                         [h_t], tag=f"Xstc{t}")
            prev_tr[ii] = h_c
            h_fill.append(h_c)
        h_rsx = sch.op("vector", lambda v: v.reciprocal(rsT[:, 0:TPC], sT[:, 0:TPC]), h_fill, tag="Xrs")
        xq_done = []
        prev_ev = None
        prev_q = None
        h_ld_prev = {0: None, 1: None}
        prev_dma = {0: None, 1: None}
        for c in range(NDC):
            exp_done = []
            for which, (dst, srcT) in enumerate(((sexp, sT), (rsexp, rsT))):
                for nsb in range(4):
                    bank = 6 + (len(exp_done) % 2)
                    dpe = [h_rsx, h_e16] + ([prev_ev] if prev_ev else []) + ([prev_q] if prev_q else [])
                    h_mm = e_matmul_acc(sch, ps[bank][:, 0:512],
                                        [(e16[:, c * 128:(c + 1) * 128],
                                          srcT[:, nsb * 512:(nsb + 1) * 512])],
                                        dpe, tag=f"Xex{c}_{which}_{nsb}")
                    h_ev = e_copy(sch, "vector", dst[:, nsb * 512:(nsb + 1) * 512],
                                  ps[bank][:, 0:512], [h_mm], tag=f"Xexe{c}_{which}_{nsb}")
                    exp_done.append(h_ev)
                    prev_ev = h_ev
            i = c % 2
            d0 = [h_rsx] + ([h_ld_prev[i]] if h_ld_prev[i] else []) + ([prev_q] if prev_q else [])
            h_ld = sch.dma_batch("sync", [(slot[i][:, 0:TPC], xn_d[c * 128:(c + 1) * 128, :])],
                                 deps=d0, tag=f"X2l{c}")
            h_ld_prev[i] = h_ld
            qeng = "vector"
            h_u = e_tt(sch, slot[2][:, 0:TPC], slot[i][:, 0:TPC], rsexp[:, 0:TPC], ALU.mult,
                       [h_ld] + exp_done + ([prev_q] if prev_q else []), tag=f"Xu{c}", eng=qeng)
            h_q = _emit_staircase(sch, slot[2][:, 0:TPC], slot[i][:, 0:TPC], slot[3][:, 0:TPC],
                                  slot[2][:, 0:TPC], slot[4][:, 0:TPC], xq[c][:, :],
                                  sexp[:, 0:TPC], [h_u], tag=f"Xq{c}_", eng=qeng)
            prev_q = h_q
            xq_done.append(h_q)

        NSL = [(i * 512, 512) for i in range(5)] + [(2560, 256)]
        prev_down_ev = None
        prev_wload = None
        for tb in range(4):
            hq_done = []
            prev_g = {0: None, 1: None}
            prev_tpp = {0: None, 1: None}
            for nsi, (noff, wns) in enumerate(NSL):
                d0 = xq_done + h_wg + h_wu + ([prev_wload] if prev_wload else []) + hq_done[-1:]
                h_lwg = sch.dma_batch("sync",
                                      [(wsg[:, k * 512:k * 512 + wns], wq_g[k * 128:(k + 1) * 128, noff:noff + wns])
                                       for k in range(NDC)], deps=d0, tag=f"T{tb}lg{nsi}")
                h_lwu = sch.dma_batch("sync",
                                      [(wsu[:, k * 512:k * 512 + wns], wq_u[k * 128:(k + 1) * 128, noff:noff + wns])
                                       for k in range(NDC)], deps=d0, tag=f"T{tb}lu{nsi}")
                prev_wload = h_lwu
                for tc in range(4):
                    t0 = tb * 512 + tc * 128
                    dmm = [h_lwg, h_lwu] + xq_done
                    if prev_g[0] is not None:
                        dmm.append(prev_g[0])
                    if prev_down_ev is not None:
                        dmm.append(prev_down_ev)
                    pairs_g = [(xq[k][:, t0:t0 + 128], wsg[:, k * 512:k * 512 + wns]) for k in range(NDC)]
                    h_gm = e_matmul_acc(sch, ps[0][:, 0:wns], pairs_g, dmm, tag=f"T{tb}g{nsi}_{tc}")
                    pairs_u = [(xq[k][:, t0:t0 + 128], wsu[:, k * 512:k * 512 + wns]) for k in range(NDC)]
                    h_um = e_matmul_acc(sch, ps[1][:, 0:wns], pairs_u, [h_gm], tag=f"T{tb}u{nsi}_{tc}")
                    h_si = e_act(sch, slot[0][:, 0:wns], ps[0][:, 0:wns], AF.Silu,
                                 [h_gm] + ([prev_g[1]] if prev_g[1] else []), tag=f"T{tb}si{nsi}_{tc}")
                    h_h = e_tt(sch, slot[1][:, 0:wns], slot[0][:, 0:wns], ps[1][:, 0:wns], ALU.mult,
                               [h_um, h_si], tag=f"T{tb}h{nsi}_{tc}")
                    prev_g[0] = h_h
                    nseg = wns // 16
                    h_m = e_red(sch, msl[:, 0:nseg], slot[1][:, 0:wns].rearrange("p (s n) -> p s n", n=16),
                                ALU.max, [h_h], tag=f"T{tb}m{nsi}_{tc}", absval=True)
                    h_s = e_ts(sch, msl[:, 0:nseg], msl[:, 0:nseg], 1.0 / 6.0, ALU.mult, [h_m], tag=f"T{tb}s{nsi}_{tc}")
                    h_sm = e_ts(sch, msl[:, 0:nseg], msl[:, 0:nseg], 1e-12, ALU.max, [h_s], tag=f"T{tb}sm{nsi}_{tc}")
                    h_rc = sch.op("vector",
                                  (lambda nseg: lambda v: v.reciprocal(msl[:, 32:32 + nseg], msl[:, 0:nseg]))(nseg),
                                  [h_sm], tag=f"T{tb}rc{nsi}_{tc}")
                    sb3 = msl[:, 0:nseg].rearrange("p (s one) -> p s one", one=1).broadcast_to([128, nseg, 16])
                    rb3 = msl[:, 32:32 + nseg].rearrange("p (s one) -> p s one", one=1).broadcast_to([128, nseg, 16])
                    h3 = lambda apx: apx.rearrange("p (s n) -> p s n", n=16)
                    qeng = "vector"
                    h_u2 = sch.op(qeng,
                                  (lambda wns, nseg, rb3: lambda v: v.tensor_tensor(
                                      h3(slot[2][:, 0:wns]), h3(slot[1][:, 0:wns]), rb3, ALU.mult))(wns, nseg, rb3),
                                  [h_rc], tag=f"T{tb}u2{nsi}_{tc}")
                    h_q = _emit_staircase(sch, slot[2][:, 0:wns], slot[3][:, 0:wns], slot[4][:, 0:wns],
                                          slot[2][:, 0:wns], slot[0][:, 0:wns], h3(slot[1][:, 0:wns]),
                                          sb3, [h_u2], tag=f"T{tb}hq{nsi}_{tc}_", eng=qeng)
                    for jj in range(wns // 128):
                        ii = jj % 2
                        dtp = [h_q]
                        if prev_tpp[ii] is not None:
                            dtp.append(prev_tpp[ii])
                        h_t = e_transpose(sch, ps[2 + ii][:, 0:128], slot[1][:, jj * 128:(jj + 1) * 128],
                                          idn[:], dtp, tag=f"T{tb}tp{nsi}_{tc}_{jj}")
                        nch = 4 * nsi + jj
                        h_c = e_copy(sch, "vector", hqb[:, nch * 512 + tc * 128:nch * 512 + (tc + 1) * 128],
                                     ps[2 + ii][:, 0:128], [h_t], tag=f"T{tb}tpc{nsi}_{tc}_{jj}")
                        prev_tpp[ii] = h_c
                        hq_done.append(h_c)
                    prev_g[1] = h_q
            h_dn = {o: None for o in range(8)}
            prev_wd = {0: None, 1: None}
            for nch in range(NFC):
                i = nch % 2
                dw = hq_done + h_wd
                if prev_wd[i] is not None:
                    dw.append(prev_wd[i])
                h_lw = sch.dma_batch("sync", [(wsd[i][:, :], wq_d[nch * 128:(nch + 1) * 128, :])],
                                     deps=dw, tag=f"T{tb}dl{nch}")
                for o in range(8):
                    dmm = [h_lw] + hq_done[-1:]
                    if nch == 0 and prev_down_ev is not None:
                        dmm.append(prev_down_ev)
                    h_dn[o] = sch.op("tensor",
                                     (lambda o, nch, i, tb:
                                      lambda t: t.matmul(ps[o][:, 0:512], wsd[i][:, o * 128:(o + 1) * 128],
                                                         hqb[:, nch * 512:(nch + 1) * 512],
                                                         start=(nch == 0), stop=(nch == NFC - 1),
                                                         skip_group_check=True))(o, nch, i, tb),
                                     dmm, tag=f"T{tb}dn{nch}_{o}")
                prev_wd[i] = h_dn[7]
            evs = []
            prev_dma2 = {0: None, 1: None}
            for o in range(8):
                ii = o % 2
                dv = [h_dn[o]]
                if prev_dma2[ii] is not None:
                    dv.append(prev_dma2[ii])
                h_ev = e_copy(sch, "vector", gout[ii][:, :], ps[o][:, 0:512], dv, tag=f"T{tb}de{o}")
                h_dd = sch.dma_batch("sync", [(o_g[o * 128:(o + 1) * 128, tb * 512:(tb + 1) * 512],
                                               gout[ii][:, :])], deps=[h_ev], tag=f"T{tb}dd{o}")
                prev_dma2[ii] = h_dd
                evs.append(h_ev)
            prev_down_ev = evs[-1]
        if _own:
            sch.emit()
    return nc


def make_e16():
    e = np.zeros((128, 16 * 128), dtype=np.float32)
    for s in range(16):
        for p in range(128):
            e[s * 8 + p // 16, s * 128 + p] = 1.0
    return e


E16 = make_e16()


def stage_fp4_ffn(inp, xn_list, pre):
    wg = f32(inp[f"{pre}_w_gate"]); wu = f32(inp[f"{pre}_w_up"]); wd = f32(inp[f"{pre}_w_down"])
    wg_t = f32(wg.T); wu_t = f32(wu.T); wd_t = f32(wd.T)
    maps = []
    for c in range(NCORES):
        maps.append(dict(xn=f32(xn_list[c]), wg_n=wg, wu_n=wu, wd_n=wd,
                         wg_t=wg_t, wu_t=wu_t, wd_t=wd_t, ident=IDENT128, e16=E16))
    return maps


def build_l2_attn(ctx=None):
    nc, _es0, _own = _stage_begin(ctx)
    D = _mk_D(nc, ctx)
    xn_d = D("xn", [DM, TPC], F32, "ExternalInput")
    wqkv_d = D("wqkv", [DM, 3 * DM], F32, "ExternalInput")
    wo_d = D("wo", [DM, DM], F32, "ExternalInput")
    bo_d = D("bo", [128, NDC], F32, "ExternalInput")
    maskc_d = D("maskc", [128, 128], F32, "ExternalInput")
    o_s = D("o_s", [DM, TPC], F32, "ExternalOutput")

    es = _es0
    sbf = lambda n, s, d=F32: es.enter_context(nc.sbuf_tensor(_uname(n), s, d))
    psm_ = lambda n, s: es.enter_context(nc.psum_tensor(_uname(n), s, F32))
    with es:
        xnb = [sbf(f"xnb{c}", [128, TPC], BF16) for c in range(NDC)]
        qB = [sbf(f"qB{c}", [128, TPC], BF16) for c in range(4)]
        kB = [sbf(f"kB{c}", [128, TPC], BF16) for c in range(4)]
        vaug = sbf("vaug", [128, NTT * 8 * 65], BF16)
        ofm = [sbf(f"ofm{c}", [128, TPC], BF16) for c in range(NDC)]
        wsl = sbf("wsl", [128, NDC * 512], BF16)
        wo_b = sbf("wo_b", [128, NDC * DM], BF16)
        maskc = sbf("maskc_s", [128, 128])
        boc = sbf("boc_s", [128, NDC])
        onesr = sbf("onesr", [1, 128])
        ptc = sbf("ptc", [128, 512], BF16)
        rrow = sbf("rrow", [1, 512])
        rbb = sbf("rbb", [64, 512])
        sout = [sbf(f"sout{i}", [128, 512]) for i in range(2)]
        pst = [psm_(f"pst{i}", [128, 512]) for i in range(2)]
        stc2 = [psm_(f"stc{i}", [128, 512]) for i in range(2)]
        oth = psm_("oth", [128, 512])
        psn = psm_("psn", [64, 512])

        sch = ctx.sch if ctx else Sched(nc, es)
        h_xn = sch.dma_batch("gpsimd", [(xnb[c][:, :], xn_d[c * 128:(c + 1) * 128, :]) for c in range(NDC)], tag="ldxn")
        h_sm = sch.dma_batch("sync", [(maskc[:], maskc_d[:, :]), (boc[:, :], bo_d[:, :])], tag="ldsm")
        h_wo = sch.dma_batch("gpsimd", [(wo_b[:, c * DM:(c + 1) * DM], wo_d[c * 128:(c + 1) * 128, :]) for c in range(NDC)], tag="ldwo")
        h_ones = [sch.op("vector", lambda v: v.memset(onesr[:], 1.0)),
                  sch.op("vector", lambda v: v.memset(vaug[:], 1.0))]
        vaug4 = vaug.rearrange("p (n h w) -> p n h w", n=NTT, h=8, w=65)

        NW = NTT
        attn_done = []
        prev_sl_user = None
        for half in range(2):
            h_wq = sch.dma_batch("gpsimd",
                                 [(wsl[:, c * 512:(c + 1) * 512],
                                   wqkv_d[c * 128:(c + 1) * 128, 512 * half:512 * half + 512])
                                  for c in range(NDC)],
                                 deps=[h_xn] + ([prev_sl_user] if prev_sl_user else []), tag=f"h{half}ldq")
            q_done = []
            prev_bank = {0: None, 1: None}
            for oi in range(4):
                for tb in range(4):
                    bank = (oi * 4 + tb) % 2
                    deps = [h_wq]
                    if prev_bank[bank] is not None:
                        deps.append(prev_bank[bank])
                    pairs = [(wsl[:, c * 512 + oi * 128:c * 512 + (oi + 1) * 128],
                              xnb[c][:, tb * 512:(tb + 1) * 512]) for c in range(NDC)]
                    h_mm = e_matmul_acc(sch, pst[bank][:, 0:512], pairs, deps, tag=f"h{half}q{oi}_{tb}")
                    h_ev = e_copy(sch, "vector", qB[oi][:, tb * 512:(tb + 1) * 512], pst[bank][:, 0:512],
                                  [h_mm], tag=f"h{half}qe{oi}_{tb}")
                    prev_bank[bank] = h_ev
                    q_done.append(h_ev)
            h_wk = sch.dma_batch("gpsimd",
                                 [(wsl[:, c * 512:(c + 1) * 512],
                                   wqkv_d[c * 128:(c + 1) * 128, DM + 512 * half:DM + 512 * half + 512])
                                  for c in range(NDC)],
                                 deps=q_done, tag=f"h{half}ldk")
            k_done = []
            prev_bank = {0: None, 1: None}
            for oi in range(4):
                for tb in range(4):
                    bank = (oi * 4 + tb) % 2
                    deps = [h_wk]
                    if prev_bank[bank] is not None:
                        deps.append(prev_bank[bank])
                    pairs = [(wsl[:, c * 512 + oi * 128:c * 512 + (oi + 1) * 128],
                              xnb[c][:, tb * 512:(tb + 1) * 512]) for c in range(NDC)]
                    h_mm = e_matmul_acc(sch, pst[bank][:, 0:512], pairs, deps, tag=f"h{half}k{oi}_{tb}")
                    h_ev = e_copy(sch, "vector", kB[oi][:, tb * 512:(tb + 1) * 512], pst[bank][:, 0:512],
                                  [h_mm], tag=f"h{half}ke{oi}_{tb}")
                    prev_bank[bank] = h_ev
                    k_done.append(h_ev)
            h_wv = sch.dma_batch("gpsimd",
                                 [(wsl[:, c * 512:(c + 1) * 512],
                                   wqkv_d[c * 128:(c + 1) * 128, 2 * DM + 512 * half:2 * DM + 512 * half + 512])
                                  for c in range(NDC)],
                                 deps=k_done, tag=f"h{half}ldv")
            v_done = []
            prev_bank = {0: None, 1: None}
            for ti in range(NTT):
                bank = ti % 2
                deps = [h_wv] + h_ones
                if prev_bank[bank] is not None:
                    deps.append(prev_bank[bank])
                pairs = [(xnb[c][:, ti * 128:(ti + 1) * 128], wsl[:, c * 512:(c + 1) * 512])
                         for c in range(NDC)]
                h_mm = e_matmul_acc(sch, pst[bank][:, 0:512], pairs, deps, tag=f"h{half}v{ti}")
                h_ev = e_copy(sch, "vector", vaug4[:, ti, :, 0:64],
                              pst[bank][:, 0:512].rearrange("p (h w) -> p h w", w=64), [h_mm],
                              tag=f"h{half}ve{ti}")
                prev_bank[bank] = h_ev
                v_done.append(h_ev)
            prev_sl_user = v_done[-1]

            exp_prev = None
            on_prev = None
            for w_i in range(NW):
                for hg in range(2):
                    h_stc = None
                    for j in range(4):
                        hj = hg * 4 + j
                        qc = qB[hj // 2]
                        kc = kB[hj // 2]
                        base = 64 * (hj % 2)
                        bk = hj % 2
                        slc = (j // 2) * 128
                        deps = q_done + k_done
                        if j == 0 and exp_prev is not None:
                            deps = deps + [exp_prev]
                        h_stc = sch.op("tensor",
                                       (lambda kc, qc, base, bk, slc, w_i:
                                        lambda t: t.matmul(stc2[bk][:, slc:slc + 128],
                                                           kc[base:base + 64, w_i * 128:(w_i + 1) * 128],
                                                           qc[base:base + 64, w_i * 128:(w_i + 1) * 128],
                                                           start=True, stop=True, skip_group_check=True))(
                                           kc, qc, base, bk, slc, w_i),
                                       deps, tag=f"h{half}stc{w_i}_{hg}_{j}")
                    ed = [h_stc]
                    if exp_prev is not None:
                        ed.append(exp_prev)
                    e_act(sch, ptc[:, 0:256], stc2[0][:, 0:256], AF.Exp, ed, scale=0.125,
                          tag=f"h{half}ec0{w_i}_{hg}")
                    h_ec = e_act(sch, ptc[:, 256:512], stc2[1][:, 0:256], AF.Exp, [], scale=0.125,
                                 tag=f"h{half}ec1{w_i}_{hg}")
                    mc = maskc[:, :].rearrange("p (one w) -> p one w", one=1).broadcast_to([128, 4, 128])
                    h_mc = e_tt(sch, ptc[:, 0:512].rearrange("p (n w) -> p n w", w=128),
                                ptc[:, 0:512].rearrange("p (n w) -> p n w", w=128), mc, ALU.mult,
                                [h_ec, h_sm], tag=f"h{half}mc{w_i}_{hg}")
                    h_ot = None
                    for j in range(4):
                        hj = hg * 4 + j
                        ptcol = (j % 2) * 256 + (j // 2) * 128
                        deps_ot = [h_mc] + v_done
                        if on_prev is not None and j == 0:
                            deps_ot.append(on_prev)
                        h_ot = sch.op("tensor",
                                      (lambda w_i, hj, j, ptcol:
                                       lambda t: t.matmul(oth[0:65, j * 128:(j + 1) * 128],
                                                          vaug4[:, w_i, hj, :],
                                                          ptc[:, ptcol:ptcol + 128],
                                                          start=True, stop=True, skip_group_check=True))(w_i, hj, j, ptcol),
                                      deps_ot, tag=f"h{half}ot_{w_i}_{hg}_{j}")
                    h_se = e_ts(sch, rrow[0:1, 0:512], oth[64:65, 0:512], 1e-6, ALU.add, [h_ot],
                                tag=f"h{half}se{w_i}_{hg}")
                    h_rr2 = sch.op("vector", lambda v: v.reciprocal(rrow[0:1, 0:512], rrow[0:1, 0:512]),
                                   [h_se], tag=f"h{half}rr{w_i}_{hg}")
                    h_rb = e_matmul_acc(sch, psn[0:64, 0:512], [(onesr[0:1, 0:64], rrow[0:1, 0:512])],
                                        [h_rr2] + h_ones, tag=f"h{half}rb{w_i}_{hg}")
                    h_rbe = e_copy(sch, "vector", rbb[:, :], psn[0:64, 0:512], [h_rb], tag=f"h{half}rbe{w_i}_{hg}")
                    ons = []
                    for j in range(4):
                        hj = hg * 4 + j
                        habs = 8 * half + hj
                        h_on = e_stt(sch, ofm[habs // 2][64 * (habs % 2):64 * (habs % 2) + 64, w_i * 128:(w_i + 1) * 128],
                                     oth[0:64, j * 128:(j + 1) * 128], 1.0, rbb[:, j * 128:(j + 1) * 128],
                                     ALU.mult, ALU.mult, [h_rbe], tag=f"h{half}on{w_i}_{hg}_{j}")
                        ons.append(h_on)
                    on_prev = ons[-1]
                    exp_prev = ons[-1]
            attn_done.append(on_prev)

        prev_bank = {0: None}
        prev_dma = {0: None, 1: None}
        for oi in range(NDC):
            for tb in range(4):
                qb_ = (oi * 4 + tb) % 2
                deps = attn_done + [h_wo]
                if prev_bank[0] is not None:
                    deps.append(prev_bank[0])
                pairs = [(wo_b[:, c * DM + oi * 128:c * DM + (oi + 1) * 128],
                          ofm[c][:, tb * 512:(tb + 1) * 512]) for c in range(NDC)]
                h_mm = e_matmul_acc(sch, pst[0][:, 0:512], pairs, deps, tag=f"wo{oi}_{tb}")
                ev_deps = [h_mm, h_sm]
                if prev_dma[qb_] is not None:
                    ev_deps.append(prev_dma[qb_])
                h_ev = e_act(sch, sout[qb_][:, 0:512], pst[0][:, 0:512], AF.Identity, ev_deps,
                             bias=boc[:, oi:oi + 1], tag=f"woev{oi}_{tb}")
                prev_bank[0] = h_ev
                h_dma = sch.dma("sync", o_s[oi * 128:(oi + 1) * 128, tb * 512:(tb + 1) * 512],
                                sout[qb_][:, 0:512], deps=[h_ev], tag=f"wodma{oi}_{tb}")
                prev_dma[qb_] = h_dma
        if _own:
            sch.emit()
    return nc


def stage_l2_attn(inp, xn_list):
    wqkv = f32(inp["L2_attn_W_qkv"])
    wo = f32(inp["L2_attn_W_proj"])
    boc = f32(inp["L2_attn_b_proj"].reshape(NDC, 128).T)
    i = np.arange(128)[:, None]; j = np.arange(128)[None, :]
    maskc = f32((i <= j))
    return [dict(xn=f32(xn_list[c]), wqkv=wqkv, wo=wo, bo=boc, maskc=maskc)
            for c in range(NCORES)]


def build_l3_scan(ctx=None):
    nc, _es0, _own = _stage_begin(ctx)
    D = _mk_D(nc, ctx)
    xn_d = D("xn", [DM, TPC], F32, "ExternalInput")
    wt_d = D("wt", [DM, 3 * DM], F32, "ExternalInput")
    gcol_d = D("gcol", [128, NDC], F32, "ExternalInput")
    o_q = D("o_q", [DM, TPC], F32, "ExternalOutput")
    o_yloc = D("o_yloc", [DM, TPC], F32, "ExternalOutput")
    o_acum = D("o_acum", [DM, TPC], F32, "ExternalOutput")
    o_tot = D("o_tot", [DM, 2], F32, "ExternalOutput")

    es = _es0
    sbf = lambda n, s, d=F32: es.enter_context(nc.sbuf_tensor(_uname(n), s, d))
    psm_ = lambda n, s: es.enter_context(nc.psum_tensor(_uname(n), s, F32))
    with es:
        xf = [sbf(f"xf{c}", [128, TPC]) for c in range(NDC)]
        xnr = [sbf(f"xnr{c}", [128, TPC], BF16) for c in range(NDC)]
        wb = sbf("wb", [128, NDC * 3 * DM], BF16)
        sq = [sbf(f"sq{i}", [128, 512]) for i in range(2)]
        srow = sbf("srow", [1, TPC])
        rstb = sbf("rstb", [128, 512])
        gcol = sbf("gcol_s", [128, NDC])
        onesc = sbf("onesc", [128, 1])
        onesr = sbf("onesr", [1, 128])
        av = sbf("av", [128, TPC])
        kv = sbf("kv", [128, TPC])
        ysc = sbf("ysc", [128, TPC])
        asc = sbf("asc", [128, TPC])
        qo = [sbf(f"qo{i}", [128, 512]) for i in range(2)]
        totb = sbf("totb", [128, 2])
        psx = psm_("psx", [1, 512])
        psbc = psm_("psbc", [128, 512])
        pst = [psm_(f"pst{i}", [128, 512]) for i in range(2)]

        sch = ctx.sch if ctx else Sched(nc, es)
        h_x = sch.dma_batch("sync", [(xf[c][:, :], xn_d[c * 128:(c + 1) * 128, :]) for c in range(NDC)], tag="ldx")
        h_w = sch.dma_batch("gpsimd", [(wb[:, c * 3 * DM:(c + 1) * 3 * DM], wt_d[c * 128:(c + 1) * 128, :]) for c in range(NDC)], tag="ldw")
        h_sm = sch.dma_batch("sync", [(gcol[:, :], gcol_d[:, :])], tag="ldg")
        h_ones = [sch.op("vector", lambda v: v.memset(onesc[:], 1.0)),
                  sch.op("vector", lambda v: v.memset(onesr[:], 1.0))]

        h_ss = {}
        last_sq_user = {0: None, 1: None}
        for tb in range(4):
            sl = slice(tb * 512, (tb + 1) * 512)
            h_mmx2 = None
            for c in range(NDC):
                i = c % 2
                sq_deps = [h_x] + ([last_sq_user[i]] if last_sq_user[i] else [])
                h_sq = e_act(sch, sq[i][:, 0:512], xf[c][:, sl], AF.Square, sq_deps, tag=f"sq{tb}_{c}")
                h_mmx2 = sch.op(
                    "tensor",
                    (lambda c, i: lambda t: t.matmul(psx[0:1, 0:512], onesc[:], sq[i][:, 0:512],
                                                     start=(c == 0), stop=(c == NDC - 1),
                                                     skip_group_check=True))(c, i),
                    [h_sq] + h_ones + ([h_ss[tb - 1]] if tb > 0 else []), tag=f"mm{tb}_{c}")
                last_sq_user[i] = h_mmx2
            h_ss[tb] = e_copy(sch, "vector", srow[0:1, sl], psx[0:1, 0:512], [h_mmx2], tag=f"ev{tb}")
        h_sd = e_act(sch, srow[0:1, :], srow[0:1, :], AF.Sqrt, list(h_ss.values()), tag="sd")
        h_mx = e_ts(sch, srow[0:1, :], srow[0:1, :], 1e-12, ALU.max, [h_sd], tag="mx")
        h_rc = sch.op("vector", lambda v: v.reciprocal(srow[0:1, :], srow[0:1, :]), [h_mx], tag="rc")
        h_sc = e_ts(sch, srow[0:1, :], srow[0:1, :], float(DM) ** 0.5, ALU.mult, [h_rc], tag="scl")
        xnr_done = []
        prev_bc = None
        for tb in range(4):
            sl = slice(tb * 512, (tb + 1) * 512)
            h_bm = e_matmul_acc(sch, psbc[:, 0:512], [(onesr[:], srow[0:1, sl])],
                                [h_sc] + ([prev_bc] if prev_bc else []), tag=f"bc{tb}")
            h_bv = e_copy(sch, "vector", rstb[:, 0:512], psbc[:, 0:512], [h_bm], tag=f"bv{tb}")
            prev_bc = h_bv
            for c in range(NDC):
                h_ap = e_stt(sch, xnr[c][:, sl], xf[c][:, sl], gcol[:, c:c + 1], rstb[:, 0:512],
                             ALU.mult, ALU.mult, [h_bv, h_sm], tag=f"ap{tb}_{c}")
                xnr_done.append(h_ap)
            prev_bc = xnr_done[-1]

        prev_bank = {0: None, 1: None}
        prev_qdma = {0: None, 1: None}

        def proj_chunk(kind, oi, evfn, deps_extra):
            evs = []
            for tb in range(4):
                bank = (oi * 4 + tb) % 2
                deps = [h_w] + xnr_done + deps_extra
                if prev_bank[bank] is not None:
                    deps.append(prev_bank[bank])
                pairs = [(wb[:, c * 3 * DM + kind * DM + oi * 128:c * 3 * DM + kind * DM + (oi + 1) * 128],
                          xnr[c][:, tb * 512:(tb + 1) * 512]) for c in range(NDC)]
                h_mm = e_matmul_acc(sch, pst[bank][:, 0:512], pairs, deps, tag=f"k{kind}o{oi}t{tb}")
                h_ev = evfn(tb, bank, h_mm)
                prev_bank[bank] = h_ev
                evs.append(h_ev)
            return evs

        for oi in range(NDC):
            def q_ev(tb, bank, h_mm, oi=oi):
                qb_ = bank
                evd = [h_mm]
                if prev_qdma[qb_] is not None:
                    evd.append(prev_qdma[qb_])
                h_ev = e_copy(sch, "vector", qo[qb_][:, 0:512], pst[bank][:, 0:512], evd, tag=f"qe{oi}_{tb}")
                h_d = sch.dma("sync", o_q[oi * 128:(oi + 1) * 128, tb * 512:(tb + 1) * 512],
                              qo[qb_][:, 0:512], deps=[h_ev], tag=f"qd{oi}_{tb}")
                prev_qdma[qb_] = h_d
                return h_ev
            proj_chunk(0, oi, q_ev, [])

        h_scan_prev = None
        prev_store = None
        for oi in range(NDC):
            sp = [h_scan_prev] if h_scan_prev else []
            kv_evs = proj_chunk(1, oi, lambda tb, bank, h_mm, oi=oi: e_copy(
                sch, "vector", kv[:, tb * 512:(tb + 1) * 512], pst[bank][:, 0:512],
                [h_mm] + sp, tag=f"kve{oi}_{tb}"), [])
            a_evs = proj_chunk(2, oi, lambda tb, bank, h_mm, oi=oi: e_act(
                sch, av[:, tb * 512:(tb + 1) * 512], pst[bank][:, 0:512], AF.Sigmoid,
                [h_mm] + sp, tag=f"ave{oi}_{tb}"), [])
            h_y = sch.op("vector",
                         lambda v: v.tensor_tensor_scan(ysc[:, :], av[:, :], kv[:, :], 0.0,
                                                        ALU.mult, ALU.add),
                         kv_evs + a_evs + ([prev_store] if prev_store else []), tag=f"scy{oi}")
            h_a = sch.op("vector",
                         lambda v: v.tensor_tensor_scan(asc[:, :], av[:, :], av[:, :], 1.0,
                                                        ALU.mult, ALU.bypass),
                         [h_y], tag=f"sca{oi}")
            h_t1 = e_copy(sch, "vector", totb[:, 0:1], asc[:, TPC - 1:TPC], [h_a], tag=f"t1{oi}")
            h_t2 = e_copy(sch, "vector", totb[:, 1:2], ysc[:, TPC - 1:TPC], [h_t1], tag=f"t2{oi}")
            h_d1 = sch.dma_batch("sync", [
                (o_yloc[oi * 128:(oi + 1) * 128, :], ysc[:, :]),
                (o_acum[oi * 128:(oi + 1) * 128, :], asc[:, :]),
                (o_tot[oi * 128:(oi + 1) * 128, :], totb[:, :]),
            ], deps=[h_t2], tag=f"sd{oi}")
            prev_store = h_d1
            h_scan_prev = h_t2
        if _own:
            sch.emit()
    return nc


def build_l3_fix(ctx=None):
    nc, _es0, _own = _stage_begin(ctx)
    D = _mk_D(nc, ctx)
    q_d = D("q", [DM, TPC], F32, "ExternalInput")
    yl_d = D("yl", [DM, TPC], F32, "ExternalInput")
    ac_d = D("ac", [DM, TPC], F32, "ExternalInput")
    gathered = ctx is not None
    if gathered:
        tp_d = D("tp", [8 * DM, 2], F32, "Internal")
        msel_d = D("msel", [128, 8], F32, "ExternalInput")
    else:
        tp_d = D("tp", [DM, 6], F32, "ExternalInput")
    o_s = D("o_s", [DM, TPC], F32, "ExternalOutput")
    es = _es0
    sbf = lambda n, s, d=F32: es.enter_context(nc.sbuf_tensor(_uname(n), s, d))
    with es:
        qb = sbf("qb", [128, TPC])
        yb = sbf("yb", [128, TPC])
        ab = sbf("ab", [128, TPC])
        nslot = 8 if gathered else 3
        tp = sbf("tp_s", [128, 2 * nslot])
        ms8 = sbf("ms8", [128, 8]) if gathered else None
        yin = sbf("yin", [128, 3])
        so = sbf("so", [128, TPC])
        sch = ctx.sch if ctx else Sched(nc, es)
        prev = None
        h_ms = sch.dma_batch("sync", [(ms8[:, :], msel_d[:, :])], tag="ldms") if gathered else None
        for oi in range(NDC):
            d0 = [prev] if prev else []
            if gathered:
                tp_src = tp_d.rearrange("(s d) two -> d s two", s=8)[oi * 128:(oi + 1) * 128, :, :]
                tp_pairs = [(tp[:, :].rearrange("p (s two) -> p s two", s=8), tp_src)]
            else:
                tp_pairs = [(tp[:, :], tp_d[oi * 128:(oi + 1) * 128, :])]
            h_ld = sch.dma_batch("sync", [
                (qb[:, :], q_d[oi * 128:(oi + 1) * 128, :]),
                (yb[:, :], yl_d[oi * 128:(oi + 1) * 128, :]),
                (ab[:, :], ac_d[oi * 128:(oi + 1) * 128, :]),
            ] + tp_pairs, deps=d0, tag=f"ld{oi}")
            h_z = e_memset(sch, "vector", yin[:, 0:1], 0.0, [h_ld] + d0 + ([h_ms] if h_ms else []), tag=f"z{oi}")
            h = h_z
            for s in range(nslot):
                h_m = e_tt(sch, yin[:, 1:2], tp[:, 2 * s:2 * s + 1], yin[:, 0:1], ALU.mult, [h, h_ld], tag=f"f{oi}_{s}m")
                if gathered:
                    h_m = e_tt(sch, yin[:, 1:2], yin[:, 1:2], tp[:, 2 * s + 1:2 * s + 2], ALU.add, [h_m], tag=f"f{oi}_{s}a2")
                    h_m = e_tt(sch, yin[:, 2:3], yin[:, 1:2], yin[:, 0:1], ALU.subtract, [h_m], tag=f"f{oi}_{s}s")
                    h_m = e_ts(sch, yin[:, 2:3], yin[:, 2:3], ms8[:, s:s + 1], ALU.mult, [h_m], tag=f"f{oi}_{s}mm")
                    h = e_tt(sch, yin[:, 0:1], yin[:, 0:1], yin[:, 2:3], ALU.add, [h_m], tag=f"f{oi}_{s}u")
                else:
                    h = e_tt(sch, yin[:, 0:1], tp[:, 2 * s + 1:2 * s + 2], yin[:, 1:2], ALU.add, [h_m], tag=f"f{oi}_{s}a")
            h_t = e_ts(sch, ab[:, :], ab[:, :], yin[:, 0:1], ALU.mult, [h], tag=f"t{oi}")
            h_yf = e_tt(sch, yb[:, :], yb[:, :], ab[:, :], ALU.add, [h_t], tag=f"yf{oi}")
            h_s = e_tt(sch, so[:, :], yb[:, :], qb[:, :], ALU.mult, [h_yf], tag=f"s{oi}")
            h_d = sch.dma_batch("sync", [(o_s[oi * 128:(oi + 1) * 128, :], so[:, :])], deps=[h_s], tag=f"sd{oi}")
            prev = h_d
        if _own:
            sch.emit()
    return nc


def stage_l3_scan(inp, xn_list):
    wt = f32(inp["L3_attn_w_qkva"].T)
    gcol = f32(inp["L3_attn_gamma"].reshape(NDC, 128).T)
    return [dict(xn=f32(xn_list[c]), wt=wt, gcol=gcol) for c in range(NCORES)]


def stage_l3_fix(scan_results):
    maps = []
    ident_tot = np.zeros((DM, 2), np.float32); ident_tot[:, 0] = 1.0
    for c in range(NCORES):
        p = c % GPB
        slots = []
        for s in range(3):
            src = c - 3 + s
            if src < c - p or src < 0 or (src // GPB) != (c // GPB):
                slots.append(ident_tot)
            else:
                slots.append(f32(scan_results[src]["o_tot"]))
        tp = np.concatenate(slots, axis=1)
        maps.append(dict(q=f32(scan_results[c]["o_q"]), yl=f32(scan_results[c]["o_yloc"]),
                         ac=f32(scan_results[c]["o_acum"]), tp=f32(tp)))
    return maps


H3 = 513
TE3 = TPC + H3
KT0 = 512
NKC3 = (TPC + 512) // 128

def build_l3f_attn(ctx=None):
    nc, _es0, _own = _stage_begin(ctx)
    D = _mk_D(nc, ctx)
    x_d = D("x", [DM, TE3], F32, "ExternalInput")
    wqkv_d = D("wqkv", [DM, 3 * 512], F32, "ExternalInput")
    wo_d = D("wo", [512, DM], F32, "ExternalInput")
    cs_d = D("cs", [128, TPC + 512], F32, "ExternalInput")
    ss_d = D("ss", [128, TPC + 512], F32, "ExternalInput")
    ci_d = D("ci", [128, TPC + 512], F32, "ExternalInput")
    si_d = D("si", [128, TPC + 512], F32, "ExternalInput")
    qs_d = D("qs", [128, 1], F32, "ExternalInput")
    ks_d = D("ks", [128, 1], F32, "ExternalInput")
    trif_d = D("trif", [128, 128], F32, "ExternalInput")
    trid_d = D("trid", [128, 128], F32, "ExternalInput")
    valid_d = D("valid", [128, 80], F32, "ExternalInput")
    o_x = D("o_x", [DM, TPC], F32, "ExternalOutput")

    TK = TPC + 512
    es = _es0
    sbf = lambda n, s, d=F32: es.enter_context(nc.sbuf_tensor(_uname(n), s, d))
    psm_ = lambda n, s: es.enter_context(nc.psum_tensor(_uname(n), s, F32))
    with es:
        xe = [sbf(f"xe{c}", [128, TE3], BF16) for c in range(NDC)]
        scr = sbf("scr", [64, TK])
        rot = sbf("rot", [64, TK])
        qh = sbf("qh", [64, TPC], BF16)
        kh = sbf("kh", [64, TK], BF16)
        vaug = sbf("vaug", [128, NKC3 * 8 * 65], BF16)
        ofm = [sbf(f"ofm{c}", [128, TPC], BF16) for c in range(4)]
        wslq = sbf("wslq", [128, NDC * 512], BF16)
        wslk = sbf("wslk", [128, NDC * 512], BF16)
        wob = sbf("wob", [128, 4 * DM], BF16)
        tabc = sbf("tabc", [128, TK])
        tabs = sbf("tabs", [128, TK])
        qsc = sbf("qsc_s", [128, 1]); ksc = sbf("ksc_s", [128, 1])
        trif = sbf("trif_s", [128, 128]); trid = sbf("trid_s", [128, 128])
        valid = sbf("valid_s", [128, 80])
        mall = sbf("mall", [128, 80 * 128], BF16)
        nrow = sbf("nrow", [1, TK])
        rtile = sbf("rtile", [64, 512])
        sqs = [sbf(f"sqs{i}", [64, 512]) for i in range(2)]
        onesc = sbf("onesc", [128, 1])
        onesr = sbf("onesr", [1, 128])
        onet = sbf("onet", [128, 128])
        ptc = sbf("ptc", [128, 640], BF16)
        rrow = sbf("rrow", [1, 128])
        rbb = sbf("rbb", [64, 128])
        xres = [sbf(f"xres{i}", [128, 512]) for i in range(2)]
        sout = [sbf(f"sout{i}", [128, 512]) for i in range(2)]
        pst = [psm_(f"pst{i}", [128, 512]) for i in range(2)]
        psx = psm_("psx", [1, 512])
        psbc = psm_("psbc", [128, 512])
        sta = psm_("sta", [128, 512])
        stb = psm_("stb", [128, 512])
        oth = psm_("oth", [128, 512])
        sch = ctx.sch if ctx else Sched(nc, es)

        h_x = sch.dma_batch("gpsimd", [(xe[c][:, :], x_d[c * 128:(c + 1) * 128, :]) for c in range(NDC)], tag="ldx")
        h_t0 = sch.dma_batch("sync", [(qsc[:], qs_d[:, :]), (ksc[:], ks_d[:, :]),
                                      (trif[:], trif_d[:, :]), (trid[:], trid_d[:, :]),
                                      (valid[:], valid_d[:, :])], tag="ldt")
        h_wo = sch.dma_batch("gpsimd", [(wob[:, k * DM:(k + 1) * DM], wo_d[k * 128:(k + 1) * 128, :]) for k in range(4)], tag="ldwo")
        h_ones = [sch.op("vector", lambda v: v.memset(onesc[:], 1.0)),
                  sch.op("vector", lambda v: v.memset(onesr[:], 1.0)),
                  sch.op("vector", lambda v: v.memset(onet[:], 1.0)),
                  sch.op("vector", lambda v: v.memset(vaug[:], 1.0))]

        h_masks = []
        prev_m = None
        for w in range(16):
            for r in range(5):
                tri = trif[:, :] if r == 0 else (trid[:, :] if r == 4 else onet[:, :])
                col = valid[:, w * 5 + r:w * 5 + r + 1]
                dst = mall[:, (w * 5 + r) * 128:(w * 5 + r + 1) * 128]
                dd = [h_t0] + h_ones + ([prev_m] if prev_m else [])
                hm = e_ts(sch, dst, tri, col, ALU.mult, dd, tag=f"m{w}_{r}")
                prev_m = hm
                h_masks.append(hm)
        h_mask_done = h_masks[-1]

        def rhs_ap(c, off, wdt):
            o = off if c < 4 else off - 1
            return xe[c][:, o:o + wdt]

        h_wv = sch.dma_batch("gpsimd", [(wslq[:, c * 512:(c + 1) * 512], wqkv_d[c * 128:(c + 1) * 128, 1024:1536]) for c in range(NDC)],
                             deps=[h_x], tag="ldwv")
        vaug4 = vaug.rearrange("p (n h w) -> p n h w", n=NKC3, h=8, w=65)
        v_done = []
        prev_bank = {0: None, 1: None}
        for ti in range(NKC3):
            bank = ti % 2
            deps = [h_wv] + h_ones
            if prev_bank[bank] is not None:
                deps.append(prev_bank[bank])
            pairs = [(rhs_ap(c, 1 + ti * 128, 128), wslq[:, c * 512:(c + 1) * 512]) for c in range(NDC)]
            h_mm = e_matmul_acc(sch, pst[bank][:, 0:512], pairs, deps, tag=f"v{ti}")
            h_ev = e_copy(sch, "vector", vaug4[:, ti, :, 0:64],
                          pst[bank][:, 0:512].rearrange("p (h w) -> p h w", w=64), [h_mm], tag=f"ve{ti}")
            prev_bank[bank] = h_ev
            v_done.append(h_ev)

        h_wq = sch.dma_batch("gpsimd", [(wslq[:, c * 512:(c + 1) * 512], wqkv_d[c * 128:(c + 1) * 128, 0:512]) for c in range(NDC)],
                             deps=[v_done[-1]], tag="ldwq")
        h_wk = sch.dma_batch("gpsimd", [(wslk[:, c * 512:(c + 1) * 512], wqkv_d[c * 128:(c + 1) * 128, 512:1024]) for c in range(NDC)],
                             deps=[v_done[-1], h_wq], tag="ldwk")
        on_prev = None
        exp_prev = None
        prev_tab = None
        prev_head_scr = None
        for h in range(8):
            d_tab = [h_x] + ([prev_tab] if prev_tab else [])
            h_tq = sch.dma_batch("sync", [(tabc[:, 0:TPC], cs_d[:, 512:512 + TPC]),
                                          (tabs[:, 0:TPC], ss_d[:, 512:512 + TPC])],
                                 deps=d_tab, tag=f"tq{h}")

            def proj_head(part, h, T0, TW, dst_bf, tagp, deps0):
                done = []
                for b2 in range(TW // 512):
                    bank = b2 % 2
                    deps = deps0 + [h_x]
                    if prev_bank[bank] is not None:
                        deps.append(prev_bank[bank])
                    off = (H3 if part == 0 else 1) + b2 * 512
                    wsl_ = wslq if part == 0 else wslk
                    pairs = [(wsl_[:, c * 512 + h * 64:c * 512 + h * 64 + 64],
                              rhs_ap(c, off, 512)) for c in range(NDC)]
                    h_mm = e_matmul_acc(sch, pst[bank][0:64, 0:512], pairs, deps, tag=f"{tagp}p{h}_{b2}")
                    h_ev = e_copy(sch, "vector", scr[:, b2 * 512:(b2 + 1) * 512], pst[bank][0:64, 0:512],
                                  [h_mm], tag=f"{tagp}e{h}_{b2}")
                    prev_bank[bank] = h_ev
                    done.append(h_ev)
                return done

            d0 = [h_wq] + ([prev_head_scr] if prev_head_scr else [])
            q_done = proj_head(0, h, 512, TPC, qh, "q", d0)
            row_done = []
            for b2 in range(TPC // 512):
                i = b2 % 2
                h_sq = e_act(sch, sqs[i][:, 0:512], scr[:, b2 * 512:(b2 + 1) * 512], AF.Square,
                             q_done + row_done[-2:], tag=f"qsq{h}_{b2}")
                h_mm = e_matmul_acc(sch, psx[0:1, 0:512], [(onesc[0:64, :], sqs[i][:, 0:512])],
                                    [h_sq], tag=f"qss{h}_{b2}")
                h_evr = e_copy(sch, "vector", nrow[0:1, b2 * 512:(b2 + 1) * 512], psx[0:1, 0:512],
                               [h_mm], tag=f"qse{h}_{b2}")
                row_done.append(h_evr)
            h_sd = e_act(sch, nrow[0:1, 0:TPC], nrow[0:1, 0:TPC], AF.Sqrt, row_done, tag=f"qsd{h}")
            h_mx = e_ts(sch, nrow[0:1, 0:TPC], nrow[0:1, 0:TPC], 1e-12, ALU.max, [h_sd], tag=f"qmx{h}")
            h_rc = sch.op("vector", lambda v: v.reciprocal(nrow[0:1, 0:TPC], nrow[0:1, 0:TPC]),
                          [h_mx], tag=f"qrc{h}")
            ap_done = []
            for b2 in range(TPC // 512):
                h_bm = e_matmul_acc(sch, psbc[0:64, 0:512], [(onesr[0:1, 0:64], nrow[0:1, b2 * 512:(b2 + 1) * 512])],
                                    [h_rc] + ap_done[-1:], tag=f"qbc{h}_{b2}")
                h_bv = e_copy(sch, "vector", rtile[:, 0:512], psbc[0:64, 0:512], [h_bm], tag=f"qbv{h}_{b2}")
                h_ap = e_stt(sch, scr[:, b2 * 512:(b2 + 1) * 512], scr[:, b2 * 512:(b2 + 1) * 512],
                             qsc[0:64, 0:1], rtile[:, 0:512], ALU.mult, ALU.mult, [h_bv, h_t0],
                             tag=f"qap{h}_{b2}")
                ap_done.append(h_ap)
            h_r1 = e_copy(sch, "vector", rot[0:32, 0:TPC], scr[32:64, 0:TPC], ap_done, tag=f"qr1{h}")
            h_r2 = e_copy(sch, "vector", rot[32:64, 0:TPC], scr[0:32, 0:TPC], [h_r1] + ap_done, tag=f"qr2{h}")
            h_m1 = e_tt(sch, scr[:, 0:TPC], scr[:, 0:TPC], tabc[0:64, 0:TPC], ALU.mult, [h_r2, h_tq], tag=f"qm1{h}")
            h_m2 = e_tt(sch, rot[:, 0:TPC], rot[:, 0:TPC], tabs[0:64, 0:TPC], ALU.mult, [h_m1, h_tq], tag=f"qm2{h}")
            h_q = sch.op("vector", lambda v: v.tensor_tensor(qh[:, :], scr[:, 0:TPC], rot[:, 0:TPC], ALU.add),
                         [h_m2], tag=f"qad{h}")
            h_tk = sch.dma_batch("sync", [(tabc[:, 0:TK], ci_d[:, :]), (tabs[:, 0:TK], si_d[:, :])],
                                 deps=[h_m1, h_m2], tag=f"tk{h}")
            k_done = proj_head(1, h, 0, TK, kh, "k", [h_wk, h_q])
            row_done = []
            for b2 in range(TK // 512):
                i = b2 % 2
                h_sq = e_act(sch, sqs[i][:, 0:512], scr[:, b2 * 512:(b2 + 1) * 512], AF.Square,
                             k_done + row_done[-2:], tag=f"ksq{h}_{b2}")
                h_mm = e_matmul_acc(sch, psx[0:1, 0:512], [(onesc[0:64, :], sqs[i][:, 0:512])],
                                    [h_sq], tag=f"kss{h}_{b2}")
                h_evr = e_copy(sch, "vector", nrow[0:1, b2 * 512:(b2 + 1) * 512], psx[0:1, 0:512],
                               [h_mm], tag=f"kse{h}_{b2}")
                row_done.append(h_evr)
            h_sd = e_act(sch, nrow[0:1, 0:TK], nrow[0:1, 0:TK], AF.Sqrt, row_done, tag=f"ksd{h}")
            h_mx = e_ts(sch, nrow[0:1, 0:TK], nrow[0:1, 0:TK], 1e-12, ALU.max, [h_sd], tag=f"kmx{h}")
            h_rc = sch.op("vector", lambda v: v.reciprocal(nrow[0:1, 0:TK], nrow[0:1, 0:TK]),
                          [h_mx], tag=f"krc{h}")
            ap_done = []
            for b2 in range(TK // 512):
                h_bm = e_matmul_acc(sch, psbc[0:64, 0:512], [(onesr[0:1, 0:64], nrow[0:1, b2 * 512:(b2 + 1) * 512])],
                                    [h_rc] + ap_done[-1:], tag=f"kbc{h}_{b2}")
                h_bv = e_copy(sch, "vector", rtile[:, 0:512], psbc[0:64, 0:512], [h_bm], tag=f"kbv{h}_{b2}")
                h_ap = e_stt(sch, scr[:, b2 * 512:(b2 + 1) * 512], scr[:, b2 * 512:(b2 + 1) * 512],
                             ksc[0:64, 0:1], rtile[:, 0:512], ALU.mult, ALU.mult, [h_bv, h_t0],
                             tag=f"kap{h}_{b2}")
                ap_done.append(h_ap)
            h_r1 = e_copy(sch, "vector", rot[0:32, 0:TK], scr[32:64, 0:TK], ap_done, tag=f"kr1{h}")
            h_r2 = e_copy(sch, "vector", rot[32:64, 0:TK], scr[0:32, 0:TK], [h_r1] + ap_done, tag=f"kr2{h}")
            h_m1 = e_tt(sch, scr[:, 0:TK], scr[:, 0:TK], tabc[0:64, 0:TK], ALU.mult, [h_r2, h_tk], tag=f"km1{h}")
            h_m2 = e_tt(sch, rot[:, 0:TK], rot[:, 0:TK], tabs[0:64, 0:TK], ALU.mult, [h_m1, h_tk], tag=f"km2{h}")
            h_k = sch.op("vector", lambda v: v.tensor_tensor(kh[:, :], scr[:, 0:TK], rot[:, 0:TK], ALU.add),
                         [h_m2], tag=f"kad{h}")
            prev_head_scr = h_k
            prev_tab = h_k

            for w in range(16):
                st_h = []
                for r in range(5):
                    kc = w + r
                    bank = sta if r < 4 else stb
                    slc = (r % 4) * 128
                    deps = [h_q, h_k]
                    if exp_prev is not None and r == 0:
                        deps = deps + [exp_prev]
                    h_st = sch.op("tensor",
                                  (lambda bank, slc, kc, w:
                                   lambda t: t.matmul(bank[:, slc:slc + 128],
                                                      kh[:, kc * 128:(kc + 1) * 128],
                                                      qh[:, w * 128:(w + 1) * 128],
                                                      start=True, stop=True, skip_group_check=True))(bank, slc, kc, w),
                                  deps, tag=f"st{w}_{h}_{r}")
                    st_h.append(h_st)
                ed = st_h + ([exp_prev] if exp_prev else [])
                e_act(sch, ptc[:, 0:512], sta[:, 0:512], AF.Exp, ed, scale=8.0, tag=f"e1{w}_{h}")
                h_e2 = e_act(sch, ptc[:, 512:640], stb[:, 0:128], AF.Exp, [], scale=8.0, tag=f"e2{w}_{h}")
                h_mk = sch.op("vector",
                              (lambda w: lambda v: v.tensor_tensor(
                                  ptc[:, 0:640].rearrange("p (n m) -> p n m", m=128),
                                  ptc[:, 0:640].rearrange("p (n m) -> p n m", m=128),
                                  mall[:, w * 640:(w + 1) * 640].rearrange("p (n m) -> p n m", m=128),
                                  ALU.mult))(w),
                              [h_e2, h_mask_done], tag=f"mk{w}_{h}")
                h_ot = None
                for r in range(5):
                    kc = w + r
                    deps_ot = [h_mk] + v_done[-1:]
                    if on_prev is not None and r == 0:
                        deps_ot.append(on_prev)
                    h_ot = sch.op("tensor",
                                  (lambda kc, r, h:
                                   lambda t: t.matmul(oth[0:65, 0:128],
                                                      vaug4[:, kc, h, :],
                                                      ptc[:, r * 128:(r + 1) * 128],
                                                      start=(r == 0), stop=(r == 4), skip_group_check=True))(kc, r, h),
                                  deps_ot, tag=f"ot{w}_{h}_{r}")
                h_rr = sch.op("vector", lambda v: v.reciprocal(rrow[0:1, 0:128], oth[64:65, 0:128]),
                              [h_ot], tag=f"rr{w}_{h}")
                h_rb = e_matmul_acc(sch, psbc[0:64, 0:128], [(onesr[0:1, 0:64], rrow[0:1, 0:128])],
                                    [h_rr], tag=f"rb{w}_{h}")
                h_rbe = e_copy(sch, "vector", rbb[:, 0:128], psbc[0:64, 0:128], [h_rb], tag=f"rbe{w}_{h}")
                h_on = e_stt(sch, ofm[h // 2][64 * (h % 2):64 * (h % 2) + 64, w * 128:(w + 1) * 128],
                             oth[0:64, 0:128], 1.0, rbb[:, 0:128],
                             ALU.mult, ALU.mult, [h_rbe], tag=f"on{w}_{h}")
                on_prev = h_on
                exp_prev = h_on
        attn_done = [on_prev]

        prev_bank2 = None
        prev_dma = {0: None, 1: None}
        prev_res = {0: None, 1: None}
        for oi in range(NDC):
            for tb in range(4):
                qb_ = (oi * 4 + tb) % 2
                h_res = sch.dma_batch("sync", [(xres[qb_][:, 0:512],
                                                x_d[oi * 128:(oi + 1) * 128, H3 + tb * 512:H3 + (tb + 1) * 512])],
                                      deps=([prev_res[qb_]] if prev_res[qb_] else []), tag=f"res{oi}_{tb}")
                deps = attn_done + [h_wo]
                if prev_bank2 is not None:
                    deps.append(prev_bank2)
                pairs = [(wob[:, k * DM + oi * 128:k * DM + (oi + 1) * 128],
                          ofm[k][:, tb * 512:(tb + 1) * 512]) for k in range(4)]
                h_mm = e_matmul_acc(sch, pst[0][:, 0:512], pairs, deps, tag=f"wo{oi}_{tb}")
                ev_deps = [h_mm, h_res]
                if prev_dma[qb_] is not None:
                    ev_deps.append(prev_dma[qb_])
                h_ev = e_stt(sch, sout[qb_][:, 0:512], pst[0][:, 0:512], 1.0, xres[qb_][:, 0:512],
                             ALU.mult, ALU.add, ev_deps, tag=f"woev{oi}_{tb}")
                prev_bank2 = h_ev
                prev_res[qb_] = h_ev
                h_dma = sch.dma("sync", o_x[oi * 128:(oi + 1) * 128, tb * 512:(tb + 1) * 512],
                                sout[qb_][:, 0:512], deps=[h_ev], tag=f"wodma{oi}_{tb}")
                prev_dma[qb_] = h_dma
        if _own:
            sch.emit()
    return nc


def rotary_tables(core):
    p = core % GPB
    g = p * TPC - 512 + np.arange(TPC + 512, dtype=np.float64)
    half = np.arange(0, 64, 2, dtype=np.float64)
    inv_freq = 1.0 / 10000.0 ** (half / 64.0)
    freqs = g[None, :] * inv_freq[:, None]
    base = (half + 0.4 * 64) / (1.4 * 64)
    power = (g - 8192 // 2) / 256.0
    scale = base[:, None] ** power[None, :]
    cos = np.cos(freqs); sin = np.sin(freqs)
    z = np.zeros((128, TPC + 512), np.float32)
    cs = z.copy(); ss = z.copy(); ci = z.copy(); si = z.copy()
    cs[0:32] = cos * scale; cs[32:64] = cos * scale
    ss[0:32] = -(sin * scale); ss[32:64] = sin * scale
    ci[0:32] = cos / scale; ci[32:64] = cos / scale
    si[0:32] = -(sin / scale); si[32:64] = sin / scale
    return f32(cs), f32(ss), f32(ci), f32(si)


def stage_l3f_attn(inp, x_list, layer):
    wqkv = f32(inp["L3_ffn_wqkv"][layer])
    wo = f32(inp["L3_ffn_wo"][layer])
    qs = np.zeros((128, 1), np.float32); qs[0:64, 0] = inp["L3_ffn_q_scale"][layer]
    ks = np.zeros((128, 1), np.float32); ks[0:64, 0] = inp["L3_ffn_k_scale"][layer]
    i = np.arange(128)[:, None]; j = np.arange(128)[None, :]
    trif = f32(i >= j)
    trid = f32(i <= j)
    maps = []
    for c in range(NCORES):
        first = (c % GPB) == 0
        halo = np.zeros((DM, H3), np.float32) if first else f32(x_list[c - 1][:, -H3:])
        xe = np.concatenate([halo, f32(x_list[c])], axis=1)
        valid = np.ones((128, 80), np.float32)
        if first:
            for w in range(16):
                for r in range(5):
                    if w + r < 4:
                        valid[:, w * 5 + r] = 0.0
        cs, ss, ci, si = rotary_tables(c)
        maps.append(dict(x=f32(xe), wqkv=wqkv, wo=wo, cs=cs, ss=ss, ci=ci, si=si,
                         qs=qs, ks=ks, trif=trif, trid=trid, valid=valid))
    return maps


HID3 = 2730
NKS = 22

def build_l3f_ffn(ctx=None):
    nc, _es0, _own = _stage_begin(ctx)
    D = _mk_D(nc, ctx)
    x_d = D("x", [DM, 1 + TPC], F32, "ExternalInput")
    w1_d = D("w1", [DM, 2 * HID3], F32, "ExternalInput")
    w2_d = D("w2", [HID3, DM], F32, "ExternalInput")
    lng_d = D("lng", [128, NDC], F32, "ExternalInput")
    lnb_d = D("lnb", [128, NDC], F32, "ExternalInput")
    b1a_d = D("b1a", [128, NKS], F32, "ExternalInput")
    b1g_d = D("b1g", [128, NKS], F32, "ExternalInput")
    b2_d = D("b2", [128, NDC], F32, "ExternalInput")
    o_x = D("o_x", [DM, TPC], F32, "ExternalOutput")

    es = _es0
    sbf = lambda n, s, d=F32: es.enter_context(nc.sbuf_tensor(_uname(n), s, d))
    psm_ = lambda n, s: es.enter_context(nc.psum_tensor(_uname(n), s, F32))
    with es:
        xe = [sbf(f"xe{c}", [128, 1 + TPC]) for c in range(NDC)]
        xnl = [sbf(f"xnl{c}", [128, TPC], BF16) for c in range(NDC)]
        w2r = sbf("w2r", [128, NKS * DM], BF16)
        w1s = [sbf(f"w1s{i}", [128, NDC * 128], BF16) for i in range(4)]
        h2b = sbf("h2b", [128, NKS * 512], BF16)
        asl = [sbf(f"asl{i}", [128, 512]) for i in range(2)]
        lngc = sbf("lngc", [128, NDC]); lnbc = sbf("lnbc", [128, NDC])
        b1ac = sbf("b1ac", [128, NKS]); b1gc = sbf("b1gc", [128, NKS])
        b2c = sbf("b2c", [128, NDC])
        srow = sbf("srow", [1, 2 * TPC])
        mub = sbf("mub", [128, 512]); rsb = sbf("rsb", [128, 512])
        onesc = sbf("onesc", [128, 1]); onesr = sbf("onesr", [1, 128])
        epsc = sbf("epsc", [1, 1]); vscr = sbf("vscr", [1, 512])
        xres = [sbf(f"xres{i}", [128, 512]) for i in range(2)]
        sout = [sbf(f"sout{i}", [128, 512]) for i in range(2)]
        sq = sout
        lnscr = asl
        ps = [psm_(f"ps{i}", [128, 512]) for i in range(8)]

        sch = ctx.sch if ctx else Sched(nc, es)
        h_x = sch.dma_batch("sync", [(xe[c][:, :], x_d[c * 128:(c + 1) * 128, :]) for c in range(NDC)], tag="ldx")
        h_sm = sch.dma_batch("sync", [(lngc[:, :], lng_d[:, :]), (lnbc[:, :], lnb_d[:, :]),
                                      (b1ac[:, :], b1a_d[:, :]), (b1gc[:, :], b1g_d[:, :]),
                                      (b2c[:, :], b2_d[:, :])], tag="ldsm")
        w2_parts = []
        def emit_w2_part(pi):
            lo, hi = pi * 6, min((pi + 1) * 6, NKS)
            w2_parts.append(sch.dma_batch(
                "gpsimd", [(w2r[0:min(128, HID3 - k * 128), k * DM:(k + 1) * DM],
                            w2_d[k * 128:min((k + 1) * 128, HID3), :]) for k in range(lo, hi)],
                tag=f"ldw2_{pi}"))
        h_ones = [sch.op("vector", lambda v: v.memset(onesc[:], 1.0)),
                  sch.op("vector", lambda v: v.memset(onesr[:], 1.0)),
                  sch.op("vector", lambda v: v.memset(epsc[:], 1e-5))]

        def hv(c):
            return xe[c][:, 1:1 + TPC] if c < 4 else xe[c][:, 0:TPC]

        bufs = dict(sq=sq, srow=srow, mub=mub, rsb=rsb, onesc=onesc, onesr=onesr,
                    epsc=epsc, psx=ps[0][0:1, 0:512], psx2=ps[1][0:1, 0:512],
                    psbc=ps[2][:, 0:512], vscr=vscr)
        hviews = [hv(c) for c in range(NDC)]

        def out_cb(c, tb, mub_ap, rsb_ap, deps):
            i = (c * 4 + tb) % 2
            h1 = e_tt(sch, lnscr[i][:, 0:512], hviews[c][:, tb * 512:(tb + 1) * 512], mub_ap,
                      ALU.subtract, deps + ([lncb_prev[i]] if lncb_prev[i] else []),
                      tag=f"lns{c}_{tb}")
            h2 = e_tt(sch, lnscr[i][:, 0:512], lnscr[i][:, 0:512], rsb_ap, ALU.mult, [h1],
                      tag=f"lnm{c}_{tb}")
            h3 = e_stt(sch, xnl[c][:, tb * 512:(tb + 1) * 512], lnscr[i][:, 0:512], lngc[:, c:c + 1],
                       lnbc[:, c:c + 1].broadcast_to([128, 512]), ALU.mult, ALU.add, [h2, h_sm],
                       tag=f"lng{c}_{tb}")
            lncb_prev[i] = h3
            return h3

        lncb_prev = {0: None, 1: None}
        ln_done = emit_ln_fm(sch, bufs, hviews, out_cb, [h_x] + h_ones, tag="ln")

        prev_down_ev = None
        prev_w1 = {0: None, 1: None}
        for tb in range(4):
            hq_done = []
            prev_ev = {0: None, 1: None}
            for ksl in range(NKS):
                klen = min(128, HID3 - ksl * 128)
                i = ksl % 2
                dw = ln_done + ([prev_w1[i]] if prev_w1[i] else [])
                h_w1 = sch.dma_batch("gpsimd",
                                     [(w1s[i][:, c * 128:c * 128 + klen],
                                       w1_d[c * 128:(c + 1) * 128, ksl * 128:ksl * 128 + klen])
                                      for c in range(NDC)] +
                                     [(w1s[2 + i][:, c * 128:c * 128 + klen],
                                       w1_d[c * 128:(c + 1) * 128, HID3 + ksl * 128:HID3 + ksl * 128 + klen])
                                      for c in range(NDC)],
                                     deps=dw, tag=f"T{tb}w1_{ksl}")
                dmm = [h_w1] + ln_done
                if prev_down_ev is not None:
                    dmm.append(prev_down_ev)
                if hq_done:
                    dmm.append(hq_done[-1])
                pairs_a = [(w1s[i][:, c * 128:c * 128 + klen],
                            xnl[c][:, tb * 512:(tb + 1) * 512]) for c in range(NDC)]
                h_ma = e_matmul_acc(sch, ps[3][0:klen, 0:512], pairs_a, dmm, tag=f"T{tb}a{ksl}")
                pairs_g = [(w1s[2 + i][:, c * 128:c * 128 + klen],
                            xnl[c][:, tb * 512:(tb + 1) * 512]) for c in range(NDC)]
                h_mg = e_matmul_acc(sch, ps[4][0:klen, 0:512], pairs_g, [h_ma], tag=f"T{tb}g{ksl}")
                h_ea = e_act(sch, asl[0][0:klen, 0:512], ps[3][0:klen, 0:512], AF.Identity,
                             [h_ma] + ([prev_ev[0]] if prev_ev[0] else []),
                             bias=b1ac[0:klen, ksl:ksl + 1], tag=f"T{tb}ea{ksl}")
                h_eg = e_act(sch, asl[1][0:klen, 0:512], ps[4][0:klen, 0:512], AF.Gelu,
                             [h_mg, h_ea], bias=b1gc[0:klen, ksl:ksl + 1], tag=f"T{tb}eg{ksl}")
                h_h2 = e_tt(sch, h2b[0:klen, ksl * 512:(ksl + 1) * 512], asl[0][0:klen, 0:512],
                            asl[1][0:klen, 0:512], ALU.mult, [h_eg], tag=f"T{tb}h2{ksl}")
                prev_ev[i] = h_h2
                prev_w1[i] = h_mg
                hq_done.append(h_h2)
                if tb == 0 and ksl in (2, 7, 12, 17):
                    emit_w2_part(len(w2_parts))
            h_dn = {o: None for o in range(8)}
            for ksl in range(NKS):
                klen = min(128, HID3 - ksl * 128)
                for o in range(8):
                    dmm = hq_done + w2_parts
                    if ksl == 0 and prev_down_ev is not None:
                        dmm.append(prev_down_ev)
                    h_dn[o] = sch.op("tensor",
                                     (lambda o, ksl, klen:
                                      lambda t: t.matmul(ps[o][:, 0:512],
                                                         w2r[0:klen, ksl * DM + o * 128:ksl * DM + (o + 1) * 128],
                                                         h2b[0:klen, ksl * 512:(ksl + 1) * 512],
                                                         start=(ksl == 0), stop=(ksl == NKS - 1),
                                                         skip_group_check=True))(o, ksl, klen),
                                     dmm, tag=f"T{tb}dn{ksl}_{o}")
            evs = []
            prev_dma2 = {0: None, 1: None}
            prev_res = {0: None, 1: None}
            for o in range(8):
                ii = o % 2
                h_res = sch.dma_batch("sync", [(xres[ii][:, 0:512],
                                                x_d[o * 128:(o + 1) * 128, 1 + tb * 512:1 + (tb + 1) * 512])],
                                      deps=([prev_res[ii]] if prev_res[ii] else []), tag=f"T{tb}res{o}")
                dv = [h_dn[o], h_res, h_sm]
                if prev_dma2[ii] is not None:
                    dv.append(prev_dma2[ii])
                h_bi = e_act(sch, sout[ii][:, 0:512], ps[o][:, 0:512], AF.Identity, dv,
                             bias=b2c[:, o:o + 1], tag=f"T{tb}bi{o}")
                h_ad = e_tt(sch, sout[ii][:, 0:512], sout[ii][:, 0:512], xres[ii][:, 0:512],
                            ALU.add, [h_bi], tag=f"T{tb}ad{o}")
                prev_res[ii] = h_ad
                h_dd = sch.dma_batch("sync", [(o_x[o * 128:(o + 1) * 128, tb * 512:(tb + 1) * 512],
                                               sout[ii][:, 0:512])], deps=[h_ad], tag=f"T{tb}dd{o}")
                prev_dma2[ii] = h_dd
                evs.append(h_ad)
            prev_down_ev = evs[-1]
        if _own:
            sch.emit()
    return nc


def stage_l3f_ffn(inp, x_list, layer):
    w1 = f32(inp["L3_ffn_w1"][layer])
    w2 = f32(inp["L3_ffn_w2"][layer])
    lng = f32(inp["L3_ffn_ln_g"][layer].reshape(NDC, 128).T)
    lnb = f32(inp["L3_ffn_ln_b"][layer].reshape(NDC, 128).T)
    b1 = inp["L3_ffn_b1"][layer]
    b1a = np.zeros((128, NKS), np.float32)
    b1g = np.zeros((128, NKS), np.float32)
    for k in range(NKS):
        klen = min(128, HID3 - k * 128)
        b1a[0:klen, k] = b1[k * 128:k * 128 + klen]
        b1g[0:klen, k] = b1[HID3 + k * 128:HID3 + k * 128 + klen]
    b2 = f32(inp["L3_ffn_b2"][layer].reshape(NDC, 128).T)
    maps = []
    for c in range(NCORES):
        first = (c % GPB) == 0
        halo = np.zeros((DM, 1), np.float32) if first else f32(x_list[c - 1][:, -1:])
        xe = np.concatenate([halo, f32(x_list[c])], axis=1)
        maps.append(dict(x=f32(xe), w1=w1, w2=w2, lng=lng, lnb=lnb, b1a=b1a, b1g=b1g, b2=b2))
    return maps


def emit_cc(sch, kind, op, rg, in_ap, out_ap, tag):
    return sch.op("gpsimd",
                  lambda g: g.collective_compute(kind, op, replica_groups=rg,
                                                 ins=[in_ap], outs=[out_ap]),
                  tag=tag)


RG4 = [[0, 1, 2, 3], [4, 5, 6, 7]]
RG8 = [list(range(NCORES))]


def emit_tailcopy(sch, dst_ap, src_ap, H, tag):
    kw = {"allow_slow_non_contiguous": True} if H == 1 else None
    for c in range(NDC):
        sch.dma("sync", dst_ap[c * 128:(c + 1) * 128, :],
                src_ap[c * 128:(c + 1) * 128, TPC - H:TPC], tag=f"{tag}{c}", dma_kw=kw)


def emit_build_ext(nc, sch, es_parent, ext_ap, src_ap, tailag_ap, msel_ap, H, tag):
    es = ExitStack()
    with es:
        tl = es.enter_context(nc.sbuf_tensor(_uname(f"{tag}_tl"), [128, 4 * H], F32))
        acc = es.enter_context(nc.sbuf_tensor(_uname(f"{tag}_acc"), [128, H], F32))
        ms = es.enter_context(nc.sbuf_tensor(_uname(f"{tag}_ms"), [128, 4], F32))
        h_ms = sch.dma_batch("sync", [(ms[:, :], msel_ap[:, :])], tag=f"{tag}ms")
        for c in range(NDC):
            src3 = tailag_ap.rearrange("(s d) h -> d s h", s=4)[c * 128:(c + 1) * 128, :, :]
            h_tl = sch.dma_batch("sync", [(tl[:, :].rearrange("p (s h) -> p s h", s=4), src3)],
                                 deps=[], tag=f"{tag}tl{c}",
                                 dma_kw=({"allow_slow_non_contiguous": True} if H == 1 else None))
            h_z = e_memset(sch, "vector", acc[:, :], 0.0, [h_tl], tag=f"{tag}z{c}")
            h = h_z
            for s in range(4):
                h = e_stt(sch, acc[:, :], tl[:, s * H:(s + 1) * H], ms[:, s:s + 1], acc[:, :],
                          ALU.mult, ALU.add, [h, h_ms], tag=f"{tag}f{c}_{s}")
            h_d = sch.dma_batch("sync", [(ext_ap[c * 128:(c + 1) * 128, 0:H], acc[:, :])],
                                deps=[h], tag=f"{tag}hd{c}",
                                dma_kw=({"allow_slow_non_contiguous": True} if H == 1 else None))
            sch.dma("sync", ext_ap[c * 128:(c + 1) * 128, H:H + TPC],
                    src_ap[c * 128:(c + 1) * 128, :], deps=[], tag=f"{tag}cp{c}")


def build_mega():
    nc = bass.Bass("TRN2", target_bir_lowering=False)
    mes = ExitStack()
    sch = Sched(nc, mes)
    ext_names = []

    def din(gname, shape, dtype=F32):
        ext_names.append(gname)
        return nc.dram_tensor(gname, shape, dtype, kind="ExternalInput").ap()

    def dint(gname, shape, dtype=F32):
        return nc.dram_tensor(gname, shape, dtype).ap()

    with mes:
        g = {}
        g["ident"] = din("ident", [128, 128])
        g["e16"] = din("e16", [128, 16 * 128])
        for n, s in [("x_tm", [TPC, DM]), ("wq", [DM, DM]), ("wk", [DM, 256]), ("wv", [DM, 256]),
                     ("gw", [DM, 8]), ("bqc", [128, NDC]), ("bkc", [128, 2]), ("bvr", [1, 256]),
                     ("gbc", [8, 1])]:
            g["b0_" + n] = din("b0_" + n, s)
        g["l0a_qidx"] = din("l0a_qidx", [128, TPC], mybir.dt.int16)
        g["l0a_kidx"] = din("l0a_kidx", [128, 64], mybir.dt.int16)
        for n, s in [("sel16", [16, 8 * 128]), ("wo", [DM, DM]),
                     ("lng", [128, NDC]), ("lnb", [128, NDC]), ("boc", [128, NDC])]:
            g["l0a_" + n] = din("l0a_" + n, s)
        for n, s in [("w1", [8 * DM, DM]), ("w2", [8 * DM, DM]), ("w3", [8 * DM, DM]),
                     ("b1", [128, 8 * NDC]), ("b2", [128, 8 * NDC]), ("b3", [1, 8 * DM]),
                     ("eye8", [8, 8])]:
            g["moe_" + n] = din("moe_" + n, s)
        for n, s in [("gcol", [128, NDC]), ("wqkv", [DM, 3 * DM]), ("wo", [DM, DM]),
                     ("maskp", [128, 128]), ("maskc", [128, 128]), ("fw", [128, 1])]:
            g["l1_" + n] = din("l1_" + n, s)
        for pre in ("f1", "f2"):
            for n, s in [("wg_n", [FF, DM]), ("wu_n", [FF, DM]), ("wd_n", [DM, FF]),
                         ("wg_t", [DM, FF]), ("wu_t", [DM, FF]), ("wd_t", [FF, DM])]:
                g[pre + "_" + n] = din(pre + "_" + n, s)
        for n, s in [("wqkv", [DM, 3 * DM]), ("wo", [DM, DM]), ("bo", [128, NDC]),
                     ("maskc", [128, 128])]:
            g["l2_" + n] = din("l2_" + n, s)
        g["l3_wt"] = din("l3_wt", [DM, 3 * DM])
        g["l3_gcol"] = din("l3_gcol", [128, NDC])
        g["l3_msel8"] = din("l3_msel8", [128, 8])
        for lay in ("a0", "a1"):
            for n, s in [("wqkv", [DM, 3 * 512]), ("wo", [512, DM]), ("qs", [128, 1]), ("ks", [128, 1])]:
                g[lay + "_" + n] = din(lay + "_" + n, s)
        for n, s in [("cs", [128, TPC + 512]), ("ss", [128, TPC + 512]),
                     ("ci", [128, TPC + 512]), ("si", [128, TPC + 512]),
                     ("trif", [128, 128]), ("trid", [128, 128]), ("valid", [128, 80])]:
            g["l3f_" + n] = din("l3f_" + n, s)
        for lay in ("m0", "m1"):
            for n, s in [("w1", [DM, 2 * HID3]), ("w2", [HID3, DM]), ("lng", [128, NDC]),
                         ("lnb", [128, NDC]), ("b1a", [128, NKS]), ("b1g", [128, NKS]),
                         ("b2", [128, NDC])]:
                g[lay + "_" + n] = din(lay + "_" + n, s)
        g["msel4"] = din("msel4", [128, 4])

        FMT = [DM, TPC]
        ints = {}
        for n, s in [("xfm", FMT), ("xn0", FMT), ("q0", FMT), ("k0", [256, TPC]),
                     ("v0", [TPC, 256]), ("lg0", [8, TPC]),
                     ("kag", [4 * 256, TPC]), ("vag", [4 * TPC, 256]), ("lgag", [32, TPC]),
                     ("s0", FMT), ("vpart", [1, 8 * DM]), ("var", [1, 8 * DM]), ("ind", [8, TPC]),
                     ("x1", FMT), ("xn1", FMT), ("tl1", [DM, L1H]), ("tlag1", [4 * DM, L1H]),
                     ("xn1e", [DM, L1H + TPC]), ("s1", FMT), ("g1", FMT),
                     ("x2", FMT), ("xn2", FMT), ("s2", FMT), ("g2", FMT),
                     ("x3", FMT), ("xn3", FMT),
                     ("q3", FMT), ("yloc", FMT), ("acum", FMT), ("tot", [DM, 2]),
                     ("totag", [8 * DM, 2]), ("s3", FMT),
                     ("tl513a", [DM, H3]), ("tag513a", [4 * DM, H3]), ("xe_a0", [DM, TE3]),
                     ("xm0", FMT), ("tl1a", [DM, 1]), ("tag1a", [4 * DM, 1]), ("xe_m0", [DM, 1 + TPC]),
                     ("xl1", FMT), ("tl513b", [DM, H3]), ("tag513b", [4 * DM, H3]), ("xe_a1", [DM, TE3]),
                     ("xm1", FMT), ("tl1b", [DM, 1]), ("tag1b", [4 * DM, 1]), ("xe_m1", [DM, 1 + TPC]),
                     ("g3", FMT)]:
            ints[n] = dint("i_" + n, s)
        for pre in ("f1", "f2"):
            ints[pre + "_wqg"] = dint(pre + "_wqg", [DM, FF], BF16)
            ints[pre + "_wqu"] = dint(pre + "_wqu", [DM, FF], BF16)
            ints[pre + "_wqd"] = dint(pre + "_wqd", [FF, DM], BF16)
        o_out = nc.dram_tensor("o_out", [TPC, DM], F32, kind="ExternalOutput").ap()

        I = ints
        def bar():
            sch.stage_barrier()

        build_b0_pre(StageCtx(nc, sch, dict(
            x_tm=g["b0_x_tm"], wq=g["b0_wq"], wk=g["b0_wk"], wv=g["b0_wv"], gw=g["b0_gw"],
            bqc=g["b0_bqc"], bkc=g["b0_bkc"], bvr=g["b0_bvr"], gbc=g["b0_gbc"], ident=g["ident"],
            o_xfm=I["xfm"], o_xn=I["xn0"], o_q=I["q0"], o_k=I["k0"], o_v=I["v0"], o_lg=I["lg0"])))
        bar()
        emit_cc(sch, "AllGather", ALU.bypass, RG4, I["k0"][:, :], I["kag"][:, :], "cck")
        emit_cc(sch, "AllGather", ALU.bypass, RG4, I["v0"][:, :], I["vag"][:, :], "ccv")
        emit_cc(sch, "AllGather", ALU.bypass, RG4, I["lg0"][:, :], I["lgag"][:, :], "cclg")
        bar()
        build_l0_attn(StageCtx(nc, sch, dict(
            q=I["q0"], k=I["kag"], v=I["vag"], qidx=g["l0a_qidx"], kidx=g["l0a_kidx"],
            sel16=g["l0a_sel16"],
            wo=g["l0a_wo"], lng=g["l0a_lng"], lnb=g["l0a_lnb"], boc=g["l0a_boc"],
            ident=g["ident"], o_s=I["s0"])))
        bar()
        build_l0_moe(StageCtx(nc, sch, dict(
            xn=I["xn0"], lgf=I["lgag"], lgm=I["lg0"], w1=g["moe_w1"], w2=g["moe_w2"], w3=g["moe_w3"],
            b1=g["moe_b1"], b2=g["moe_b2"], b3=g["moe_b3"], eye8=g["moe_eye8"], ident=g["ident"],
            o_vpart=I["vpart"], o_ind=I["ind"])))
        bar()
        emit_cc(sch, "AllReduce", ALU.add, RG8, I["vpart"][:, :], I["var"][:, :], "ccvp")
        bar()
        build_combine("moe", StageCtx(nc, sch, dict(
            x=I["xfm"], s=I["s0"], vall=I["var"], ind=I["ind"], o_x=I["x1"], o_xn=I["xn1"])))
        bar()
        emit_tailcopy(sch, I["tl1"], I["xn1"], L1H, "t1")
        bar()
        emit_cc(sch, "AllGather", ALU.bypass, RG4, I["tl1"][:, :], I["tlag1"][:, :], "cct1")
        bar()
        emit_build_ext(nc, sch, mes, I["xn1e"], I["xn1"], I["tlag1"], g["msel4"], L1H, "e1")
        bar()
        build_l1_attn(4, StageCtx(nc, sch, dict(
            xn=I["xn1e"], gcol=g["l1_gcol"], wqkv=g["l1_wqkv"], wo=g["l1_wo"],
            maskp=g["l1_maskp"], maskc=g["l1_maskc"], fw=g["l1_fw"], o_s=I["s1"])))
        bar()
        build_fp4_ffn(StageCtx(nc, sch, dict(
            xn=I["xn1"], wg_n=g["f1_wg_n"], wu_n=g["f1_wu_n"], wd_n=g["f1_wd_n"],
            wg_t=g["f1_wg_t"], wu_t=g["f1_wu_t"], wd_t=g["f1_wd_t"], ident=g["ident"],
            e16=g["e16"], o_g=I["g1"], wq_g=I["f1_wqg"], wq_u=I["f1_wqu"], wq_d=I["f1_wqd"])))
        bar()
        build_combine("plain", StageCtx(nc, sch, dict(
            x=I["x1"], s=I["s1"], g=I["g1"], o_x=I["x2"], o_xn=I["xn2"])))
        bar()
        build_l2_attn(StageCtx(nc, sch, dict(
            xn=I["xn2"], wqkv=g["l2_wqkv"], wo=g["l2_wo"], bo=g["l2_bo"], maskc=g["l2_maskc"],
            o_s=I["s2"])))
        bar()
        build_fp4_ffn(StageCtx(nc, sch, dict(
            xn=I["xn2"], wg_n=g["f2_wg_n"], wu_n=g["f2_wu_n"], wd_n=g["f2_wd_n"],
            wg_t=g["f2_wg_t"], wu_t=g["f2_wu_t"], wd_t=g["f2_wd_t"], ident=g["ident"],
            e16=g["e16"], o_g=I["g2"], wq_g=I["f2_wqg"], wq_u=I["f2_wqu"], wq_d=I["f2_wqd"])))
        bar()
        build_combine("plain", StageCtx(nc, sch, dict(
            x=I["x2"], s=I["s2"], g=I["g2"], o_x=I["x3"], o_xn=I["xn3"])))
        bar()
        build_l3_scan(StageCtx(nc, sch, dict(
            xn=I["xn3"], wt=g["l3_wt"], gcol=g["l3_gcol"],
            o_q=I["q3"], o_yloc=I["yloc"], o_acum=I["acum"], o_tot=I["tot"])))
        bar()
        emit_cc(sch, "AllGather", ALU.bypass, RG8, I["tot"][:, :], I["totag"][:, :], "cctot")
        bar()
        build_l3_fix(StageCtx(nc, sch, dict(
            q=I["q3"], yl=I["yloc"], ac=I["acum"], tp=I["totag"], msel=g["l3_msel8"],
            o_s=I["s3"])))
        bar()
        emit_tailcopy(sch, I["tl513a"], I["xn3"], H3, "t3a")
        bar()
        emit_cc(sch, "AllGather", ALU.bypass, RG4, I["tl513a"][:, :], I["tag513a"][:, :], "cc3a")
        bar()
        emit_build_ext(nc, sch, mes, I["xe_a0"], I["xn3"], I["tag513a"], g["msel4"], H3, "e3a")
        bar()
        build_l3f_attn(StageCtx(nc, sch, dict(
            x=I["xe_a0"], wqkv=g["a0_wqkv"], wo=g["a0_wo"], cs=g["l3f_cs"], ss=g["l3f_ss"],
            ci=g["l3f_ci"], si=g["l3f_si"], qs=g["a0_qs"], ks=g["a0_ks"], trif=g["l3f_trif"],
            trid=g["l3f_trid"], valid=g["l3f_valid"], o_x=I["xm0"])))
        bar()
        emit_tailcopy(sch, I["tl1a"], I["xm0"], 1, "t3b")
        bar()
        emit_cc(sch, "AllGather", ALU.bypass, RG4, I["tl1a"][:, :], I["tag1a"][:, :], "cc3b")
        bar()
        emit_build_ext(nc, sch, mes, I["xe_m0"], I["xm0"], I["tag1a"], g["msel4"], 1, "e3b")
        bar()
        build_l3f_ffn(StageCtx(nc, sch, dict(
            x=I["xe_m0"], w1=g["m0_w1"], w2=g["m0_w2"], lng=g["m0_lng"], lnb=g["m0_lnb"],
            b1a=g["m0_b1a"], b1g=g["m0_b1g"], b2=g["m0_b2"], o_x=I["xl1"])))
        bar()
        emit_tailcopy(sch, I["tl513b"], I["xl1"], H3, "t3c")
        bar()
        emit_cc(sch, "AllGather", ALU.bypass, RG4, I["tl513b"][:, :], I["tag513b"][:, :], "cc3c")
        bar()
        emit_build_ext(nc, sch, mes, I["xe_a1"], I["xl1"], I["tag513b"], g["msel4"], H3, "e3c")
        bar()
        build_l3f_attn(StageCtx(nc, sch, dict(
            x=I["xe_a1"], wqkv=g["a1_wqkv"], wo=g["a1_wo"], cs=g["l3f_cs"], ss=g["l3f_ss"],
            ci=g["l3f_ci"], si=g["l3f_si"], qs=g["a1_qs"], ks=g["a1_ks"], trif=g["l3f_trif"],
            trid=g["l3f_trid"], valid=g["l3f_valid"], o_x=I["xm1"])))
        bar()
        emit_tailcopy(sch, I["tl1b"], I["xm1"], 1, "t3d")
        bar()
        emit_cc(sch, "AllGather", ALU.bypass, RG4, I["tl1b"][:, :], I["tag1b"][:, :], "cc3d")
        bar()
        emit_build_ext(nc, sch, mes, I["xe_m1"], I["xm1"], I["tag1b"], g["msel4"], 1, "e3d")
        bar()
        build_l3f_ffn(StageCtx(nc, sch, dict(
            x=I["xe_m1"], w1=g["m1_w1"], w2=g["m1_w2"], lng=g["m1_lng"], lnb=g["m1_lnb"],
            b1a=g["m1_b1a"], b1g=g["m1_b1g"], b2=g["m1_b2"], o_x=I["g3"])))
        bar()
        build_combine("final", StageCtx(nc, sch, dict(
            x=I["x3"], s=I["s3"], g=I["g3"], ident=g["ident"], o_out=o_out)))
        sch.emit()
    return nc, ext_names


def stage_mega(inp):
    xs = shard_tokens(inp["L0_attn_x"])
    i = np.arange(128)[:, None]; j = np.arange(128)[None, :]
    shared = {
        "ident": IDENT128, "e16": E16,
        "b0_wq": f32(inp["L0_attn_Wq"]), "b0_wk": f32(inp["L0_attn_Wk"]),
        "b0_wv": f32(inp["L0_attn_Wv"]), "b0_gw": f32(inp["L0_ffn_gate_w"]),
        "b0_bqc": f32(inp["L0_attn_bq"].reshape(NDC, 128).T),
        "b0_bkc": f32(inp["L0_attn_bk"].reshape(2, 128).T),
        "b0_bvr": f32(inp["L0_attn_bv"].reshape(1, 256)),
        "b0_gbc": f32(inp["L0_ffn_gate_b"].reshape(8, 1)),
        "l0a_kidx": np.ascontiguousarray(
            (np.arange(64)[None, :] * 128 + np.arange(128)[:, None]).astype(np.int16)),
        "l0a_sel16": _sel16_table(),
        "l0a_wo": f32(inp["L0_attn_Wo"]),
        "l0a_lng": f32(inp["L0_attn_ln_g"].reshape(NDC, 128).T),
        "l0a_lnb": f32(inp["L0_attn_ln_b"].reshape(NDC, 128).T),
        "l0a_boc": f32(inp["L0_attn_bo"].reshape(NDC, 128).T),
        "moe_w1": f32(inp["L0_ffn_w1"].reshape(8 * DM, DM)),
        "moe_w2": f32(inp["L0_ffn_w2"].reshape(8 * DM, DM)),
        "moe_w3": f32(inp["L0_ffn_w3"].reshape(8 * DM, DM)),
        "moe_b1": f32(inp["L0_ffn_b1"].reshape(8, NDC, 128).transpose(2, 0, 1).reshape(128, 8 * NDC)),
        "moe_b2": f32(inp["L0_ffn_b2"].reshape(8, NDC, 128).transpose(2, 0, 1).reshape(128, 8 * NDC)),
        "moe_b3": f32(inp["L0_ffn_b3"].reshape(1, 8 * DM)),
        "moe_eye8": np.eye(8, dtype=np.float32),
        "l1_gcol": f32(inp["L1_attn_g"].reshape(NDC, 128).T),
        "l1_wqkv": f32(inp["L1_attn_Wqkv"]), "l1_wo": f32(inp["L1_attn_Wo"]),
        "l1_maskp": f32(i > j), "l1_maskc": f32(i <= j),
        "l2_wqkv": f32(inp["L2_attn_W_qkv"]), "l2_wo": f32(inp["L2_attn_W_proj"]),
        "l2_bo": f32(inp["L2_attn_b_proj"].reshape(NDC, 128).T),
        "l2_maskc": f32(i <= j),
        "l3_wt": f32(inp["L3_attn_w_qkva"].T),
        "l3_gcol": f32(inp["L3_attn_gamma"].reshape(NDC, 128).T),
        "l3f_trif": f32(i >= j), "l3f_trid": f32(i <= j),
    }
    for pre, key in (("f1", "L1_ffn"), ("f2", "L2_ffn")):
        wg = f32(inp[f"{key}_w_gate"]); wu = f32(inp[f"{key}_w_up"]); wd = f32(inp[f"{key}_w_down"])
        shared.update({f"{pre}_wg_n": wg, f"{pre}_wu_n": wu, f"{pre}_wd_n": wd,
                       f"{pre}_wg_t": f32(wg.T), f"{pre}_wu_t": f32(wu.T), f"{pre}_wd_t": f32(wd.T)})
    for li, lay in enumerate(("a0", "a1")):
        qs = np.zeros((128, 1), np.float32); qs[0:64, 0] = inp["L3_ffn_q_scale"][li]
        ks = np.zeros((128, 1), np.float32); ks[0:64, 0] = inp["L3_ffn_k_scale"][li]
        shared.update({f"{lay}_wqkv": f32(inp["L3_ffn_wqkv"][li]),
                       f"{lay}_wo": f32(inp["L3_ffn_wo"][li]),
                       f"{lay}_qs": qs, f"{lay}_ks": ks})
    for li, lay in enumerate(("m0", "m1")):
        b1 = inp["L3_ffn_b1"][li]
        b1a = np.zeros((128, NKS), np.float32); b1g = np.zeros((128, NKS), np.float32)
        for k in range(NKS):
            klen = min(128, HID3 - k * 128)
            b1a[0:klen, k] = b1[k * 128:k * 128 + klen]
            b1g[0:klen, k] = b1[HID3 + k * 128:HID3 + k * 128 + klen]
        shared.update({f"{lay}_w1": f32(inp["L3_ffn_w1"][li]), f"{lay}_w2": f32(inp["L3_ffn_w2"][li]),
                       f"{lay}_lng": f32(inp["L3_ffn_ln_g"][li].reshape(NDC, 128).T),
                       f"{lay}_lnb": f32(inp["L3_ffn_ln_b"][li].reshape(NDC, 128).T),
                       f"{lay}_b1a": b1a, f"{lay}_b1g": b1g,
                       f"{lay}_b2": f32(inp["L3_ffn_b2"][li].reshape(NDC, 128).T)})
    maps = []
    for c in range(NCORES):
        p = c % GPB
        first = p == 0
        m = dict(shared)
        m["b0_x_tm"] = xs[c]
        m["l0a_qidx"] = np.ascontiguousarray(
            np.broadcast_to((p * TPC + np.arange(TPC)).astype(np.int16)[None, :], (128, TPC)))
        m["l1_fw"] = np.zeros((128, 1), np.float32) if first else np.ones((128, 1), np.float32)
        msel4 = np.zeros((128, 4), np.float32)
        if not first:
            msel4[:, p - 1] = 1.0
        m["msel4"] = msel4
        msel8 = np.zeros((128, 8), np.float32)
        for r in range(NCORES):
            if r // GPB == c // GPB and r < c:
                msel8[:, r] = 1.0
        m["l3_msel8"] = msel8
        valid = np.ones((128, 80), np.float32)
        if first:
            for w in range(16):
                for r in range(5):
                    if w + r < 4:
                        valid[:, w * 5 + r] = 0.0
        m["l3f_valid"] = valid
        cs, ss, ci, si = rotary_tables(c)
        m.update({"l3f_cs": cs, "l3f_ss": ss, "l3f_ci": ci, "l3f_si": si})
        maps.append(m)
    return maps


_mega_cache = {}


def kernel(**inp):
    if "nc" not in _mega_cache:
        _mega_cache["nc"], _mega_cache["ext"] = build_mega()
    maps = stage_mega(inp)
    res = run_bass_kernel_spmd(_mega_cache["nc"], maps, core_ids=list(range(NCORES)))
    out = np.concatenate([res.results[c]["o_out"] for c in range(NCORES)], axis=0)
    return np.ascontiguousarray(out.reshape(B, S, DM), dtype=np.float32)
